# version 06
# baseline (speedup 1.0000x reference)
import numpy as np
import ml_dtypes
from contextlib import ExitStack

import concourse.bass as bass
import concourse.mybir as mybir
from concourse.bass_utils import run_bass_kernel_spmd
from concourse.expressions_rust import RuntimeValue

F32 = mybir.dt.float32
BF16 = mybir.dt.bfloat16
I32 = mybir.dt.int32
AF = mybir.ActivationFunctionType
ALU = mybir.AluOpType
AX = mybir.AxisListType
ds = bass.ds
ts = bass.ts

D, NCD, HF, T = 2048, 4096, 8192, 2048
DC, JCN, HCN = 16, 32, 64
GT, NG, STW, NS = 1024, 2, 512, 2
NSLOT = 16
EPS_LN = 1e-5
EPS_RMS = float(np.finfo(np.float32).eps)
QSCALE = float(1.0 / np.sqrt(2048.0))
WSCALE = float(1.0 / np.sqrt(8192.0))
INV_D = float(1.0 / 2048.0)
NEGINF = -1e30
FFN_HARDEN = True

def gtile(h, s):
    if h == 0:
        return 2 * s + (1 if s % 2 else 0)
    return 2 * s + (0 if s % 2 else 1)



class Arena:
    def __init__(self, nc, ctx, elems=100 * 1024):
        self.t = ctx.enter_context(nc.sbuf_tensor("arena", [128, elems], BF16))
        self.off = 0
        self.elems = elems

    def _take(self, n_bf):
        off = (self.off + 31) & ~31
        self.off = off + n_bf
        assert self.off <= self.elems, f"SBUF arena overflow: {self.off} > {self.elems}"
        return off

    def bf(self, *shape):
        n = int(np.prod(shape[1:]))
        off = self._take(n)
        ap = self.t[0:shape[0], off:off + n]
        if len(shape) == 3:
            ap = ap.rearrange("p (a b) -> p a b", a=shape[1], b=shape[2])
        return ap

    def f32(self, *shape):
        n = int(np.prod(shape[1:]))
        off = self._take(2 * n)
        ap = self.t[0:shape[0], off:off + 2 * n].bitcast(F32)
        if len(shape) == 3:
            ap = ap.rearrange("p (a b) -> p a b", a=shape[1], b=shape[2])
        return ap

    def i32(self, *shape):
        n = int(np.prod(shape[1:]))
        off = self._take(2 * n)
        return self.t[0:shape[0], off:off + 2 * n].bitcast(I32)

    def mark(self):
        return self.off

    def reset(self, m):
        self.off = m


class KB:
    ENGINES = ("sync", "gpsimd", "scalar", "vector", "tensor")

    def __init__(self, nc, ctx):
        self.nc, self.ctx = nc, ctx
        self.th = {e: [] for e in self.ENGINES}
        self.semh, self.cnt = {}, {}
        for s in ("sL", "sG", "sP", "sV", "sA", "sB", "sO", "sF"):
            self.semh[s] = ctx.enter_context(nc.semaphore(s))
            self.cnt[s] = 0
        self.regs = {}

    def bump(self, s, amt):
        self.cnt[s] += amt
        return self.cnt[s]

    def emit(self, eng, fn):
        self.th[eng].append(fn)

    def now(self):
        return tuple((s, self.cnt[s]) for s in ("sP", "sV", "sA", "sL", "sG")
                     if self.cnt[s] > 0)

    def op(self, eng, build, wait=(), inc=None, amt=1):
        semh = self.semh
        tgt = None
        if inc:
            tgt = (semh[inc], amt)
            self.bump(inc, amt)
        wl = []
        for w in wait:
            if isinstance(w, tuple) and len(w) == 2 and isinstance(w[0], str):
                wl.append(w)
            else:
                wl.extend(w)
        wl = tuple(wl)

        def f(e):
            for sn, v in wl:
                e.wait_ge(semh[sn], v)
            ins = build(e)
            if tgt is not None:
                ins.then_inc(*tgt)
        self.emit(eng, f)
        return self.cnt[inc] if inc else None

    def wait(self, eng, *waits):
        semh = self.semh
        wl = tuple(waits)

        def f(e):
            for sn, v in wl:
                e.wait_ge(semh[sn], v)
        self.emit(eng, f)

    def barrier(self):
        semh = self.semh["sB"]
        self.bump("sB", 5)
        target = self.cnt["sB"]
        own_sem = {"sync": "sL", "gpsimd": "sG", "vector": "sV",
                   "scalar": "sA", "tensor": "sP"}
        for en in self.ENGINES:
            own = own_sem[en]
            own_v = self.cnt[own]
            ownh = self.semh[own]

            def f(e, own_v=own_v, ownh=ownh):
                if own_v > 0:
                    e.wait_ge(ownh, own_v)
                e.nop().then_inc(semh, 1)
                e.wait_ge(semh, target)
            self.emit(en, f)

    def finalize(self):
        with self.nc.Block() as block:
            for ename, meth in (("sync", block.sync), ("gpsimd", block.gpsimd),
                                ("scalar", block.scalar), ("vector", block.vector),
                                ("tensor", block.tensor)):
                ths = self.th[ename]

                def body(eng, ths=ths):
                    for f in ths:
                        f(eng)
                meth(body)



def emit_ln(kb, A, src_chunk_fn, xn_bf, pb, consts, g0, eps, n_loads_eng="sync"):
    onescol_bf, onesrow_f = consts["onescol_bf"], consts["onesrow_f"]
    m = A.mark()
    xf = [A.f32(128, STW), A.f32(128, STW)]
    xb = A.bf(128, STW)
    sq = A.bf(128, STW)
    rows = A.f32(1, 8 * STW).rearrange("p (a b) -> p a b", a=8, b=STW)
    xf_free = [kb.now(), kb.now()]
    xb_free = kb.now()
    sq_free = kb.now()

    for st in range(NS):
        statx = pb[0][0:1, :]
        statq = pb[1][0:1, :]
        for dc in range(DC):
            b = dc % 2
            cL = kb.op(n_loads_eng, lambda e, dc=dc, st=st, b=b: e.dma_start(
                out=xf[b], in_=src_chunk_fn(dc, st)),
                wait=[xf_free[b]], inc="sL", amt=16)
            cV = kb.op("vector", lambda e, b=b: e.tensor_copy(xb, xf[b]),
                       wait=[("sL", cL), xb_free], inc="sV")
            xf_free[b] = ("sV", cV)
            cA = kb.op("scalar", lambda e: e.activation(sq, xb, AF.Square),
                       wait=[("sV", cV), sq_free], inc="sA")
            xb_free = ("sA", cA)
            cP1 = kb.op("tensor", lambda e, dc=dc: e.matmul(
                statx, onescol_bf, xb, start=(dc == 0), stop=(dc == DC - 1)),
                wait=[("sV", cV)], inc="sP")
            xb_free = ("sP", cP1)
            cP2 = kb.op("tensor", lambda e, dc=dc: e.matmul(
                statq, onescol_bf, sq, start=(dc == 0), stop=(dc == DC - 1)),
                wait=[("sA", cA)], inc="sP")
            sq_free = ("sP", cP2)
        mean_r = rows[:, 0, :]
        msq_r = rows[:, 1, :]
        var_r = rows[:, 2, :]
        sd_r = rows[:, 3, :]
        rstd_r = rows[:, 4, :]
        mr_r = rows[:, 5, :]
        c1 = kb.op("vector", lambda e: e.tensor_scalar_mul(mean_r, statx, INV_D),
                   wait=[("sP", kb.cnt["sP"])], inc="sV")
        c2 = kb.op("vector", lambda e: e.tensor_scalar_mul(msq_r, statq, INV_D),
                   wait=[("sV", c1)], inc="sV")
        c3 = kb.op("vector", lambda e: e.tensor_tensor(var_r, mean_r, mean_r, ALU.mult),
                   wait=[("sV", c2)], inc="sV")
        c4 = kb.op("vector", lambda e: e.tensor_tensor(var_r, msq_r, var_r, ALU.subtract),
                   wait=[("sV", c3)], inc="sV")
        c5 = kb.op("scalar", lambda e: e.activation(sd_r, var_r, AF.Sqrt, bias=eps),
                   wait=[("sV", c4)], inc="sA")
        c6 = kb.op("vector", lambda e: e.reciprocal(rstd_r, sd_r),
                   wait=[("sA", c5)], inc="sV")
        c7 = kb.op("vector", lambda e: e.tensor_tensor(mr_r, mean_r, rstd_r, ALU.mult),
                   wait=[("sV", c6)], inc="sV")
        rstd_bc = pb[2][:, :]
        mr_bc = pb[3][:, :]
        cb1 = kb.op("tensor", lambda e: e.matmul(rstd_bc, onesrow_f, rstd_r,
                                                 start=True, stop=True),
                    wait=[("sV", c7)], inc="sP")
        cb2 = kb.op("tensor", lambda e: e.matmul(mr_bc, onesrow_f, mr_r,
                                                 start=True, stop=True),
                    wait=[("sV", c7)], inc="sP")
        tmp = A.f32(128, STW)
        tmp_free = kb.now()
        for dc in range(DC):
            b = dc % 2
            cL = kb.op(n_loads_eng, lambda e, dc=dc, st=st, b=b: e.dma_start(
                out=xf[b], in_=src_chunk_fn(dc, st)),
                wait=[xf_free[b]], inc="sL", amt=16)
            cV = kb.op("vector", lambda e, b=b: e.tensor_tensor(tmp, xf[b], rstd_bc, ALU.mult),
                       wait=[("sL", cL), ("sP", cb1), tmp_free], inc="sV")
            xf_free[b] = ("sV", cV)
            cV2 = kb.op("vector", lambda e, dc=dc, st=st: e.tensor_tensor(
                xn_bf[dc][:, st * STW:(st + 1) * STW], tmp, mr_bc, ALU.subtract),
                wait=[("sV", cV), ("sP", cb2)], inc="sV")
            tmp_free = ("sV", cV2)
        kb.wait("tensor", ("sV", kb.cnt["sV"]))
    A.reset(m)


def emit_w2prep(kb, A, tn, layer, pb, consts):
    h64_bf = consts["h64_bf"]
    w2r = tn[f"w2r{layer}"]
    w2p = tn[f"w2p{layer}"]
    m = A.mark()
    NSTR = 8
    wbuf = A.bf(64, NSTR * STW)
    obuf = A.bf(64, NSTR * STW)
    wfree, ofree = kb.now(), kb.now()
    total = NCD * 64
    for blk in range(total // (NSTR * STW)):
        x0 = blk * NSTR * STW
        cG = kb.op("gpsimd", lambda e, x0=x0: e.dma_start(
            out=wbuf, in_=w2r[:, x0:x0 + NSTR * STW]),
            wait=[wfree], inc="sG", amt=16)
        lastA = None
        for k in range(NSTR):
            ps = pb[k % 2][0:64, :]
            cP = kb.op("tensor", lambda e, k=k, ps=ps: e.matmul(
                ps, h64_bf, wbuf[:, k * STW:(k + 1) * STW], start=True, stop=True),
                wait=[("sG", cG), ("sA", kb.cnt["sA"])],
                inc="sP")
            cA = kb.op("scalar", lambda e, k=k, ps=ps: e.activation(
                obuf[:, k * STW:(k + 1) * STW], ps, AF.Copy, scale=WSCALE),
                wait=[("sP", cP), ofree], inc="sA")
            lastA = cA
        wfree = ("sP", kb.cnt["sP"])
        b0 = x0 // 2048
        cS = kb.op("sync", lambda e, b0=b0: e.dma_start(
            out=w2p[:, b0:b0 + 2, :], in_=obuf.rearrange("p (b d) -> p b d", b=2, d=2048)),
            wait=[("sA", lastA)], inc="sL", amt=16)
        ofree = ("sL", cS)
    A.reset(m)


def emit_kprep(kb, A, tn, consts, k_sb):
    m = A.mark()
    tiles = {}
    for nm in ("kbr", "kbi", "kcr", "kci"):
        t = A.f32(128, JCN, 16)
        c = kb.op("sync", lambda e, nm=nm, t=t: e.dma_start(
            out=t, in_=tn[nm].rearrange("(c p) s -> p c s", p=128)),
            inc="sL", amt=16)
        tiles[nm] = (t, c)
    t1 = A.f32(128, JCN, 16)
    t2 = A.f32(128, JCN, 16)
    cw = kb.cnt["sL"]
    c1 = kb.op("vector", lambda e: e.tensor_mul(t1, tiles["kcr"][0], tiles["kbr"][0]),
               wait=[("sL", cw)], inc="sV")
    c2 = kb.op("vector", lambda e: e.tensor_mul(t2, tiles["kci"][0], tiles["kbi"][0]),
               wait=[("sV", c1)], inc="sV")
    c3 = kb.op("vector", lambda e: e.tensor_tensor(t1, t1, t2, ALU.subtract),
               wait=[("sV", c2)], inc="sV")
    c4 = kb.op("vector", lambda e: e.reduce_sum(k_sb.unsqueeze(2), t1, axis=AX.X),
               wait=[("sV", c3)], inc="sV")
    A.reset(m)
    return c4


def emit_proj(kb, A, tn, wname, wcols, kchunks, xn_bf, out_cb, pb, bank0,
              extra_wait=(), col_off=0):
    w = tn[wname]
    nfc = wcols // 128
    m = A.mark()
    wsb = [A.bf(128, kchunks, 128), A.bf(128, kchunks, 128)]
    wfree = [kb.now(), kb.now()]
    bank_free = [kb.now(), kb.now()]
    for fc in range(nfc):
        b = fc % 2
        cG = kb.op("gpsimd", lambda e, fc=fc, b=b: e.dma_start(
            out=wsb[b], in_=w.rearrange("(k p) f -> p k f", p=128)[
                :, :, col_off + fc * 128: col_off + (fc + 1) * 128]),
            wait=[wfree[b]] + list(extra_wait), inc="sG", amt=16)
        for st in range(NS):
            ps = pb[bank0 + (st % 2)][:, :]
            cP = None
            for dk in range(kchunks):
                cP = kb.op("tensor", lambda e, dk=dk, st=st, b=b, ps=ps: e.matmul(
                    ps, wsb[b][:, dk, :], xn_bf[dk][:, st * STW:(st + 1) * STW],
                    start=(dk == 0), stop=(dk == kchunks - 1)),
                    wait=([("sG", cG), bank_free[st % 2]] if dk == 0 else []),
                    inc="sP")
            bank_free[st % 2] = out_cb(fc, st, ps, cP)
        wfree[b] = ("sP", kb.cnt["sP"])
    A.reset(m)


def emit_layer0(kb, A, tn, pb, consts, k_sb, kprep_done):
    xT = tn["xT"]
    x1 = tn["x1"]
    s_sp = tn["s_sp"]
    normw_sb = consts["normw_sb"]
    onescol_bf, onesrow_f = consts["onescol_bf"], consts["onesrow_f"]
    h128_bf = consts["h128_bf"]

    for g in range(NG):
        t0 = g * GT
        mG = A.mark()
        xn_bf = [A.bf(128, GT) for _ in range(DC)]

        emit_ln(kb, A, lambda dc, st, t0=t0: xT[dc * 128:(dc + 1) * 128,
                                                t0 + st * STW: t0 + (st + 1) * STW],
                xn_bf, pb, consts, g, consts["eps_ln"])
        kb.barrier()

        mM = A.mark()
        y_bf = [A.bf(128, GT) for _ in range(JCN)]

        def y_evac(jc, st, ps, cP):
            cV = kb.op("vector", lambda e, jc=jc, st=st, ps=ps: e.tensor_scalar(
                y_bf[jc][:, st * STW:(st + 1) * STW], ps,
                k_sb[:, jc:jc + 1], None, ALU.mult),
                wait=[("sP", cP), ("sV", kprep_done[0])], inc="sV")
            return ("sV", cV)
        emit_proj(kb, A, tn, "w_inT", NCD, DC, xn_bf, y_evac, pb, 0)
        kb.wait("tensor", ("sV", kb.cnt["sV"]))

        out0_bf = [A.bf(128, GT) for _ in range(DC)]
        sq_t = A.bf(128, STW)
        sq_free = [kb.now()]
        rms_ps = [pb[4][0:1, :], pb[5][0:1, :]]

        def out0_evac(dc, st, ps, cP):
            cV = kb.op("vector", lambda e, dc=dc, st=st, ps=ps: e.tensor_copy(
                out0_bf[dc][:, st * STW:(st + 1) * STW], ps),
                wait=[("sP", cP)], inc="sV")
            cA = kb.op("scalar", lambda e, ps=ps: e.activation(sq_t, ps, AF.Square),
                       wait=[("sV", cV), sq_free[0]], inc="sA")
            cP2 = kb.op("tensor", lambda e, dc=dc, st=st: e.matmul(
                rms_ps[st], onescol_bf, sq_t, start=(dc == 0), stop=(dc == DC - 1)),
                wait=[("sA", cA)], inc="sP")
            sq_free[0] = ("sP", cP2)
            return ("sA", cA)
        emit_proj(kb, A, tn, "w_outT", D, JCN, y_bf, out0_evac, pb, 0)
        sV_out0 = kb.cnt["sV"]

        mrow = A.mark()
        rows = A.f32(1, 4 * STW).rearrange("p (a b) -> p a b", a=4, b=STW)
        rstd2_bc = [pb[6][:, :], pb[7][:, :]]
        for st in range(NS):
            ms_r, sd_r, rstd_r = rows[:, 0, :], rows[:, 1, :], rows[:, 2, :]
            c1 = kb.op("vector", lambda e, st=st: e.tensor_scalar_mul(ms_r, rms_ps[st], INV_D),
                       wait=[("sP", kb.cnt["sP"])], inc="sV")
            c2 = kb.op("scalar", lambda e: e.activation(sd_r, ms_r, AF.Sqrt, bias=consts["eps_rms"]),
                       wait=[("sV", c1)], inc="sA")
            c3 = kb.op("vector", lambda e: e.reciprocal(rstd_r, sd_r),
                       wait=[("sA", c2)], inc="sV")
            cb = kb.op("tensor", lambda e, st=st: e.matmul(
                rstd2_bc[st], onesrow_f, rstd_r, start=True, stop=True),
                wait=[("sV", c3)], inc="sP")
        cb_done = kb.cnt["sP"]
        tmpa = A.f32(128, STW)
        tmpa_free = kb.now()
        for dc in range(DC):
            for st in range(NS):
                cV = kb.op("vector", lambda e, dc=dc, st=st: e.tensor_scalar(
                    tmpa, out0_bf[dc][:, st * STW:(st + 1) * STW],
                    normw_sb[:, dc:dc + 1], None, ALU.mult),
                    wait=[tmpa_free, ("sP", cb_done), ("sV", sV_out0)], inc="sV")
                cV = kb.op("vector", lambda e, st=st: e.tensor_tensor(
                    tmpa, tmpa, rstd2_bc[st], ALU.mult),
                    wait=[("sV", cV)], inc="sV")
                cV = kb.op("vector", lambda e, dc=dc, st=st: e.tensor_tensor(
                    tmpa, tmpa, xn_bf[dc][:, st * STW:(st + 1) * STW], ALU.add),
                    wait=[("sV", cV)], inc="sV")
                cS = kb.op("sync", lambda e, dc=dc, st=st, t0=t0: e.dma_start(
                    out=s_sp[dc, :, t0 + st * STW: t0 + (st + 1) * STW], in_=tmpa),
                    wait=[("sV", cV)], inc="sL", amt=16)
                tmpa_free = ("sL", cS)
        if "y_dbg" in tn and g == 0:
            for jc in range(JCN):
                kb.op("sync", lambda e, jc=jc: e.dma_start(
                    out=tn["y_dbg"][jc, :, 0:GT], in_=y_bf[jc]),
                    wait=[("sV", kb.cnt["sV"])], inc="sL", amt=16)
            for dc in range(DC):
                kb.op("sync", lambda e, dc=dc: e.dma_start(
                    out=tn["o0_dbg"][dc, :, 0:GT], in_=out0_bf[dc]),
                    inc="sL", amt=16)
            for dc in range(DC):
                kb.op("sync", lambda e, dc=dc: e.dma_start(
                    out=tn["xn_dbg"][dc, :, 0:GT], in_=xn_bf[dc]),
                    inc="sL", amt=16)
            kb.wait("sync", ("sL", kb.cnt["sL"]))
        A.reset(mrow)
        A.reset(mM)
        kb.barrier()

        emit_ffn(kb, A, tn, pb, consts, xn_bf, layer=0, g=g)
        A.reset(mG)
        kb.barrier()


def emit_ffn(kb, A, tn, pb, consts, xn_bf, layer, g):
    h128_bf = consts["h128_bf"]
    t0 = g * GT
    w1 = tn[f"w1t{layer}"]
    w3 = tn[f"w3t{layer}"]
    w2p = tn[f"w2p{layer}"]
    mF = A.mark()
    y1 = [A.bf(128, GT) for _ in range(HCN)]
    w1sb = [A.bf(128, DC, 128), A.bf(128, DC, 128)]
    w3sb = [A.bf(128, DC, 128), A.bf(128, DC, 128)]
    silu_t = A.bf(128, STW)
    hid_t = A.bf(128, STW)
    w1free = [kb.now(), kb.now()]
    w3free = [kb.now(), kb.now()]
    silu_free, hid_free = kb.now(), kb.now()
    bga_free = kb.now()
    bgl_free = kb.now()
    by1_free = kb.now()

    for hc in range(HCN):
        b = hc % 2
        cG1 = kb.op("gpsimd", lambda e, hc=hc, b=b: e.dma_start(
            out=w1sb[b], in_=w1.rearrange("(k p) f -> p k f", p=128)[:, :, hc * 128:(hc + 1) * 128]),
            wait=[w1free[b]] + ([kb.now()] if FFN_HARDEN else []), inc="sG", amt=16)
        cG3 = kb.op("gpsimd", lambda e, hc=hc, b=b: e.dma_start(
            out=w3sb[b], in_=w3.rearrange("(k p) f -> p k f", p=128)[:, :, hc * 128:(hc + 1) * 128]),
            wait=[w3free[b]] + ([kb.now()] if FFN_HARDEN else []), inc="sG", amt=16)
        for st in range(NS):
            psg = pb[0 + st][:, :]
            psl = pb[2 + st][:, :]
            psy = pb[4 + st][:, :]
            cPg = None
            for dk in range(DC):
                cPg = kb.op("tensor", lambda e, dk=dk, st=st, b=b: e.matmul(
                    psg, w1sb[b][:, dk, :], xn_bf[dk][:, st * STW:(st + 1) * STW],
                    start=(dk == 0), stop=(dk == DC - 1)),
                    wait=([("sG", cG1), bga_free] + ([kb.now()] if FFN_HARDEN else []) if dk == 0 else []), inc="sP")
            cPl = None
            for dk in range(DC):
                cPl = kb.op("tensor", lambda e, dk=dk, st=st, b=b: e.matmul(
                    psl, w3sb[b][:, dk, :], xn_bf[dk][:, st * STW:(st + 1) * STW],
                    start=(dk == 0), stop=(dk == DC - 1)),
                    wait=([("sG", cG3), bgl_free] + ([kb.now()] if FFN_HARDEN else []) if dk == 0 else []), inc="sP")
            cA = kb.op("scalar", lambda e: e.activation(silu_t, psg, AF.Silu),
                       wait=[("sP", cPg), silu_free] + ([kb.now()] if FFN_HARDEN else []), inc="sA")
            bga_free = ("sA", cA)
            cV = kb.op("vector", lambda e: e.tensor_tensor(hid_t, silu_t, psl, ALU.mult),
                       wait=[("sA", cA), ("sP", cPl), hid_free] + ([kb.now()] if FFN_HARDEN else []), inc="sV")
            silu_free = ("sV", cV)
            bgl_free = ("sV", cV)
            cPy = kb.op("tensor", lambda e: e.matmul(psy, h128_bf, hid_t,
                                                     start=True, stop=True),
                        wait=[("sV", cV), by1_free] + ([kb.now()] if FFN_HARDEN else []), inc="sP")
            hid_free = ("sP", cPy)
            cA2 = kb.op("scalar", lambda e, hc=hc, st=st: e.activation(
                y1[hc][:, st * STW:(st + 1) * STW], psy, AF.Copy),
                wait=[("sP", cPy)] + ([kb.now()] if FFN_HARDEN else []), inc="sA")
            by1_free = ("sA", cA2)
        w1free[b] = ("sP", kb.cnt["sP"])
        w3free[b] = ("sP", kb.cnt["sP"])

    mC = A.mark()
    w2sb = [A.bf(128, 256), A.bf(128, 256)]
    w2free = [kb.now(), kb.now()]
    gx_t = A.f32(128, STW)
    sx_t = A.f32(128, STW)
    xx_t = A.f32(128, STW)
    o_t = A.f32(128, STW)
    ev_free = kb.now()
    for ps_i in range(8):
        if FFN_HARDEN:
            kb.barrier()
        dc0 = ps_i * 2
        banks = [pb[4 + i][:, :] for i in range(4)]
        bank_wait = ("sA", kb.cnt["sA"])
        cP = None
        for hc in range(HCN):
            b = hc % 2
            cG = kb.op("gpsimd", lambda e, hc=hc, dc0=dc0, b=b: e.dma_start(
                out=w2sb[b], in_=w2p[hc, :, dc0 * 128:(dc0 + 2) * 128]),
                wait=[w2free[b]] + ([kb.now()] if FFN_HARDEN else []), inc="sG", amt=16)
            for dcl in range(2):
                for st in range(NS):
                    ps = banks[dcl * 2 + st]
                    cP = kb.op("tensor", lambda e, hc=hc, dcl=dcl, st=st, b=b, ps=ps: e.matmul(
                        ps, w2sb[b][:, dcl * 128:(dcl + 1) * 128],
                        y1[hc][:, st * STW:(st + 1) * STW],
                        start=(hc == 0), stop=(hc == HCN - 1)),
                        wait=([("sG", cG)] + ([bank_wait] if hc == 0 else [])),
                        inc="sP")
            w2free[b] = ("sP", kb.cnt["sP"])
        for dcl in range(2):
            dc = dc0 + dcl
            for st in range(NS):
                ps = banks[dcl * 2 + st]
                cA = kb.op("scalar", lambda e, ps=ps: e.activation(gx_t, ps, AF.Silu),
                           wait=[("sP", cP), ev_free], inc="sA")
                if layer == 0:
                    cL1 = kb.op("sync", lambda e, dc=dc, st=st: e.dma_start(
                        out=sx_t, in_=tn["s_sp"][dc, :, t0 + st * STW: t0 + (st + 1) * STW]),
                        wait=[ev_free], inc="sL", amt=16)
                    cL2 = kb.op("sync", lambda e, dc=dc, st=st: e.dma_start(
                        out=xx_t, in_=tn["xT"][dc * 128:(dc + 1) * 128,
                                               t0 + st * STW: t0 + (st + 1) * STW]),
                        inc="sL", amt=16)
                    cV = kb.op("vector", lambda e: e.tensor_tensor(o_t, sx_t, gx_t, ALU.mult),
                               wait=[("sA", cA), ("sL", cL2), ev_free], inc="sV")
                    cV = kb.op("vector", lambda e: e.tensor_tensor(o_t, o_t, xx_t, ALU.add),
                               wait=[("sV", cV)], inc="sV")
                    cS = kb.op("sync", lambda e, dc=dc, st=st: e.dma_start(
                        out=tn["x1"][dc, :, t0 + st * STW: t0 + (st + 1) * STW], in_=o_t),
                        wait=[("sV", cV)], inc="sL", amt=16)
                    ev_free = ("sL", cS)
                else:
                    cS = kb.op("sync", lambda e, dc=dc, st=st: e.dma_start(
                        out=tn["g1"][dc, :, t0 + st * STW: t0 + (st + 1) * STW], in_=gx_t),
                        wait=[("sA", cA)], inc="sL", amt=16)
                    ev_free = ("sL", cS)
    A.reset(mC)
    A.reset(mF)


def emit_ln_inner(kb, A, src_bf, out_bf, pb, consts, eps):
    onescol_bf, onesrow_f = consts["onescol_bf"], consts["onesrow_f"]
    gamma_sb, beta_sb = consts["gamma_sb"], consts["beta_sb"]
    m = A.mark()
    sq = A.bf(128, STW)
    rows = A.f32(1, 8 * STW).rearrange("p (a b) -> p a b", a=8, b=STW)
    tmp = A.f32(128, STW)
    sq_free = kb.now()
    tmp_free = kb.now()
    for st in range(NS):
        statx = pb[0][0:1, :]
        statq = pb[1][0:1, :]
        for dc in range(DC):
            cA = kb.op("scalar", lambda e, dc=dc, st=st: e.activation(
                sq, src_bf[dc][:, st * STW:(st + 1) * STW], AF.Square),
                wait=[sq_free], inc="sA")
            cP1 = kb.op("tensor", lambda e, dc=dc, st=st: e.matmul(
                statx, onescol_bf, src_bf[dc][:, st * STW:(st + 1) * STW],
                start=(dc == 0), stop=(dc == DC - 1)), inc="sP")
            cP2 = kb.op("tensor", lambda e, dc=dc: e.matmul(
                statq, onescol_bf, sq, start=(dc == 0), stop=(dc == DC - 1)),
                wait=[("sA", cA)], inc="sP")
            sq_free = ("sP", cP2)
        mean_r, msq_r, var_r = rows[:, 0, :], rows[:, 1, :], rows[:, 2, :]
        sd_r, rstd_r, mr_r = rows[:, 3, :], rows[:, 4, :], rows[:, 5, :]
        c1 = kb.op("vector", lambda e: e.tensor_scalar_mul(mean_r, statx, INV_D),
                   wait=[("sP", kb.cnt["sP"])], inc="sV")
        c2 = kb.op("vector", lambda e: e.tensor_scalar_mul(msq_r, statq, INV_D),
                   wait=[("sV", c1)], inc="sV")
        c3 = kb.op("vector", lambda e: e.tensor_tensor(var_r, mean_r, mean_r, ALU.mult),
                   wait=[("sV", c2)], inc="sV")
        c4 = kb.op("vector", lambda e: e.tensor_tensor(var_r, msq_r, var_r, ALU.subtract),
                   wait=[("sV", c3)], inc="sV")
        c5 = kb.op("scalar", lambda e: e.activation(sd_r, var_r, AF.Sqrt, bias=eps),
                   wait=[("sV", c4)], inc="sA")
        c6 = kb.op("vector", lambda e: e.reciprocal(rstd_r, sd_r),
                   wait=[("sA", c5)], inc="sV")
        c7 = kb.op("vector", lambda e: e.tensor_tensor(mr_r, mean_r, rstd_r, ALU.mult),
                   wait=[("sV", c6)], inc="sV")
        rstd_bc = pb[2][:, :]
        mr_bc = pb[3][:, :]
        cb1 = kb.op("tensor", lambda e: e.matmul(rstd_bc, onesrow_f, rstd_r,
                                                 start=True, stop=True),
                    wait=[("sV", c7)], inc="sP")
        cb2 = kb.op("tensor", lambda e: e.matmul(mr_bc, onesrow_f, mr_r,
                                                 start=True, stop=True),
                    wait=[("sV", c7)], inc="sP")
        for dc in range(DC):
            cV = kb.op("vector", lambda e, dc=dc, st=st: e.tensor_tensor(
                tmp, src_bf[dc][:, st * STW:(st + 1) * STW], rstd_bc, ALU.mult),
                wait=[("sP", cb1), tmp_free], inc="sV")
            cV = kb.op("vector", lambda e: e.tensor_tensor(tmp, tmp, mr_bc, ALU.subtract),
                       wait=[("sV", cV), ("sP", cb2)], inc="sV")
            cV = kb.op("vector", lambda e, dc=dc, st=st: e.tensor_scalar(
                out_bf[dc][:, st * STW:(st + 1) * STW], tmp,
                gamma_sb[:, dc:dc + 1], beta_sb[:, dc:dc + 1], ALU.mult, ALU.add),
                wait=[("sV", cV)], inc="sV")
            tmp_free = ("sV", cV)
        kb.wait("tensor", ("sV", kb.cnt["sV"]))
    A.reset(m)


def emit_l1_front(kb, A, tn, pb, consts, fused):
    x1 = tn["x1"]
    for g in range(NG):
        t0 = g * GT
        mG = A.mark()
        xn1_bf = [A.bf(128, GT) for _ in range(DC)]
        emit_ln(kb, A, lambda dc, st, t0=t0: x1[dc, :, t0 + st * STW: t0 + (st + 1) * STW],
                xn1_bf, pb, consts, g, consts["eps_ln"])
        for dc in range(DC):
            kb.op("sync", lambda e, dc=dc, t0=t0: e.dma_start(
                out=tn["xn1b"][dc, :, t0:t0 + GT], in_=xn1_bf[dc]),
                wait=[("sV", kb.cnt["sV"])], inc="sL", amt=16)
        kb.barrier()

        mX = A.mark()
        xnorm_bf = [A.bf(128, GT) for _ in range(DC)]
        emit_ln_inner(kb, A, xn1_bf, xnorm_bf, pb, consts, consts["eps_ln"])
        if "xnorm_dbg" in tn:
            for dc in range(DC):
                kb.op("sync", lambda e, dc=dc, t0=t0: e.dma_start(
                    out=tn["xnorm_dbg"][dc, :, t0:t0 + GT], in_=xnorm_bf[dc]),
                    wait=[("sV", kb.cnt["sV"])], inc="sL", amt=16)
        kb.barrier()

        qd = tn["qd"]
        mQ = A.mark()
        q_tile = A.bf(128, STW)
        qk_state = {"free": kb.now()}

        def q_evac(fc, st, ps, cP):
            cV = kb.op("vector", lambda e, ps=ps: e.tensor_scalar_mul(
                q_tile, ps, QSCALE), wait=[("sP", cP), qk_state["free"]], inc="sV")
            slot0 = g * 8 + st * 4
            dst = qd[slot0:slot0 + 4, :, fc, :].rearrange("j p t -> p j t")
            cS = kb.op("sync", lambda e, dst=dst: e.dma_start(
                out=dst, in_=q_tile.rearrange("p (j t) -> p j t", j=4, t=128)),
                wait=[("sV", cV)], inc="sL", amt=16)
            qk_state["free"] = ("sL", cS)
            return ("sV", cV)
        emit_proj(kb, A, tn, "qkv", D, DC, xnorm_bf, q_evac, pb, 0, col_off=0)

        def k_evac(fc, st, ps, cP):
            cV = kb.op("vector", lambda e, ps=ps: e.tensor_copy(q_tile, ps),
                       wait=[("sP", cP), qk_state["free"]], inc="sV")
            dst = tn["kown"][fc, :, t0 + st * STW: t0 + (st + 1) * STW]
            cS = kb.op("sync", lambda e, dst=dst: e.dma_start(
                out=dst, in_=q_tile), wait=[("sV", cV)], inc="sL", amt=16)
            qk_state["free"] = ("sL", cS)
            return ("sV", cV)
        emit_proj(kb, A, tn, "qkv", D, DC, xnorm_bf, k_evac, pb, 0, col_off=D)
        A.reset(mQ)
        kb.wait("vector", ("sL", kb.cnt["sL"]))

        mV = A.mark()
        vw = [A.bf(128, DC, STW), A.bf(128, DC, STW)]
        v_t = A.bf(128, STW)
        vw_free = [kb.now(), kb.now()]
        vt_free = kb.now()
        qkvr = tn["qkv"].rearrange("(k p) f -> p k f", p=128)
        for fst in range(4):
            b = fst % 2
            cG = kb.op("gpsimd", lambda e, fst=fst, b=b: e.dma_start(
                out=vw[b], in_=qkvr[:, :, 2 * D + fst * STW: 2 * D + (fst + 1) * STW]),
                wait=[vw_free[b]], inc="sG", amt=16)
            for tc in range(8):
                ps = pb[fst % 2][:, :]
                cP = None
                for dk in range(DC):
                    cP = kb.op("tensor", lambda e, dk=dk, tc=tc, b=b, ps=ps: e.matmul(
                        ps, xnorm_bf[dk][:, tc * 128:(tc + 1) * 128],
                        vw[b][:, dk, :], start=(dk == 0), stop=(dk == DC - 1)),
                        wait=([("sG", cG), ("sV", kb.cnt["sV"])]
                              if dk == 0 else []), inc="sP")
                cV = kb.op("vector", lambda e, ps=ps: e.tensor_copy(v_t, ps),
                           wait=[("sP", cP), vt_free], inc="sV")
                slot = g * 8 + tc
                dst = tn["vown"][slot, :, fst * STW:(fst + 1) * STW]
                cS = kb.op("sync", lambda e, dst=dst: e.dma_start(out=dst, in_=v_t),
                           wait=[("sV", cV)], inc="sL", amt=16)
                vt_free = ("sL", cS)
            vw_free[b] = ("sP", kb.cnt["sP"])
        A.reset(mV)
        A.reset(mX)
        kb.barrier()

        emit_ffn(kb, A, tn, pb, consts, xn1_bf, layer=1, g=g)
        A.reset(mG)
        kb.barrier()


def emit_attention(kb, A, tn, pb, consts, fused):
    ident_bf = consts["ident_bf"]
    kg = tn["kg"]
    vg = tn["vg"]
    mask = tn["mask"]
    mAt = A.mark()
    ao_fm = [A.bf(128, T) for _ in range(DC)]
    pt_bf = pb[2].bitcast(BF16)[:, 0:128]
    av = [pb[4 + i][:, :] for i in range(4)]

    for s in range(NSLOT):
        kb.barrier()
        W = 2 * (s + 1) * 128
        off_s = 128 * s * (s + 1)
        mS = A.mark()
        q_sb = A.bf(128, DC, 128)
        mask_sb = A.bf(128, W)
        S_f = A.f32(128, W)
        P_bf = A.bf(128, W)
        k_sb = [A.bf(128, DC, 256), A.bf(128, DC, 256)]
        v_sb = [A.bf(128, D), A.bf(128, D)]
        nm = A.f32(128, 1)
        l_f = A.f32(128, 1)
        linv = A.f32(128, 1)
        ao_tm = A.bf(128, D)
        pt_sb = A.bf(128, 128)

        cq = kb.op("sync", lambda e, s=s, q_sb=q_sb: e.dma_start(
            out=q_sb, in_=tn["qd"][s]), wait=[("sV", kb.cnt["sV"])], inc="sL", amt=16)
        cm = kb.op("sync", lambda e, off_s=off_s, W=W, mask_sb=mask_sb: e.dma_start(
            out=mask_sb, in_=mask[:, off_s:off_s + W]), inc="sL", amt=16)

        kb_free = [kb.now(), kb.now()]
        sbank_free = [kb.now(), kb.now()]
        si = 0
        for r in range(2):
            ncols = (s + 1) * 128
            c0 = 0
            while c0 < ncols:
                w = min(256, ncols - c0)
                b = si % 2
                cK = kb.op("sync", lambda e, r=r, c0=c0, w=w, b=b, k_sb=k_sb: e.dma_start(
                    out=k_sb[b][:, :, 0:w],
                    in_=kg[r * 16:(r + 1) * 16, :, c0:c0 + w].rearrange("f p t -> p f t")),
                    wait=[kb_free[b]], inc="sL", amt=16)
                ps = pb[si % 2][:, 0:w]
                cP = None
                for fc in range(DC):
                    cP = kb.op("tensor", lambda e, fc=fc, b=b, w=w, ps=ps, q_sb=q_sb, k_sb=k_sb: e.matmul(
                        ps, q_sb[:, fc, :], k_sb[b][:, fc, 0:w],
                        start=(fc == 0), stop=(fc == DC - 1)),
                        wait=([("sL", cK), sbank_free[si % 2]] if fc == 0 else []),
                        inc="sP")
                col0 = r * (s + 1) * 128 + c0
                cV = kb.op("vector", lambda e, col0=col0, w=w, ps=ps, S_f=S_f, mask_sb=mask_sb: e.tensor_tensor(
                    S_f[:, col0:col0 + w], ps, mask_sb[:, col0:col0 + w], ALU.add),
                    wait=[("sP", cP), ("sL", cm)], inc="sV")
                kb_free[b] = ("sP", cP)
                sbank_free[si % 2] = ("sV", cV)
                c0 += w
                si += 1

        c1 = kb.op("vector", lambda e, S_f=S_f, nm=nm: e.reduce_max(
            nm, S_f, axis=AX.X, negate=True),
            wait=[("sV", kb.cnt["sV"])], inc="sV")
        c2 = kb.op("scalar", lambda e, P_bf=P_bf, S_f=S_f, nm=nm, l_f=l_f: e.activation(
            P_bf, S_f, AF.Exp, bias=nm, accum_out=l_f),
            wait=[("sV", c1)], inc="sA")
        c3 = kb.op("vector", lambda e, linv=linv, l_f=l_f: e.reciprocal(linv, l_f),
                   wait=[("sA", c2)], inc="sV")

        pt_free = kb.now()
        ptsb_free = kb.now()
        v_free = [kb.now(), kb.now()]
        njs = 2 * (s + 1)
        cPav = None
        for j in range(njs):
            r = 0 if j <= s else 1
            l = j if j <= s else j - (s + 1)
            b = j % 2
            cT = kb.op("tensor", lambda e, j=j, P_bf=P_bf: e.transpose(
                pt_bf, P_bf[:, j * 128:(j + 1) * 128], ident_bf),
                wait=[("sA", c2), pt_free], inc="sP")
            cVt = kb.op("vector", lambda e: e.tensor_copy(pt_sb, pt_bf),
                        wait=[("sP", cT), ptsb_free], inc="sV")
            pt_free = ("sV", cVt)
            cL = kb.op("sync", lambda e, r=r, l=l, b=b, v_sb=v_sb: e.dma_start(
                out=v_sb[b], in_=vg[r * 16 + l]),
                wait=[v_free[b]], inc="sL", amt=16)
            for fst in range(4):
                cPav = kb.op("tensor", lambda e, fst=fst, j=j, b=b, v_sb=v_sb: e.matmul(
                    av[fst], pt_sb, v_sb[b][:, fst * STW:(fst + 1) * STW],
                    start=(j == 0), stop=(j == njs - 1)),
                    wait=([("sV", cVt), ("sL", cL)] +
                          ([("sV", kb.cnt["sV"])] if j == 0 else [])) if fst == 0
                    else [],
                    inc="sP")
            ptsb_free = ("sP", cPav)
            v_free[b] = ("sP", cPav)

        for fst in range(4):
            cV = kb.op("vector", lambda e, fst=fst, ao_tm=ao_tm, linv=linv: e.tensor_scalar(
                ao_tm[:, fst * STW:(fst + 1) * STW], av[fst], linv, None, ALU.mult),
                wait=[("sP", cPav), ("sV", c3)], inc="sV")
        sv_ao = kb.cnt["sV"]
        for dc in range(DC):
            cT = kb.op("tensor", lambda e, dc=dc, ao_tm=ao_tm: e.transpose(
                pt_bf, ao_tm[:, dc * 128:(dc + 1) * 128], ident_bf),
                wait=[("sV", sv_ao), pt_free], inc="sP")
            cVt = kb.op("vector", lambda e, dc=dc, s=s: e.tensor_copy(
                ao_fm[dc][:, s * 128:(s + 1) * 128], pt_bf),
                wait=[("sP", cT)], inc="sV")
            pt_free = ("sV", cVt)
        kb.wait("tensor", ("sV", kb.cnt["sV"]))
        A.reset(mS)

    kb.barrier()
    mO = A.mark()
    wsb = [A.bf(128, DC, 128), A.bf(128, DC, 128)]
    xn1_t = A.bf(128, STW)
    s1_t = A.f32(128, STW)
    g1_t = A.f32(128, STW)
    x1_t = A.f32(128, STW)
    o_t = A.f32(128, STW)
    w_free = [kb.now(), kb.now()]
    ev_free = kb.now()
    bank_free = [kb.now(), kb.now()]
    opr = tn["o_proj"].rearrange("(k p) f -> p k f", p=128)
    for fc in range(DC):
        b = fc % 2
        cG = kb.op("gpsimd", lambda e, fc=fc, b=b: e.dma_start(
            out=wsb[b], in_=opr[:, :, fc * 128:(fc + 1) * 128]),
            wait=[w_free[b]], inc="sG", amt=16)
        for st4 in range(4):
            ps = pb[st4 % 2][:, :]
            cP = None
            for dk in range(DC):
                cP = kb.op("tensor", lambda e, dk=dk, st4=st4, b=b, ps=ps: e.matmul(
                    ps, wsb[b][:, dk, :], ao_fm[dk][:, st4 * STW:(st4 + 1) * STW],
                    start=(dk == 0), stop=(dk == DC - 1)),
                    wait=([("sG", cG), bank_free[st4 % 2]] if dk == 0 else []),
                    inc="sP")
            cL1 = kb.op("sync", lambda e, fc=fc, st4=st4: e.dma_start(
                out=xn1_t, in_=tn["xn1b"][fc, :, st4 * STW:(st4 + 1) * STW]),
                wait=[ev_free], inc="sL", amt=16)
            cL2 = kb.op("sync", lambda e, fc=fc, st4=st4: e.dma_start(
                out=g1_t, in_=tn["g1"][fc, :, st4 * STW:(st4 + 1) * STW]),
                inc="sL", amt=16)
            cL3 = kb.op("sync", lambda e, fc=fc, st4=st4: e.dma_start(
                out=x1_t, in_=tn["x1"][fc, :, st4 * STW:(st4 + 1) * STW]),
                inc="sL", amt=16)
            cV = kb.op("vector", lambda e, ps=ps: e.tensor_tensor(s1_t, ps, xn1_t, ALU.add),
                       wait=[("sP", cP), ("sL", cL3), ev_free], inc="sV")
            cV = kb.op("vector", lambda e: e.tensor_tensor(s1_t, s1_t, g1_t, ALU.mult),
                       wait=[("sV", cV)], inc="sV")
            cV = kb.op("vector", lambda e: e.tensor_tensor(o_t, s1_t, x1_t, ALU.add),
                       wait=[("sV", cV)], inc="sV")
            cS = kb.op("sync", lambda e, fc=fc, st4=st4: e.dma_start(
                out=tn["outT"][fc * 128:(fc + 1) * 128, st4 * STW:(st4 + 1) * STW],
                in_=o_t), wait=[("sV", cV)], inc="sL", amt=16)
            ev_free = ("sL", cS)
            bank_free[st4 % 2] = ("sV", cV)
        w_free[b] = ("sP", kb.cnt["sP"])
    A.reset(mO)
    A.reset(mAt)


def emit_reg_setup(kb, par_sb, non_sb, wait_c):
    semh = kb.semh

    def f(e):
        e.wait_ge(semh["sL"], wait_c)
        pr = e.alloc_register("pr")
        e.reg_load(pr, par_sb[0:1, 0:1])
        kb.regs["par"] = RuntimeValue(pr, 0, 1)
        qr = e.alloc_register("qr")
        e.reg_alu(qr, 1, RuntimeValue(pr, 0, 1), ALU.subtract)
        kb.regs["peer"] = RuntimeValue(qr, 0, 1)
        nr = e.alloc_register("nr")
        e.reg_load(nr, non_sb[0:1, 0:1])
        kb.regs["nonce"] = RuntimeValue(nr)
    kb.emit("sync", f)


def emit_kv_handshake(kb, A, tn, non_sb, fsb):
    shf = tn["shflag"]
    shk2 = tn["shk"].rearrange("(r x) t -> r (x t)", r=2)
    shv2 = tn["shv"].rearrange("(r x) t -> r (x t)", r=2)
    ksrc = tn["kown"].rearrange("a p t -> (a p t)").rearrange("(o e) -> o e", o=1)
    vsrc = tn["vown"].rearrange("a p t -> (a p t)").rearrange("(o e) -> o e", o=1)
    cK = kb.op("sync", lambda e: e.dma_start(
        out=shk2[ds(kb.regs["par"], 1), :], in_=ksrc),
        wait=[("sL", kb.cnt["sL"])], inc="sL", amt=16)
    cV = kb.op("sync", lambda e: e.dma_start(
        out=shv2[ds(kb.regs["par"], 1), :], in_=vsrc),
        inc="sL", amt=16)
    cW = kb.op("sync", lambda e: e.dma_start(
        out=shf[ds(kb.regs["par"], 1), :], in_=non_sb),
        wait=[("sL", cV)], inc="sL", amt=16)
    kb.wait("sync", ("sL", cW))
    semF = kb.semh["sF"]

    def poll(e):
        freg = e.alloc_register("freg")
        creg = e.alloc_register("creg")
        e.reg_mov(creg, 1)
        with e.While(creg):
            e.sem_clear(semF)
            e.dma_start(out=fsb, in_=shf[ds(kb.regs["peer"], 1), :]).then_inc(semF, 16)
            e.wait_ge(semF, 16)
            e.reg_load(freg, fsb[0:1, 0:1])
            e.reg_alu(creg, RuntimeValue(freg), kb.regs["nonce"], ALU.subtract)
    kb.emit("sync", poll)



def _token_idx(h):
    gs = [gtile(h, s) for s in range(NSLOT)]
    return np.concatenate([np.arange(g * 128, (g + 1) * 128) for g in gs])


def _host_common(inputs):
    c = {}
    c["w_inT"] = np.ascontiguousarray(inputs["L0_attn_W_in"].T)
    c["w_outT"] = np.ascontiguousarray(inputs["L0_attn_W_out"].T)
    for nm, key in (("kbr", "L0_attn_B_real"), ("kbi", "L0_attn_B_imag"),
                    ("kcr", "L0_attn_C_real"), ("kci", "L0_attn_C_imag")):
        c[nm] = np.ascontiguousarray(inputs[key].reshape(NCD, 16))
    c["normw"] = np.ascontiguousarray(inputs["L0_attn_norm_w"].reshape(DC, 128).T)
    for L in (0, 1):
        c[f"w1t{L}"] = np.ascontiguousarray(inputs[f"L{L}_ffn_W1"].T)
        c[f"w3t{L}"] = np.ascontiguousarray(inputs[f"L{L}_ffn_W3"].T)
        c[f"w2r{L}"] = np.ascontiguousarray(
            inputs[f"L{L}_ffn_W2"].reshape(D, 64, 128).transpose(1, 2, 0).reshape(64, 64 * NCD))
    c["qkv"] = np.ascontiguousarray(inputs["L1_attn_qkv"])
    c["o_proj"] = np.ascontiguousarray(inputs["L1_attn_o_proj"])
    c["gamma"] = np.ascontiguousarray(inputs["L1_attn_gamma"].reshape(DC, 128).T)
    c["beta"] = np.ascontiguousarray(inputs["L1_attn_beta"].reshape(DC, 128).T)
    c["h128"] = _hadamard(128)
    c["h64"] = _hadamard(64)
    c["ident"] = np.eye(128, dtype=np.float32)
    c["onescol"] = np.ones((128, 1), dtype=np.float32)
    c["onesrow"] = np.ones((1, 128), dtype=np.float32)
    return c


def _hadamard(n):
    h = np.array([[1.0]], dtype=np.float32)
    while h.shape[0] < n:
        h = np.block([[h, h], [h, -h]])
    return h.astype(np.float32)


def _common_tensors(nc, mode_l0=True):
    tn = {}

    def inp(name, shape, dt=F32):
        tn[name] = nc.dram_tensor(name, list(shape), dt, kind="ExternalInput").ap()

    def outp(name, shape, dt=F32):
        tn[name] = nc.dram_tensor(name, list(shape), dt, kind="ExternalOutput").ap()
    return tn, inp, outp


def _consts_prologue(kb, A, tn, need=("h128", "h64", "ident", "normw", "gamma", "beta")):
    nc = kb.nc
    consts = {}
    consts["onescol_bf"] = A.bf(128, 1)
    consts["onesrow_f"] = A.f32(1, 128)
    consts["eps_ln"] = A.f32(1, 1)
    consts["eps_rms"] = A.f32(1, 1)
    kb.op("vector", lambda e: e.memset(consts["eps_ln"], EPS_LN), inc="sV")
    kb.op("vector", lambda e: e.memset(consts["eps_rms"], EPS_RMS), inc="sV")
    kb.op("gpsimd", lambda e: e.dma_start(out=consts["onescol_bf"], in_=tn["onescol"]),
          inc="sG", amt=16)
    kb.op("sync", lambda e: e.dma_start(out=consts["onesrow_f"], in_=tn["onesrow"]),
          inc="sL", amt=16)
    if "h128" in need:
        consts["h128_bf"] = A.bf(128, 128)
        kb.op("gpsimd", lambda e: e.dma_start(out=consts["h128_bf"], in_=tn["h128"]),
              inc="sG", amt=16)
    if "h64" in need:
        consts["h64_bf"] = A.bf(64, 64)
        kb.op("gpsimd", lambda e: e.dma_start(out=consts["h64_bf"], in_=tn["h64"]),
              inc="sG", amt=16)
    if "ident" in need:
        consts["ident_bf"] = A.bf(128, 128)
        kb.op("gpsimd", lambda e: e.dma_start(out=consts["ident_bf"], in_=tn["ident"]),
              inc="sG", amt=16)
    if "normw" in need:
        consts["normw_sb"] = A.f32(128, DC)
        kb.op("sync", lambda e: e.dma_start(
            out=consts["normw_sb"], in_=tn["normw"]), inc="sL", amt=16)
    if "gamma" in need:
        consts["gamma_sb"] = A.f32(128, DC)
        consts["beta_sb"] = A.f32(128, DC)
        kb.op("sync", lambda e: e.dma_start(out=consts["gamma_sb"], in_=tn["gamma"]),
              inc="sL", amt=16)
        kb.op("sync", lambda e: e.dma_start(out=consts["beta_sb"], in_=tn["beta"]),
              inc="sL", amt=16)
    kb.wait("vector", ("sG", kb.cnt["sG"]), ("sL", kb.cnt["sL"]))
    kb.wait("tensor", ("sG", kb.cnt["sG"]), ("sL", kb.cnt["sL"]))
    kb.wait("scalar", ("sG", kb.cnt["sG"]), ("sL", kb.cnt["sL"]))
    return consts


def build_l0(dbg=False):
    nc = bass.Bass("TRN2", target_bir_lowering=False)
    tn, inp, outp = _common_tensors(nc)
    inp("xT", (D, T))
    inp("w_inT", (D, NCD))
    inp("w_outT", (NCD, D))
    for nm in ("kbr", "kbi", "kcr", "kci"):
        inp(nm, (NCD, 16))
    inp("normw", (128, DC))
    inp("w1t0", (D, HF))
    inp("w3t0", (D, HF))
    inp("w2r0", (64, NCD * 64))
    inp("h128", (128, 128))
    inp("h64", (64, 64))
    inp("onescol", (128, 1))
    inp("onesrow", (1, 128))
    outp("x1f", (DC, 128, T))
    tn["x1"] = tn["x1f"]
    if dbg:
        outp("s_dbg", (DC, 128, T))
        tn["s_sp"] = tn["s_dbg"]
        outp("w2p_dbg", (64, 128, D), BF16)
        tn["w2p0"] = tn["w2p_dbg"]
        outp("y_dbg", (JCN, 128, GT), BF16)
        outp("o0_dbg", (DC, 128, GT), BF16)
        outp("xn_dbg", (DC, 128, GT), BF16)
    else:
        tn["s_sp"] = nc.dram_tensor("s_sp", [DC, 128, T], F32, kind="Internal").ap()
        tn["w2p0"] = nc.dram_tensor("w2p0", [64, 128, D], BF16, kind="Internal").ap()

    with ExitStack() as ctx:
        kb = KB(nc, ctx)
        A = Arena(nc, ctx)
        pbh = [ctx.enter_context(nc.psum_tensor(f"pb{i}", [128, 512], F32))
               for i in range(8)]
        pb = [p[:, :] for p in pbh]
        consts = _consts_prologue(kb, A, tn, need=("h128", "h64", "normw"))
        k_sb = A.f32(128, JCN)
        kdone = emit_kprep(kb, A, tn, consts, k_sb)
        kb.barrier()
        emit_w2prep(kb, A, tn, 0, pb, consts)
        kb.barrier()
        emit_layer0(kb, A, tn, pb, consts, k_sb, [kdone])
        kb.wait("sync", ("sL", kb.cnt["sL"]))
        kb.finalize()
    return nc


def build_l1a(dbg=False):
    nc = bass.Bass("TRN2", target_bir_lowering=False)
    tn, inp, outp = _common_tensors(nc)
    inp("x1f", (DC, 128, T))
    inp("qkv", (D, 3 * D))
    inp("gamma", (128, DC))
    inp("beta", (128, DC))
    inp("w1t1", (D, HF))
    inp("w3t1", (D, HF))
    inp("w2r1", (64, NCD * 64))
    inp("h128", (128, 128))
    inp("h64", (64, 64))
    inp("onescol", (128, 1))
    inp("onesrow", (1, 128))
    outp("qd", (NSLOT, 128, DC, 128), BF16)
    outp("kown", (DC, 128, T), BF16)
    outp("vown", (NSLOT, 128, D), BF16)
    outp("xn1b", (DC, 128, T), BF16)
    outp("g1f", (DC, 128, T))
    if dbg:
        outp("xnorm_dbg", (DC, 128, T), BF16)
    tn["x1"] = tn["x1f"]
    tn["g1"] = tn["g1f"]
    tn["w2p1"] = nc.dram_tensor("w2p1", [64, 128, D], BF16, kind="Internal").ap()

    with ExitStack() as ctx:
        kb = KB(nc, ctx)
        A = Arena(nc, ctx)
        pbh = [ctx.enter_context(nc.psum_tensor(f"pb{i}", [128, 512], F32))
               for i in range(8)]
        pb = [p[:, :] for p in pbh]
        consts = _consts_prologue(kb, A, tn, need=("h128", "h64", "gamma"))
        emit_w2prep(kb, A, tn, 1, pb, consts)
        kb.barrier()
        emit_l1_front(kb, A, tn, pb, consts, fused=False)
        kb.wait("sync", ("sL", kb.cnt["sL"]))
        kb.finalize()
    return nc


def build_l1b():
    nc = bass.Bass("TRN2", target_bir_lowering=False)
    tn, inp, outp = _common_tensors(nc)
    inp("qd", (NSLOT, 128, DC, 128), BF16)
    inp("kg", (2 * DC, 128, T), BF16)
    inp("vg", (2 * NSLOT, 128, D), BF16)
    inp("mask", (128, 128 * NSLOT * (NSLOT + 1) * 2 // 2), BF16)
    inp("o_proj", (D, D))
    inp("xn1b", (DC, 128, T), BF16)
    inp("g1f", (DC, 128, T))
    inp("x1f", (DC, 128, T))
    inp("ident", (128, 128))
    inp("onescol", (128, 1))
    inp("onesrow", (1, 128))
    outp("outT", (D, T))
    tn["x1"] = tn["x1f"]
    tn["g1"] = tn["g1f"]

    with ExitStack() as ctx:
        kb = KB(nc, ctx)
        A = Arena(nc, ctx)
        pbh = [ctx.enter_context(nc.psum_tensor(f"pb{i}", [128, 512], F32))
               for i in range(8)]
        pb = [p[:, :] for p in pbh]
        consts = _consts_prologue(kb, A, tn, need=("ident",))
        emit_attention(kb, A, tn, pb, consts, fused=False)
        kb.wait("sync", ("sL", kb.cnt["sL"]))
        kb.finalize()
    return nc


def _mask_for(h):
    tri = np.triu(np.full((128, 128), NEGINF, np.float32), 1)
    zero = np.zeros((128, 128), np.float32)
    full = np.full((128, 128), NEGINF, np.float32)
    cols = []
    for s in range(NSLOT):
        g = gtile(h, s)
        for r in (0, 1):
            for l in range(s + 1):
                gp = gtile(r, l)
                cols.append(zero if gp < g else (tri if gp == g else full))
    return np.ascontiguousarray(
        np.concatenate(cols, axis=1).astype(ml_dtypes.bfloat16))


def build_fused():
    nc = bass.Bass("TRN2", target_bir_lowering=False)
    tn, inp, outp = _common_tensors(nc)
    inp("xT", (D, T))
    inp("w_inT", (D, NCD))
    inp("w_outT", (NCD, D))
    for nm in ("kbr", "kbi", "kcr", "kci"):
        inp(nm, (NCD, 16))
    inp("normw", (128, DC))
    inp("w1t0", (D, HF))
    inp("w3t0", (D, HF))
    inp("w2r0", (64, NCD * 64))
    inp("qkv", (D, 3 * D))
    inp("gamma", (128, DC))
    inp("beta", (128, DC))
    inp("w1t1", (D, HF))
    inp("w3t1", (D, HF))
    inp("w2r1", (64, NCD * 64))
    inp("o_proj", (D, D))
    inp("mask", (128, 128 * NSLOT * (NSLOT + 1)), BF16)
    inp("h128", (128, 128))
    inp("h64", (64, 64))
    inp("ident", (128, 128))
    inp("onescol", (128, 1))
    inp("onesrow", (1, 128))
    inp("parity", (1, 1), I32)
    inp("nonce", (1, 16), I32)
    outp("outT", (D, T))

    tn["x1"] = nc.dram_tensor("x1i", [DC, 128, T], F32, kind="Internal").ap()
    tn["s_sp"] = nc.dram_tensor("s_sp", [DC, 128, T], F32, kind="Internal").ap()
    tn["w2p0"] = nc.dram_tensor("w2p0", [64, 128, D], BF16, kind="Internal").ap()
    tn["w2p1"] = nc.dram_tensor("w2p1", [64, 128, D], BF16, kind="Internal").ap()
    tn["qd"] = nc.dram_tensor("qdi", [NSLOT, 128, DC, 128], BF16, kind="Internal").ap()
    tn["kown"] = nc.dram_tensor("kowni", [DC, 128, T], BF16, kind="Internal").ap()
    tn["vown"] = nc.dram_tensor("vowni", [NSLOT, 128, D], BF16, kind="Internal").ap()
    tn["xn1b"] = nc.dram_tensor("xn1bi", [DC, 128, T], BF16, kind="Internal").ap()
    tn["g1"] = nc.dram_tensor("g1i", [DC, 128, T], F32, kind="Internal").ap()
    tn["shk"] = nc.dram_tensor("shk", [2 * T, T], BF16, kind="Internal",
                               addr_space="Shared").ap()
    tn["shv"] = nc.dram_tensor("shv", [2 * T, D], BF16, kind="Internal",
                               addr_space="Shared").ap()
    tn["shflag"] = nc.dram_tensor("shflag", [2, 16], I32, kind="Internal",
                                  addr_space="Shared").ap()
    tn["kg"] = tn["shk"].rearrange("(a p) t -> a p t", p=128)
    tn["vg"] = tn["shv"].rearrange("(a p) t -> a p t", p=128)

    with ExitStack() as ctx:
        kb = KB(nc, ctx)
        A = Arena(nc, ctx)
        pbh = [ctx.enter_context(nc.psum_tensor(f"pb{i}", [128, 512], F32))
               for i in range(8)]
        pb = [p[:, :] for p in pbh]
        consts = _consts_prologue(kb, A, tn,
                                  need=("h128", "h64", "ident", "normw", "gamma"))
        par_sb = A.i32(1, 1)
        non_sb = A.i32(1, 16)
        fsb = A.i32(1, 16)
        kb.op("sync", lambda e: e.dma_start(out=par_sb, in_=tn["parity"]),
              inc="sL", amt=16)
        cN = kb.op("sync", lambda e: e.dma_start(out=non_sb, in_=tn["nonce"]),
                   inc="sL", amt=16)
        emit_reg_setup(kb, par_sb, non_sb, cN)
        k_sb = A.f32(128, JCN)
        kdone = emit_kprep(kb, A, tn, consts, k_sb)
        kb.barrier()
        emit_w2prep(kb, A, tn, 0, pb, consts)
        kb.barrier()
        emit_w2prep(kb, A, tn, 1, pb, consts)
        kb.barrier()
        emit_layer0(kb, A, tn, pb, consts, k_sb, [kdone])
        emit_l1_front(kb, A, tn, pb, consts, fused=True)
        emit_kv_handshake(kb, A, tn, non_sb, fsb)
        emit_attention(kb, A, tn, pb, consts, fused=True)
        kb.wait("sync", ("sL", kb.cnt["sL"]))
        kb.finalize()
    return nc


_CACHE = {}
KERNEL_MODE = "fused"


def kernel(**inputs):
    if KERNEL_MODE == "fused":
        return kernel_fused(**inputs)
    return kernel_3launch(**inputs)


def kernel_fused(**inputs):
    com = _host_common(inputs)
    masks = [_mask_for(h) for h in (0, 1)]
    x = inputs["L0_attn_x"]
    if "ncf" not in _CACHE:
        _CACHE["ncf"] = build_fused()
    ncf = _CACHE["ncf"]
    idxs = [_token_idx(0), _token_idx(1)]
    cores = list(range(8))
    nonce = np.array([[np.random.randint(1, 2 ** 30)] * 16], dtype=np.int32)
    in_maps = []
    for c in cores:
        b, h = c // 2, c % 2
        m = {"xT": np.ascontiguousarray(x[b][idxs[h]].T),
             "mask": masks[h],
             "parity": np.array([[h]], dtype=np.int32),
             "nonce": nonce}
        for k in ("w_inT", "w_outT", "kbr", "kbi", "kcr", "kci", "normw",
                  "w1t0", "w3t0", "w2r0", "qkv", "gamma", "beta",
                  "w1t1", "w3t1", "w2r1", "o_proj",
                  "h128", "h64", "ident", "onescol", "onesrow"):
            m[k] = com[k]
        in_maps.append(m)
    res = run_bass_kernel_spmd(ncf, in_maps, core_ids=cores)
    out = np.empty((4, 4096, D), dtype=np.float32)
    for c in cores:
        b, h = c // 2, c % 2
        out[b][idxs[h]] = res.results[c]["outT"].T
    return out


def kernel_3launch(**inputs):
    com = _host_common(inputs)
    masks = [_mask_for(h) for h in (0, 1)]
    x = inputs["L0_attn_x"]

    if "ncs" not in _CACHE:
        _CACHE["ncs"] = (build_l0(), build_l1a(), build_l1b())
    nc0, nc1, nc2 = _CACHE["ncs"]

    idxs = [_token_idx(0), _token_idx(1)]
    cores = list(range(8))

    m1 = []
    for c in cores:
        b, h = c // 2, c % 2
        m = {"xT": np.ascontiguousarray(x[b][idxs[h]].T)}
        for k in ("w_inT", "w_outT", "kbr", "kbi", "kcr", "kci", "normw",
                  "w1t0", "w3t0", "w2r0", "h128", "h64", "onescol", "onesrow"):
            m[k] = com[k]
        m1.append(m)
    r1 = run_bass_kernel_spmd(nc0, m1, core_ids=cores)

    m2 = []
    for c in cores:
        m = {"x1f": r1.results[c]["x1f"]}
        for k in ("qkv", "gamma", "beta", "w1t1", "w3t1", "w2r1",
                  "h128", "h64", "onescol", "onesrow"):
            m[k] = com[k]
        m2.append(m)
    r2 = run_bass_kernel_spmd(nc1, m2, core_ids=cores)

    m3 = []
    for c in cores:
        b, h = c // 2, c % 2
        e, o = 2 * b, 2 * b + 1
        kgather = np.concatenate([r2.results[e]["kown"], r2.results[o]["kown"]], axis=0)
        vgather = np.concatenate([r2.results[e]["vown"], r2.results[o]["vown"]], axis=0)
        m = {"qd": r2.results[c]["qd"], "kg": kgather, "vg": vgather,
             "mask": masks[h], "o_proj": com["o_proj"],
             "xn1b": r2.results[c]["xn1b"], "g1f": r2.results[c]["g1f"],
             "x1f": r1.results[c]["x1f"], "ident": com["ident"],
             "onescol": com["onescol"], "onesrow": com["onesrow"]}
        m3.append(m)
    r3 = run_bass_kernel_spmd(nc2, m3, core_ids=cores)

    out = np.empty((4, 4096, D), dtype=np.float32)
    for c in cores:
        b, h = c // 2, c % 2
        out[b][idxs[h]] = r3.results[c]["outT"].T
    return out


# version 07
# speedup vs baseline: 1.2534x; 1.2534x over previous
import numpy as np
import ml_dtypes
from contextlib import ExitStack

import concourse.bass as bass
import concourse.mybir as mybir
from concourse.bass_utils import run_bass_kernel_spmd
from concourse.expressions_rust import RuntimeValue

F32 = mybir.dt.float32
BF16 = mybir.dt.bfloat16
I32 = mybir.dt.int32
AF = mybir.ActivationFunctionType
ALU = mybir.AluOpType
AX = mybir.AxisListType
ds = bass.ds
ts = bass.ts

D, NCD, HF, T = 2048, 4096, 8192, 2048
DC, JCN, HCN = 16, 32, 64
GT, NG, STW, NS = 1024, 2, 512, 2
NSLOT = 16
EPS_LN = 1e-5
EPS_RMS = float(np.finfo(np.float32).eps)
QSCALE = float(1.0 / np.sqrt(2048.0))
WSCALE = float(1.0 / np.sqrt(8192.0))
INV_D = float(1.0 / 2048.0)
NEGINF = -1e30
FFN_HARDEN = True

def gtile(h, s):
    if h == 0:
        return 2 * s + (1 if s % 2 else 0)
    return 2 * s + (0 if s % 2 else 1)



class Arena:
    def __init__(self, nc, ctx, elems=100 * 1024):
        self.t = ctx.enter_context(nc.sbuf_tensor("arena", [128, elems], BF16))
        self.off = 0
        self.elems = elems

    def _take(self, n_bf):
        off = (self.off + 31) & ~31
        self.off = off + n_bf
        assert self.off <= self.elems, f"SBUF arena overflow: {self.off} > {self.elems}"
        return off

    def bf(self, *shape):
        n = int(np.prod(shape[1:]))
        off = self._take(n)
        ap = self.t[0:shape[0], off:off + n]
        if len(shape) == 3:
            ap = ap.rearrange("p (a b) -> p a b", a=shape[1], b=shape[2])
        return ap

    def f32(self, *shape):
        n = int(np.prod(shape[1:]))
        off = self._take(2 * n)
        ap = self.t[0:shape[0], off:off + 2 * n].bitcast(F32)
        if len(shape) == 3:
            ap = ap.rearrange("p (a b) -> p a b", a=shape[1], b=shape[2])
        return ap

    def i32(self, *shape):
        n = int(np.prod(shape[1:]))
        off = self._take(2 * n)
        return self.t[0:shape[0], off:off + 2 * n].bitcast(I32)

    def mark(self):
        return self.off

    def reset(self, m):
        self.off = m


class KB:
    ENGINES = ("sync", "gpsimd", "scalar", "vector", "tensor")

    def __init__(self, nc, ctx):
        self.nc, self.ctx = nc, ctx
        self.th = {e: [] for e in self.ENGINES}
        self.semh, self.cnt = {}, {}
        for s in ("sL", "sG", "sP", "sV", "sA", "sB", "sO", "sF"):
            self.semh[s] = ctx.enter_context(nc.semaphore(s))
            self.cnt[s] = 0
        self.regs = {}

    def bump(self, s, amt):
        self.cnt[s] += amt
        return self.cnt[s]

    def emit(self, eng, fn):
        self.th[eng].append(fn)

    def now(self):
        return tuple((s, self.cnt[s]) for s in ("sP", "sV", "sA", "sL", "sG")
                     if self.cnt[s] > 0)

    def op(self, eng, build, wait=(), inc=None, amt=1):
        semh = self.semh
        tgt = None
        if inc:
            tgt = (semh[inc], amt)
            self.bump(inc, amt)
        wl = []
        for w in wait:
            if isinstance(w, tuple) and len(w) == 2 and isinstance(w[0], str):
                wl.append(w)
            else:
                wl.extend(w)
        wl = tuple(wl)

        def f(e):
            for sn, v in wl:
                e.wait_ge(semh[sn], v)
            ins = build(e)
            if tgt is not None:
                ins.then_inc(*tgt)
        self.emit(eng, f)
        return self.cnt[inc] if inc else None

    def wait(self, eng, *waits):
        semh = self.semh
        wl = tuple(waits)

        def f(e):
            for sn, v in wl:
                e.wait_ge(semh[sn], v)
        self.emit(eng, f)

    def barrier(self):
        semh = self.semh["sB"]
        self.bump("sB", 5)
        target = self.cnt["sB"]
        own_sem = {"sync": "sL", "gpsimd": "sG", "vector": "sV",
                   "scalar": "sA", "tensor": "sP"}
        for en in self.ENGINES:
            own = own_sem[en]
            own_v = self.cnt[own]
            ownh = self.semh[own]

            def f(e, own_v=own_v, ownh=ownh):
                if own_v > 0:
                    e.wait_ge(ownh, own_v)
                e.nop().then_inc(semh, 1)
                e.wait_ge(semh, target)
            self.emit(en, f)

    def finalize(self):
        with self.nc.Block() as block:
            for ename, meth in (("sync", block.sync), ("gpsimd", block.gpsimd),
                                ("scalar", block.scalar), ("vector", block.vector),
                                ("tensor", block.tensor)):
                ths = self.th[ename]

                def body(eng, ths=ths):
                    for f in ths:
                        f(eng)
                meth(body)



def emit_ln(kb, A, src_chunk_fn, xn_bf, pb, consts, g0, eps, n_loads_eng="sync"):
    onescol_bf, onesrow_f = consts["onescol_bf"], consts["onesrow_f"]
    m = A.mark()
    xf = [A.f32(128, STW), A.f32(128, STW)]
    xb = A.bf(128, STW)
    sq = A.bf(128, STW)
    rows = A.f32(1, 8 * STW).rearrange("p (a b) -> p a b", a=8, b=STW)
    xf_free = [kb.now(), kb.now()]
    xb_free = kb.now()
    sq_free = kb.now()

    for st in range(NS):
        statx = pb[0][0:1, :]
        statq = pb[1][0:1, :]
        for dc in range(DC):
            b = dc % 2
            cL = kb.op(n_loads_eng, lambda e, dc=dc, st=st, b=b: e.dma_start(
                out=xf[b], in_=src_chunk_fn(dc, st)),
                wait=[xf_free[b]], inc="sL", amt=16)
            cV = kb.op("vector", lambda e, b=b: e.tensor_copy(xb, xf[b]),
                       wait=[("sL", cL), xb_free], inc="sV")
            xf_free[b] = ("sV", cV)
            cA = kb.op("scalar", lambda e: e.activation(sq, xb, AF.Square),
                       wait=[("sV", cV), sq_free], inc="sA")
            xb_free = ("sA", cA)
            cP1 = kb.op("tensor", lambda e, dc=dc: e.matmul(
                statx, onescol_bf, xb, start=(dc == 0), stop=(dc == DC - 1)),
                wait=[("sV", cV)], inc="sP")
            xb_free = ("sP", cP1)
            cP2 = kb.op("tensor", lambda e, dc=dc: e.matmul(
                statq, onescol_bf, sq, start=(dc == 0), stop=(dc == DC - 1)),
                wait=[("sA", cA)], inc="sP")
            sq_free = ("sP", cP2)
        mean_r = rows[:, 0, :]
        msq_r = rows[:, 1, :]
        var_r = rows[:, 2, :]
        sd_r = rows[:, 3, :]
        rstd_r = rows[:, 4, :]
        mr_r = rows[:, 5, :]
        c1 = kb.op("vector", lambda e: e.tensor_scalar_mul(mean_r, statx, INV_D),
                   wait=[("sP", kb.cnt["sP"])], inc="sV")
        c2 = kb.op("vector", lambda e: e.tensor_scalar_mul(msq_r, statq, INV_D),
                   wait=[("sV", c1)], inc="sV")
        c3 = kb.op("vector", lambda e: e.tensor_tensor(var_r, mean_r, mean_r, ALU.mult),
                   wait=[("sV", c2)], inc="sV")
        c4 = kb.op("vector", lambda e: e.tensor_tensor(var_r, msq_r, var_r, ALU.subtract),
                   wait=[("sV", c3)], inc="sV")
        c5 = kb.op("scalar", lambda e: e.activation(sd_r, var_r, AF.Sqrt, bias=eps),
                   wait=[("sV", c4)], inc="sA")
        c6 = kb.op("vector", lambda e: e.reciprocal(rstd_r, sd_r),
                   wait=[("sA", c5)], inc="sV")
        c7 = kb.op("vector", lambda e: e.tensor_tensor(mr_r, mean_r, rstd_r, ALU.mult),
                   wait=[("sV", c6)], inc="sV")
        rstd_bc = pb[2][:, :]
        mr_bc = pb[3][:, :]
        cb1 = kb.op("tensor", lambda e: e.matmul(rstd_bc, onesrow_f, rstd_r,
                                                 start=True, stop=True),
                    wait=[("sV", c7)], inc="sP")
        cb2 = kb.op("tensor", lambda e: e.matmul(mr_bc, onesrow_f, mr_r,
                                                 start=True, stop=True),
                    wait=[("sV", c7)], inc="sP")
        tmp = A.f32(128, STW)
        tmp_free = kb.now()
        for dc in range(DC):
            b = dc % 2
            cL = kb.op(n_loads_eng, lambda e, dc=dc, st=st, b=b: e.dma_start(
                out=xf[b], in_=src_chunk_fn(dc, st)),
                wait=[xf_free[b]], inc="sL", amt=16)
            cV = kb.op("vector", lambda e, b=b: e.tensor_tensor(tmp, xf[b], rstd_bc, ALU.mult),
                       wait=[("sL", cL), ("sP", cb1), tmp_free], inc="sV")
            xf_free[b] = ("sV", cV)
            cV2 = kb.op("vector", lambda e, dc=dc, st=st: e.tensor_tensor(
                xn_bf[dc][:, st * STW:(st + 1) * STW], tmp, mr_bc, ALU.subtract),
                wait=[("sV", cV), ("sP", cb2)], inc="sV")
            tmp_free = ("sV", cV2)
        kb.wait("tensor", ("sV", kb.cnt["sV"]))
    A.reset(m)


def emit_w2prep(kb, A, tn, layer, pb, consts):
    h64_bf = consts["h64_bf"]
    w2r = tn[f"w2r{layer}"]
    w2p = tn[f"w2p{layer}"]
    m = A.mark()
    NSTR = 8
    wbuf = A.bf(64, NSTR * STW)
    obuf = A.bf(64, NSTR * STW)
    wfree, ofree = kb.now(), kb.now()
    total = NCD * 64
    for blk in range(total // (NSTR * STW)):
        x0 = blk * NSTR * STW
        cG = kb.op("gpsimd", lambda e, x0=x0: e.dma_start(
            out=wbuf, in_=w2r[:, x0:x0 + NSTR * STW]),
            wait=[wfree], inc="sG", amt=16)
        lastA = None
        for k in range(NSTR):
            ps = pb[k % 2][0:64, :]
            cP = kb.op("tensor", lambda e, k=k, ps=ps: e.matmul(
                ps, h64_bf, wbuf[:, k * STW:(k + 1) * STW], start=True, stop=True),
                wait=[("sG", cG), ("sA", kb.cnt["sA"])],
                inc="sP")
            cA = kb.op("scalar", lambda e, k=k, ps=ps: e.activation(
                obuf[:, k * STW:(k + 1) * STW], ps, AF.Copy, scale=WSCALE),
                wait=[("sP", cP), ofree], inc="sA")
            lastA = cA
        wfree = ("sP", kb.cnt["sP"])
        b0 = x0 // 2048
        cS = kb.op("sync", lambda e, b0=b0: e.dma_start(
            out=w2p[:, b0:b0 + 2, :], in_=obuf.rearrange("p (b d) -> p b d", b=2, d=2048)),
            wait=[("sA", lastA)], inc="sL", amt=16)
        ofree = ("sL", cS)
    A.reset(m)


def emit_kprep(kb, A, tn, consts, k_sb):
    m = A.mark()
    tiles = {}
    for nm in ("kbr", "kbi", "kcr", "kci"):
        t = A.f32(128, JCN, 16)
        c = kb.op("sync", lambda e, nm=nm, t=t: e.dma_start(
            out=t, in_=tn[nm].rearrange("(c p) s -> p c s", p=128)),
            inc="sL", amt=16)
        tiles[nm] = (t, c)
    t1 = A.f32(128, JCN, 16)
    t2 = A.f32(128, JCN, 16)
    cw = kb.cnt["sL"]
    c1 = kb.op("vector", lambda e: e.tensor_mul(t1, tiles["kcr"][0], tiles["kbr"][0]),
               wait=[("sL", cw)], inc="sV")
    c2 = kb.op("vector", lambda e: e.tensor_mul(t2, tiles["kci"][0], tiles["kbi"][0]),
               wait=[("sV", c1)], inc="sV")
    c3 = kb.op("vector", lambda e: e.tensor_tensor(t1, t1, t2, ALU.subtract),
               wait=[("sV", c2)], inc="sV")
    c4 = kb.op("vector", lambda e: e.reduce_sum(k_sb.unsqueeze(2), t1, axis=AX.X),
               wait=[("sV", c3)], inc="sV")
    A.reset(m)
    return c4


def emit_proj(kb, A, tn, wname, wcols, kchunks, xn_bf, out_cb, pb, bank0,
              extra_wait=(), col_off=0):
    w = tn[wname]
    nfc = wcols // 128
    m = A.mark()
    wsb = [A.bf(128, kchunks, 128), A.bf(128, kchunks, 128)]
    wfree = [kb.now(), kb.now()]
    bank_free = [kb.now(), kb.now()]
    for fc in range(nfc):
        b = fc % 2
        cG = kb.op("gpsimd", lambda e, fc=fc, b=b: e.dma_start(
            out=wsb[b], in_=w.rearrange("(k p) f -> p k f", p=128)[
                :, :, col_off + fc * 128: col_off + (fc + 1) * 128]),
            wait=[wfree[b]] + list(extra_wait), inc="sG", amt=16)
        for st in range(NS):
            ps = pb[bank0 + (st % 2)][:, :]
            cP = None
            for dk in range(kchunks):
                cP = kb.op("tensor", lambda e, dk=dk, st=st, b=b, ps=ps: e.matmul(
                    ps, wsb[b][:, dk, :], xn_bf[dk][:, st * STW:(st + 1) * STW],
                    start=(dk == 0), stop=(dk == kchunks - 1)),
                    wait=([("sG", cG), bank_free[st % 2]] if dk == 0 else []),
                    inc="sP")
            bank_free[st % 2] = out_cb(fc, st, ps, cP)
        wfree[b] = ("sP", kb.cnt["sP"])
    A.reset(m)


def emit_layer0(kb, A, tn, pb, consts, k_sb, kprep_done):
    xT = tn["xT"]
    x1 = tn["x1"]
    s_sp = tn["s_sp"]
    normw_sb = consts["normw_sb"]
    onescol_bf, onesrow_f = consts["onescol_bf"], consts["onesrow_f"]
    h128_bf = consts["h128_bf"]

    for g in range(NG):
        t0 = g * GT
        mG = A.mark()
        xn_bf = [A.bf(128, GT) for _ in range(DC)]

        emit_ln(kb, A, lambda dc, st, t0=t0: xT[dc * 128:(dc + 1) * 128,
                                                t0 + st * STW: t0 + (st + 1) * STW],
                xn_bf, pb, consts, g, consts["eps_ln"])
        kb.barrier()

        mM = A.mark()
        y_bf = [A.bf(128, GT) for _ in range(JCN)]

        def y_evac(jc, st, ps, cP):
            cV = kb.op("vector", lambda e, jc=jc, st=st, ps=ps: e.tensor_scalar(
                y_bf[jc][:, st * STW:(st + 1) * STW], ps,
                k_sb[:, jc:jc + 1], None, ALU.mult),
                wait=[("sP", cP), ("sV", kprep_done[0])], inc="sV")
            return ("sV", cV)
        emit_proj(kb, A, tn, "w_inT", NCD, DC, xn_bf, y_evac, pb, 0)
        kb.wait("tensor", ("sV", kb.cnt["sV"]))

        out0_bf = [A.bf(128, GT) for _ in range(DC)]
        sq_t = A.bf(128, STW)
        sq_free = [kb.now()]
        rms_ps = [pb[4][0:1, :], pb[5][0:1, :]]

        def out0_evac(dc, st, ps, cP):
            cV = kb.op("vector", lambda e, dc=dc, st=st, ps=ps: e.tensor_copy(
                out0_bf[dc][:, st * STW:(st + 1) * STW], ps),
                wait=[("sP", cP)], inc="sV")
            cA = kb.op("scalar", lambda e, ps=ps: e.activation(sq_t, ps, AF.Square),
                       wait=[("sV", cV), sq_free[0]], inc="sA")
            cP2 = kb.op("tensor", lambda e, dc=dc, st=st: e.matmul(
                rms_ps[st], onescol_bf, sq_t, start=(dc == 0), stop=(dc == DC - 1)),
                wait=[("sA", cA)], inc="sP")
            sq_free[0] = ("sP", cP2)
            return ("sA", cA)
        emit_proj(kb, A, tn, "w_outT", D, JCN, y_bf, out0_evac, pb, 0)
        sV_out0 = kb.cnt["sV"]

        mrow = A.mark()
        rows = A.f32(1, 4 * STW).rearrange("p (a b) -> p a b", a=4, b=STW)
        rstd2_bc = [pb[6][:, :], pb[7][:, :]]
        for st in range(NS):
            ms_r, sd_r, rstd_r = rows[:, 0, :], rows[:, 1, :], rows[:, 2, :]
            c1 = kb.op("vector", lambda e, st=st: e.tensor_scalar_mul(ms_r, rms_ps[st], INV_D),
                       wait=[("sP", kb.cnt["sP"])], inc="sV")
            c2 = kb.op("scalar", lambda e: e.activation(sd_r, ms_r, AF.Sqrt, bias=consts["eps_rms"]),
                       wait=[("sV", c1)], inc="sA")
            c3 = kb.op("vector", lambda e: e.reciprocal(rstd_r, sd_r),
                       wait=[("sA", c2)], inc="sV")
            cb = kb.op("tensor", lambda e, st=st: e.matmul(
                rstd2_bc[st], onesrow_f, rstd_r, start=True, stop=True),
                wait=[("sV", c3)], inc="sP")
        cb_done = kb.cnt["sP"]
        tmpa = A.f32(128, STW)
        tmpa_free = kb.now()
        for dc in range(DC):
            for st in range(NS):
                cV = kb.op("vector", lambda e, dc=dc, st=st: e.tensor_scalar(
                    tmpa, out0_bf[dc][:, st * STW:(st + 1) * STW],
                    normw_sb[:, dc:dc + 1], None, ALU.mult),
                    wait=[tmpa_free, ("sP", cb_done), ("sV", sV_out0)], inc="sV")
                cV = kb.op("vector", lambda e, st=st: e.tensor_tensor(
                    tmpa, tmpa, rstd2_bc[st], ALU.mult),
                    wait=[("sV", cV)], inc="sV")
                cV = kb.op("vector", lambda e, dc=dc, st=st: e.tensor_tensor(
                    tmpa, tmpa, xn_bf[dc][:, st * STW:(st + 1) * STW], ALU.add),
                    wait=[("sV", cV)], inc="sV")
                cS = kb.op("sync", lambda e, dc=dc, st=st, t0=t0: e.dma_start(
                    out=s_sp[dc, :, t0 + st * STW: t0 + (st + 1) * STW], in_=tmpa),
                    wait=[("sV", cV)], inc="sL", amt=16)
                tmpa_free = ("sL", cS)
        if "y_dbg" in tn and g == 0:
            for jc in range(JCN):
                kb.op("sync", lambda e, jc=jc: e.dma_start(
                    out=tn["y_dbg"][jc, :, 0:GT], in_=y_bf[jc]),
                    wait=[("sV", kb.cnt["sV"])], inc="sL", amt=16)
            for dc in range(DC):
                kb.op("sync", lambda e, dc=dc: e.dma_start(
                    out=tn["o0_dbg"][dc, :, 0:GT], in_=out0_bf[dc]),
                    inc="sL", amt=16)
            for dc in range(DC):
                kb.op("sync", lambda e, dc=dc: e.dma_start(
                    out=tn["xn_dbg"][dc, :, 0:GT], in_=xn_bf[dc]),
                    inc="sL", amt=16)
            kb.wait("sync", ("sL", kb.cnt["sL"]))
        A.reset(mrow)
        A.reset(mM)
        kb.barrier()

        emit_ffn(kb, A, tn, pb, consts, xn_bf, layer=0, g=g)
        A.reset(mG)
        kb.barrier()


def emit_ffn(kb, A, tn, pb, consts, xn_bf, layer, g):
    h128_bf = consts["h128_bf"]
    t0 = g * GT
    w1 = tn[f"w1t{layer}"]
    w3 = tn[f"w3t{layer}"]
    w2p = tn[f"w2p{layer}"]
    mF = A.mark()
    y1 = [A.bf(128, GT) for _ in range(HCN)]
    w1sb = [A.bf(128, DC, 128), A.bf(128, DC, 128)]
    w3sb = [A.bf(128, DC, 128), A.bf(128, DC, 128)]
    silu_t = A.bf(128, STW)
    hid_t = A.bf(128, STW)
    w1free = [kb.now(), kb.now()]
    w3free = [kb.now(), kb.now()]
    silu_free, hid_free = kb.now(), kb.now()
    bga_free = kb.now()
    bgl_free = kb.now()
    by1_free = kb.now()

    for hc in range(HCN):
        b = hc % 2
        cG1 = kb.op("gpsimd", lambda e, hc=hc, b=b: e.dma_start(
            out=w1sb[b], in_=w1.rearrange("(k p) f -> p k f", p=128)[:, :, hc * 128:(hc + 1) * 128]),
            wait=[w1free[b]] + ([kb.now()] if FFN_HARDEN else []), inc="sG", amt=16)
        cG3 = kb.op("gpsimd", lambda e, hc=hc, b=b: e.dma_start(
            out=w3sb[b], in_=w3.rearrange("(k p) f -> p k f", p=128)[:, :, hc * 128:(hc + 1) * 128]),
            wait=[w3free[b]] + ([kb.now()] if FFN_HARDEN else []), inc="sG", amt=16)
        for st in range(NS):
            psg = pb[0 + st][:, :]
            psl = pb[2 + st][:, :]
            psy = pb[4 + st][:, :]
            cPg = None
            for dk in range(DC):
                cPg = kb.op("tensor", lambda e, dk=dk, st=st, b=b: e.matmul(
                    psg, w1sb[b][:, dk, :], xn_bf[dk][:, st * STW:(st + 1) * STW],
                    start=(dk == 0), stop=(dk == DC - 1)),
                    wait=([("sG", cG1), bga_free] + ([kb.now()] if FFN_HARDEN else []) if dk == 0 else []), inc="sP")
            cPl = None
            for dk in range(DC):
                cPl = kb.op("tensor", lambda e, dk=dk, st=st, b=b: e.matmul(
                    psl, w3sb[b][:, dk, :], xn_bf[dk][:, st * STW:(st + 1) * STW],
                    start=(dk == 0), stop=(dk == DC - 1)),
                    wait=([("sG", cG3), bgl_free] + ([kb.now()] if FFN_HARDEN else []) if dk == 0 else []), inc="sP")
            cA = kb.op("scalar", lambda e: e.activation(silu_t, psg, AF.Silu),
                       wait=[("sP", cPg), silu_free] + ([kb.now()] if FFN_HARDEN else []), inc="sA")
            bga_free = ("sA", cA)
            cV = kb.op("vector", lambda e: e.tensor_tensor(hid_t, silu_t, psl, ALU.mult),
                       wait=[("sA", cA), ("sP", cPl), hid_free] + ([kb.now()] if FFN_HARDEN else []), inc="sV")
            silu_free = ("sV", cV)
            bgl_free = ("sV", cV)
            cPy = kb.op("tensor", lambda e: e.matmul(psy, h128_bf, hid_t,
                                                     start=True, stop=True),
                        wait=[("sV", cV), by1_free] + ([kb.now()] if FFN_HARDEN else []), inc="sP")
            hid_free = ("sP", cPy)
            cA2 = kb.op("scalar", lambda e, hc=hc, st=st: e.activation(
                y1[hc][:, st * STW:(st + 1) * STW], psy, AF.Copy),
                wait=[("sP", cPy)] + ([kb.now()] if FFN_HARDEN else []), inc="sA")
            by1_free = ("sA", cA2)
        w1free[b] = ("sP", kb.cnt["sP"])
        w3free[b] = ("sP", kb.cnt["sP"])

    mC = A.mark()
    w2sb = [A.bf(128, 8, 256), A.bf(128, 8, 256)]
    w2free = [kb.now(), kb.now()]
    gx_t = A.f32(128, STW)
    sx_t = A.f32(128, STW)
    xx_t = A.f32(128, STW)
    o_t = A.f32(128, STW)
    ev_free = kb.now()
    for ps_i in range(8):
        if FFN_HARDEN:
            kb.barrier()
        dc0 = ps_i * 2
        banks = [pb[4 + i][:, :] for i in range(4)]
        bank_wait = ("sA", kb.cnt["sA"])
        cP = None
        for hb in range(HCN // 8):
            b = hb % 2
            cG = kb.op("gpsimd", lambda e, hb=hb, dc0=dc0, b=b: e.dma_start(
                out=w2sb[b], in_=w2p[hb * 8:(hb + 1) * 8, :,
                                     dc0 * 128:(dc0 + 2) * 128].rearrange(
                    "h p t -> p h t")),
                wait=[w2free[b]] + ([kb.now()] if FFN_HARDEN else []), inc="sG", amt=16)
            for hcl in range(8):
                hc = hb * 8 + hcl
                for dcl in range(2):
                    for st in range(NS):
                        ps = banks[dcl * 2 + st]
                        cP = kb.op("tensor", lambda e, hcl=hcl, hc=hc, dcl=dcl, st=st, b=b, ps=ps: e.matmul(
                            ps, w2sb[b][:, hcl, dcl * 128:(dcl + 1) * 128],
                            y1[hc][:, st * STW:(st + 1) * STW],
                            start=(hc == 0), stop=(hc == HCN - 1)),
                            wait=([("sG", cG)] + ([bank_wait] if hc == 0 else [])
                                  if hcl == 0 and dcl == 0 and st == 0 else []),
                            inc="sP")
            w2free[b] = ("sP", kb.cnt["sP"])
        for dcl in range(2):
            dc = dc0 + dcl
            for st in range(NS):
                ps = banks[dcl * 2 + st]
                cA = kb.op("scalar", lambda e, ps=ps: e.activation(gx_t, ps, AF.Silu),
                           wait=[("sP", cP), ev_free], inc="sA")
                if layer == 0:
                    cL1 = kb.op("sync", lambda e, dc=dc, st=st: e.dma_start(
                        out=sx_t, in_=tn["s_sp"][dc, :, t0 + st * STW: t0 + (st + 1) * STW]),
                        wait=[ev_free], inc="sL", amt=16)
                    cL2 = kb.op("sync", lambda e, dc=dc, st=st: e.dma_start(
                        out=xx_t, in_=tn["xT"][dc * 128:(dc + 1) * 128,
                                               t0 + st * STW: t0 + (st + 1) * STW]),
                        inc="sL", amt=16)
                    cV = kb.op("vector", lambda e: e.tensor_tensor(o_t, sx_t, gx_t, ALU.mult),
                               wait=[("sA", cA), ("sL", cL2), ev_free], inc="sV")
                    cV = kb.op("vector", lambda e: e.tensor_tensor(o_t, o_t, xx_t, ALU.add),
                               wait=[("sV", cV)], inc="sV")
                    cS = kb.op("sync", lambda e, dc=dc, st=st: e.dma_start(
                        out=tn["x1"][dc, :, t0 + st * STW: t0 + (st + 1) * STW], in_=o_t),
                        wait=[("sV", cV)], inc="sL", amt=16)
                    ev_free = ("sL", cS)
                else:
                    cS = kb.op("sync", lambda e, dc=dc, st=st: e.dma_start(
                        out=tn["g1"][dc, :, t0 + st * STW: t0 + (st + 1) * STW], in_=gx_t),
                        wait=[("sA", cA)], inc="sL", amt=16)
                    ev_free = ("sL", cS)
    A.reset(mC)
    A.reset(mF)


def emit_ln_inner(kb, A, src_bf, out_bf, pb, consts, eps):
    onescol_bf, onesrow_f = consts["onescol_bf"], consts["onesrow_f"]
    gamma_sb, beta_sb = consts["gamma_sb"], consts["beta_sb"]
    m = A.mark()
    sq = A.bf(128, STW)
    rows = A.f32(1, 8 * STW).rearrange("p (a b) -> p a b", a=8, b=STW)
    tmp = A.f32(128, STW)
    sq_free = kb.now()
    tmp_free = kb.now()
    for st in range(NS):
        statx = pb[0][0:1, :]
        statq = pb[1][0:1, :]
        for dc in range(DC):
            cA = kb.op("scalar", lambda e, dc=dc, st=st: e.activation(
                sq, src_bf[dc][:, st * STW:(st + 1) * STW], AF.Square),
                wait=[sq_free], inc="sA")
            cP1 = kb.op("tensor", lambda e, dc=dc, st=st: e.matmul(
                statx, onescol_bf, src_bf[dc][:, st * STW:(st + 1) * STW],
                start=(dc == 0), stop=(dc == DC - 1)), inc="sP")
            cP2 = kb.op("tensor", lambda e, dc=dc: e.matmul(
                statq, onescol_bf, sq, start=(dc == 0), stop=(dc == DC - 1)),
                wait=[("sA", cA)], inc="sP")
            sq_free = ("sP", cP2)
        mean_r, msq_r, var_r = rows[:, 0, :], rows[:, 1, :], rows[:, 2, :]
        sd_r, rstd_r, mr_r = rows[:, 3, :], rows[:, 4, :], rows[:, 5, :]
        c1 = kb.op("vector", lambda e: e.tensor_scalar_mul(mean_r, statx, INV_D),
                   wait=[("sP", kb.cnt["sP"])], inc="sV")
        c2 = kb.op("vector", lambda e: e.tensor_scalar_mul(msq_r, statq, INV_D),
                   wait=[("sV", c1)], inc="sV")
        c3 = kb.op("vector", lambda e: e.tensor_tensor(var_r, mean_r, mean_r, ALU.mult),
                   wait=[("sV", c2)], inc="sV")
        c4 = kb.op("vector", lambda e: e.tensor_tensor(var_r, msq_r, var_r, ALU.subtract),
                   wait=[("sV", c3)], inc="sV")
        c5 = kb.op("scalar", lambda e: e.activation(sd_r, var_r, AF.Sqrt, bias=eps),
                   wait=[("sV", c4)], inc="sA")
        c6 = kb.op("vector", lambda e: e.reciprocal(rstd_r, sd_r),
                   wait=[("sA", c5)], inc="sV")
        c7 = kb.op("vector", lambda e: e.tensor_tensor(mr_r, mean_r, rstd_r, ALU.mult),
                   wait=[("sV", c6)], inc="sV")
        rstd_bc = pb[2][:, :]
        mr_bc = pb[3][:, :]
        cb1 = kb.op("tensor", lambda e: e.matmul(rstd_bc, onesrow_f, rstd_r,
                                                 start=True, stop=True),
                    wait=[("sV", c7)], inc="sP")
        cb2 = kb.op("tensor", lambda e: e.matmul(mr_bc, onesrow_f, mr_r,
                                                 start=True, stop=True),
                    wait=[("sV", c7)], inc="sP")
        for dc in range(DC):
            cV = kb.op("vector", lambda e, dc=dc, st=st: e.tensor_tensor(
                tmp, src_bf[dc][:, st * STW:(st + 1) * STW], rstd_bc, ALU.mult),
                wait=[("sP", cb1), tmp_free], inc="sV")
            cV = kb.op("vector", lambda e: e.tensor_tensor(tmp, tmp, mr_bc, ALU.subtract),
                       wait=[("sV", cV), ("sP", cb2)], inc="sV")
            cV = kb.op("vector", lambda e, dc=dc, st=st: e.tensor_scalar(
                out_bf[dc][:, st * STW:(st + 1) * STW], tmp,
                gamma_sb[:, dc:dc + 1], beta_sb[:, dc:dc + 1], ALU.mult, ALU.add),
                wait=[("sV", cV)], inc="sV")
            tmp_free = ("sV", cV)
        kb.wait("tensor", ("sV", kb.cnt["sV"]))
    A.reset(m)


def emit_l1_front(kb, A, tn, pb, consts, fused):
    x1 = tn["x1"]
    for g in range(NG):
        t0 = g * GT
        mG = A.mark()
        xn1_bf = [A.bf(128, GT) for _ in range(DC)]
        emit_ln(kb, A, lambda dc, st, t0=t0: x1[dc, :, t0 + st * STW: t0 + (st + 1) * STW],
                xn1_bf, pb, consts, g, consts["eps_ln"])
        for dc in range(DC):
            kb.op("sync", lambda e, dc=dc, t0=t0: e.dma_start(
                out=tn["xn1b"][dc, :, t0:t0 + GT], in_=xn1_bf[dc]),
                wait=[("sV", kb.cnt["sV"])], inc="sL", amt=16)
        kb.barrier()

        mX = A.mark()
        xnorm_bf = [A.bf(128, GT) for _ in range(DC)]
        emit_ln_inner(kb, A, xn1_bf, xnorm_bf, pb, consts, consts["eps_ln"])
        if "xnorm_dbg" in tn:
            for dc in range(DC):
                kb.op("sync", lambda e, dc=dc, t0=t0: e.dma_start(
                    out=tn["xnorm_dbg"][dc, :, t0:t0 + GT], in_=xnorm_bf[dc]),
                    wait=[("sV", kb.cnt["sV"])], inc="sL", amt=16)
        kb.barrier()

        qd = tn["qd"]
        mQ = A.mark()
        q_tile = A.bf(128, STW)
        qk_state = {"free": kb.now()}

        def q_evac(fc, st, ps, cP):
            cV = kb.op("vector", lambda e, ps=ps: e.tensor_scalar_mul(
                q_tile, ps, QSCALE), wait=[("sP", cP), qk_state["free"]], inc="sV")
            slot0 = g * 8 + st * 4
            dst = qd[slot0:slot0 + 4, :, fc, :].rearrange("j p t -> p j t")
            cS = kb.op("sync", lambda e, dst=dst: e.dma_start(
                out=dst, in_=q_tile.rearrange("p (j t) -> p j t", j=4, t=128)),
                wait=[("sV", cV)], inc="sL", amt=16)
            qk_state["free"] = ("sL", cS)
            return ("sV", cV)
        emit_proj(kb, A, tn, "qkv", D, DC, xnorm_bf, q_evac, pb, 0, col_off=0)

        def k_evac(fc, st, ps, cP):
            cV = kb.op("vector", lambda e, ps=ps: e.tensor_copy(q_tile, ps),
                       wait=[("sP", cP), qk_state["free"]], inc="sV")
            dst = tn["kown"][fc, :, t0 + st * STW: t0 + (st + 1) * STW]
            cS = kb.op("sync", lambda e, dst=dst: e.dma_start(
                out=dst, in_=q_tile), wait=[("sV", cV)], inc="sL", amt=16)
            qk_state["free"] = ("sL", cS)
            return ("sV", cV)
        emit_proj(kb, A, tn, "qkv", D, DC, xnorm_bf, k_evac, pb, 0, col_off=D)
        A.reset(mQ)
        kb.wait("vector", ("sL", kb.cnt["sL"]))

        mV = A.mark()
        vw = [A.bf(128, DC, STW), A.bf(128, DC, STW)]
        v_t = A.bf(128, STW)
        vw_free = [kb.now(), kb.now()]
        vt_free = kb.now()
        qkvr = tn["qkv"].rearrange("(k p) f -> p k f", p=128)
        for fst in range(4):
            b = fst % 2
            cG = kb.op("gpsimd", lambda e, fst=fst, b=b: e.dma_start(
                out=vw[b], in_=qkvr[:, :, 2 * D + fst * STW: 2 * D + (fst + 1) * STW]),
                wait=[vw_free[b]], inc="sG", amt=16)
            for tc in range(8):
                ps = pb[fst % 2][:, :]
                cP = None
                for dk in range(DC):
                    cP = kb.op("tensor", lambda e, dk=dk, tc=tc, b=b, ps=ps: e.matmul(
                        ps, xnorm_bf[dk][:, tc * 128:(tc + 1) * 128],
                        vw[b][:, dk, :], start=(dk == 0), stop=(dk == DC - 1)),
                        wait=([("sG", cG), ("sV", kb.cnt["sV"])]
                              if dk == 0 else []), inc="sP")
                cV = kb.op("vector", lambda e, ps=ps: e.tensor_copy(v_t, ps),
                           wait=[("sP", cP), vt_free], inc="sV")
                slot = g * 8 + tc
                dst = tn["vown"][slot, :, fst * STW:(fst + 1) * STW]
                cS = kb.op("sync", lambda e, dst=dst: e.dma_start(out=dst, in_=v_t),
                           wait=[("sV", cV)], inc="sL", amt=16)
                vt_free = ("sL", cS)
            vw_free[b] = ("sP", kb.cnt["sP"])
        A.reset(mV)
        A.reset(mX)
        kb.barrier()

        emit_ffn(kb, A, tn, pb, consts, xn1_bf, layer=1, g=g)
        A.reset(mG)
        kb.barrier()


def emit_attention(kb, A, tn, pb, consts, fused):
    ident_bf = consts["ident_bf"]
    kg = tn["kg"]
    vg = tn["vg"]
    mask = tn["mask"]
    mAt = A.mark()
    ao_fm = [A.bf(128, T) for _ in range(DC)]
    pt_bf = pb[2].bitcast(BF16)[:, 0:128]
    av = [pb[4 + i][:, :] for i in range(4)]

    for s in range(NSLOT):
        kb.barrier()
        W = 2 * (s + 1) * 128
        off_s = 128 * s * (s + 1)
        mS = A.mark()
        q_sb = A.bf(128, DC, 128)
        mask_sb = A.bf(128, W)
        S_f = A.f32(128, W)
        P_bf = A.bf(128, W)
        k_sb = [A.bf(128, DC, 256), A.bf(128, DC, 256)]
        v_sb = [A.bf(128, D), A.bf(128, D)]
        nm = A.f32(128, 1)
        l_f = A.f32(128, 1)
        linv = A.f32(128, 1)
        ao_tm = A.bf(128, D)
        pt_sb = A.bf(128, 128)

        cq = kb.op("sync", lambda e, s=s, q_sb=q_sb: e.dma_start(
            out=q_sb, in_=tn["qd"][s]), wait=[("sV", kb.cnt["sV"])], inc="sL", amt=16)
        cm = kb.op("sync", lambda e, off_s=off_s, W=W, mask_sb=mask_sb: e.dma_start(
            out=mask_sb, in_=mask[:, off_s:off_s + W]), inc="sL", amt=16)

        kb_free = [kb.now(), kb.now()]
        sbank_free = [kb.now(), kb.now()]
        si = 0
        for r in range(2):
            ncols = (s + 1) * 128
            c0 = 0
            while c0 < ncols:
                w = min(256, ncols - c0)
                b = si % 2
                cK = kb.op("sync", lambda e, r=r, c0=c0, w=w, b=b, k_sb=k_sb: e.dma_start(
                    out=k_sb[b][:, :, 0:w],
                    in_=kg[r * 16:(r + 1) * 16, :, c0:c0 + w].rearrange("f p t -> p f t")),
                    wait=[kb_free[b]], inc="sL", amt=16)
                ps = pb[si % 2][:, 0:w]
                cP = None
                for fc in range(DC):
                    cP = kb.op("tensor", lambda e, fc=fc, b=b, w=w, ps=ps, q_sb=q_sb, k_sb=k_sb: e.matmul(
                        ps, q_sb[:, fc, :], k_sb[b][:, fc, 0:w],
                        start=(fc == 0), stop=(fc == DC - 1)),
                        wait=([("sL", cK), sbank_free[si % 2]] if fc == 0 else []),
                        inc="sP")
                col0 = r * (s + 1) * 128 + c0
                cV = kb.op("vector", lambda e, col0=col0, w=w, ps=ps, S_f=S_f, mask_sb=mask_sb: e.tensor_tensor(
                    S_f[:, col0:col0 + w], ps, mask_sb[:, col0:col0 + w], ALU.add),
                    wait=[("sP", cP), ("sL", cm)], inc="sV")
                kb_free[b] = ("sP", cP)
                sbank_free[si % 2] = ("sV", cV)
                c0 += w
                si += 1

        c1 = kb.op("vector", lambda e, S_f=S_f, nm=nm: e.reduce_max(
            nm, S_f, axis=AX.X, negate=True),
            wait=[("sV", kb.cnt["sV"])], inc="sV")
        c2 = kb.op("scalar", lambda e, P_bf=P_bf, S_f=S_f, nm=nm, l_f=l_f: e.activation(
            P_bf, S_f, AF.Exp, bias=nm, accum_out=l_f),
            wait=[("sV", c1)], inc="sA")
        c3 = kb.op("vector", lambda e, linv=linv, l_f=l_f: e.reciprocal(linv, l_f),
                   wait=[("sA", c2)], inc="sV")

        pt_free = kb.now()
        ptsb_free = kb.now()
        v_free = [kb.now(), kb.now()]
        njs = 2 * (s + 1)
        cPav = None
        for j in range(njs):
            r = 0 if j <= s else 1
            l = j if j <= s else j - (s + 1)
            b = j % 2
            cT = kb.op("tensor", lambda e, j=j, P_bf=P_bf: e.transpose(
                pt_bf, P_bf[:, j * 128:(j + 1) * 128], ident_bf),
                wait=[("sA", c2), pt_free], inc="sP")
            cVt = kb.op("vector", lambda e: e.tensor_copy(pt_sb, pt_bf),
                        wait=[("sP", cT), ptsb_free], inc="sV")
            pt_free = ("sV", cVt)
            cL = kb.op("sync", lambda e, r=r, l=l, b=b, v_sb=v_sb: e.dma_start(
                out=v_sb[b], in_=vg[r * 16 + l]),
                wait=[v_free[b]], inc="sL", amt=16)
            for fst in range(4):
                cPav = kb.op("tensor", lambda e, fst=fst, j=j, b=b, v_sb=v_sb: e.matmul(
                    av[fst], pt_sb, v_sb[b][:, fst * STW:(fst + 1) * STW],
                    start=(j == 0), stop=(j == njs - 1)),
                    wait=([("sV", cVt), ("sL", cL)] +
                          ([("sV", kb.cnt["sV"])] if j == 0 else [])) if fst == 0
                    else [],
                    inc="sP")
            ptsb_free = ("sP", cPav)
            v_free[b] = ("sP", cPav)

        for fst in range(4):
            cV = kb.op("vector", lambda e, fst=fst, ao_tm=ao_tm, linv=linv: e.tensor_scalar(
                ao_tm[:, fst * STW:(fst + 1) * STW], av[fst], linv, None, ALU.mult),
                wait=[("sP", cPav), ("sV", c3)], inc="sV")
        sv_ao = kb.cnt["sV"]
        for dc in range(DC):
            cT = kb.op("tensor", lambda e, dc=dc, ao_tm=ao_tm: e.transpose(
                pt_bf, ao_tm[:, dc * 128:(dc + 1) * 128], ident_bf),
                wait=[("sV", sv_ao), pt_free], inc="sP")
            cVt = kb.op("vector", lambda e, dc=dc, s=s: e.tensor_copy(
                ao_fm[dc][:, s * 128:(s + 1) * 128], pt_bf),
                wait=[("sP", cT)], inc="sV")
            pt_free = ("sV", cVt)
        kb.wait("tensor", ("sV", kb.cnt["sV"]))
        A.reset(mS)

    kb.barrier()
    mO = A.mark()
    wsb = [A.bf(128, DC, 128), A.bf(128, DC, 128)]
    xn1_t = A.bf(128, STW)
    s1_t = A.f32(128, STW)
    g1_t = A.f32(128, STW)
    x1_t = A.f32(128, STW)
    o_t = A.f32(128, STW)
    w_free = [kb.now(), kb.now()]
    ev_free = kb.now()
    bank_free = [kb.now(), kb.now()]
    opr = tn["o_proj"].rearrange("(k p) f -> p k f", p=128)
    for fc in range(DC):
        b = fc % 2
        cG = kb.op("gpsimd", lambda e, fc=fc, b=b: e.dma_start(
            out=wsb[b], in_=opr[:, :, fc * 128:(fc + 1) * 128]),
            wait=[w_free[b]], inc="sG", amt=16)
        for st4 in range(4):
            ps = pb[st4 % 2][:, :]
            cP = None
            for dk in range(DC):
                cP = kb.op("tensor", lambda e, dk=dk, st4=st4, b=b, ps=ps: e.matmul(
                    ps, wsb[b][:, dk, :], ao_fm[dk][:, st4 * STW:(st4 + 1) * STW],
                    start=(dk == 0), stop=(dk == DC - 1)),
                    wait=([("sG", cG), bank_free[st4 % 2]] if dk == 0 else []),
                    inc="sP")
            cL1 = kb.op("sync", lambda e, fc=fc, st4=st4: e.dma_start(
                out=xn1_t, in_=tn["xn1b"][fc, :, st4 * STW:(st4 + 1) * STW]),
                wait=[ev_free], inc="sL", amt=16)
            cL2 = kb.op("sync", lambda e, fc=fc, st4=st4: e.dma_start(
                out=g1_t, in_=tn["g1"][fc, :, st4 * STW:(st4 + 1) * STW]),
                inc="sL", amt=16)
            cL3 = kb.op("sync", lambda e, fc=fc, st4=st4: e.dma_start(
                out=x1_t, in_=tn["x1"][fc, :, st4 * STW:(st4 + 1) * STW]),
                inc="sL", amt=16)
            cV = kb.op("vector", lambda e, ps=ps: e.tensor_tensor(s1_t, ps, xn1_t, ALU.add),
                       wait=[("sP", cP), ("sL", cL3), ev_free], inc="sV")
            cV = kb.op("vector", lambda e: e.tensor_tensor(s1_t, s1_t, g1_t, ALU.mult),
                       wait=[("sV", cV)], inc="sV")
            cV = kb.op("vector", lambda e: e.tensor_tensor(o_t, s1_t, x1_t, ALU.add),
                       wait=[("sV", cV)], inc="sV")
            cS = kb.op("sync", lambda e, fc=fc, st4=st4: e.dma_start(
                out=tn["outT"][fc * 128:(fc + 1) * 128, st4 * STW:(st4 + 1) * STW],
                in_=o_t), wait=[("sV", cV)], inc="sL", amt=16)
            ev_free = ("sL", cS)
            bank_free[st4 % 2] = ("sV", cV)
        w_free[b] = ("sP", kb.cnt["sP"])
    A.reset(mO)
    A.reset(mAt)


def emit_reg_setup(kb, par_sb, non_sb, wait_c):
    semh = kb.semh

    def f(e):
        e.wait_ge(semh["sL"], wait_c)
        pr = e.alloc_register("pr")
        e.reg_load(pr, par_sb[0:1, 0:1])
        kb.regs["par"] = RuntimeValue(pr, 0, 1)
        qr = e.alloc_register("qr")
        e.reg_alu(qr, 1, RuntimeValue(pr, 0, 1), ALU.subtract)
        kb.regs["peer"] = RuntimeValue(qr, 0, 1)
        nr = e.alloc_register("nr")
        e.reg_load(nr, non_sb[0:1, 0:1])
        kb.regs["nonce"] = RuntimeValue(nr)
    kb.emit("sync", f)


def emit_kv_handshake(kb, A, tn, non_sb, fsb):
    shf = tn["shflag"]
    shk2 = tn["shk"].rearrange("(r x) t -> r (x t)", r=2)
    shv2 = tn["shv"].rearrange("(r x) t -> r (x t)", r=2)
    ksrc = tn["kown"].rearrange("a p t -> (a p t)").rearrange("(o e) -> o e", o=1)
    vsrc = tn["vown"].rearrange("a p t -> (a p t)").rearrange("(o e) -> o e", o=1)
    cK = kb.op("sync", lambda e: e.dma_start(
        out=shk2[ds(kb.regs["par"], 1), :], in_=ksrc),
        wait=[("sL", kb.cnt["sL"])], inc="sL", amt=16)
    cV = kb.op("sync", lambda e: e.dma_start(
        out=shv2[ds(kb.regs["par"], 1), :], in_=vsrc),
        inc="sL", amt=16)
    cW = kb.op("sync", lambda e: e.dma_start(
        out=shf[ds(kb.regs["par"], 1), :], in_=non_sb),
        wait=[("sL", cV)], inc="sL", amt=16)
    kb.wait("sync", ("sL", cW))
    semF = kb.semh["sF"]

    def poll(e):
        freg = e.alloc_register("freg")
        creg = e.alloc_register("creg")
        e.reg_mov(creg, 1)
        with e.While(creg):
            e.sem_clear(semF)
            e.dma_start(out=fsb, in_=shf[ds(kb.regs["peer"], 1), :]).then_inc(semF, 16)
            e.wait_ge(semF, 16)
            e.reg_load(freg, fsb[0:1, 0:1])
            e.reg_alu(creg, RuntimeValue(freg), kb.regs["nonce"], ALU.subtract)
    kb.emit("sync", poll)



def _token_idx(h):
    gs = [gtile(h, s) for s in range(NSLOT)]
    return np.concatenate([np.arange(g * 128, (g + 1) * 128) for g in gs])


def _host_common(inputs):
    c = {}
    c["w_inT"] = np.ascontiguousarray(inputs["L0_attn_W_in"].T)
    c["w_outT"] = np.ascontiguousarray(inputs["L0_attn_W_out"].T)
    for nm, key in (("kbr", "L0_attn_B_real"), ("kbi", "L0_attn_B_imag"),
                    ("kcr", "L0_attn_C_real"), ("kci", "L0_attn_C_imag")):
        c[nm] = np.ascontiguousarray(inputs[key].reshape(NCD, 16))
    c["normw"] = np.ascontiguousarray(inputs["L0_attn_norm_w"].reshape(DC, 128).T)
    for L in (0, 1):
        c[f"w1t{L}"] = np.ascontiguousarray(inputs[f"L{L}_ffn_W1"].T)
        c[f"w3t{L}"] = np.ascontiguousarray(inputs[f"L{L}_ffn_W3"].T)
        c[f"w2r{L}"] = np.ascontiguousarray(
            inputs[f"L{L}_ffn_W2"].reshape(D, 64, 128).transpose(1, 2, 0).reshape(64, 64 * NCD))
    c["qkv"] = np.ascontiguousarray(inputs["L1_attn_qkv"])
    c["o_proj"] = np.ascontiguousarray(inputs["L1_attn_o_proj"])
    c["gamma"] = np.ascontiguousarray(inputs["L1_attn_gamma"].reshape(DC, 128).T)
    c["beta"] = np.ascontiguousarray(inputs["L1_attn_beta"].reshape(DC, 128).T)
    c["h128"] = _hadamard(128)
    c["h64"] = _hadamard(64)
    c["ident"] = np.eye(128, dtype=np.float32)
    c["onescol"] = np.ones((128, 1), dtype=np.float32)
    c["onesrow"] = np.ones((1, 128), dtype=np.float32)
    return c


def _hadamard(n):
    h = np.array([[1.0]], dtype=np.float32)
    while h.shape[0] < n:
        h = np.block([[h, h], [h, -h]])
    return h.astype(np.float32)


def _common_tensors(nc, mode_l0=True):
    tn = {}

    def inp(name, shape, dt=F32):
        tn[name] = nc.dram_tensor(name, list(shape), dt, kind="ExternalInput").ap()

    def outp(name, shape, dt=F32):
        tn[name] = nc.dram_tensor(name, list(shape), dt, kind="ExternalOutput").ap()
    return tn, inp, outp


def _consts_prologue(kb, A, tn, need=("h128", "h64", "ident", "normw", "gamma", "beta")):
    nc = kb.nc
    consts = {}
    consts["onescol_bf"] = A.bf(128, 1)
    consts["onesrow_f"] = A.f32(1, 128)
    consts["eps_ln"] = A.f32(1, 1)
    consts["eps_rms"] = A.f32(1, 1)
    kb.op("vector", lambda e: e.memset(consts["eps_ln"], EPS_LN), inc="sV")
    kb.op("vector", lambda e: e.memset(consts["eps_rms"], EPS_RMS), inc="sV")
    kb.op("gpsimd", lambda e: e.dma_start(out=consts["onescol_bf"], in_=tn["onescol"]),
          inc="sG", amt=16)
    kb.op("sync", lambda e: e.dma_start(out=consts["onesrow_f"], in_=tn["onesrow"]),
          inc="sL", amt=16)
    if "h128" in need:
        consts["h128_bf"] = A.bf(128, 128)
        kb.op("gpsimd", lambda e: e.dma_start(out=consts["h128_bf"], in_=tn["h128"]),
              inc="sG", amt=16)
    if "h64" in need:
        consts["h64_bf"] = A.bf(64, 64)
        kb.op("gpsimd", lambda e: e.dma_start(out=consts["h64_bf"], in_=tn["h64"]),
              inc="sG", amt=16)
    if "ident" in need:
        consts["ident_bf"] = A.bf(128, 128)
        kb.op("gpsimd", lambda e: e.dma_start(out=consts["ident_bf"], in_=tn["ident"]),
              inc="sG", amt=16)
    if "normw" in need:
        consts["normw_sb"] = A.f32(128, DC)
        kb.op("sync", lambda e: e.dma_start(
            out=consts["normw_sb"], in_=tn["normw"]), inc="sL", amt=16)
    if "gamma" in need:
        consts["gamma_sb"] = A.f32(128, DC)
        consts["beta_sb"] = A.f32(128, DC)
        kb.op("sync", lambda e: e.dma_start(out=consts["gamma_sb"], in_=tn["gamma"]),
              inc="sL", amt=16)
        kb.op("sync", lambda e: e.dma_start(out=consts["beta_sb"], in_=tn["beta"]),
              inc="sL", amt=16)
    kb.wait("vector", ("sG", kb.cnt["sG"]), ("sL", kb.cnt["sL"]))
    kb.wait("tensor", ("sG", kb.cnt["sG"]), ("sL", kb.cnt["sL"]))
    kb.wait("scalar", ("sG", kb.cnt["sG"]), ("sL", kb.cnt["sL"]))
    return consts


def build_l0(dbg=False):
    nc = bass.Bass("TRN2", target_bir_lowering=False)
    tn, inp, outp = _common_tensors(nc)
    inp("xT", (D, T))
    inp("w_inT", (D, NCD))
    inp("w_outT", (NCD, D))
    for nm in ("kbr", "kbi", "kcr", "kci"):
        inp(nm, (NCD, 16))
    inp("normw", (128, DC))
    inp("w1t0", (D, HF))
    inp("w3t0", (D, HF))
    inp("w2r0", (64, NCD * 64))
    inp("h128", (128, 128))
    inp("h64", (64, 64))
    inp("onescol", (128, 1))
    inp("onesrow", (1, 128))
    outp("x1f", (DC, 128, T))
    tn["x1"] = tn["x1f"]
    if dbg:
        outp("s_dbg", (DC, 128, T))
        tn["s_sp"] = tn["s_dbg"]
        outp("w2p_dbg", (64, 128, D), BF16)
        tn["w2p0"] = tn["w2p_dbg"]
        outp("y_dbg", (JCN, 128, GT), BF16)
        outp("o0_dbg", (DC, 128, GT), BF16)
        outp("xn_dbg", (DC, 128, GT), BF16)
    else:
        tn["s_sp"] = nc.dram_tensor("s_sp", [DC, 128, T], F32, kind="Internal").ap()
        tn["w2p0"] = nc.dram_tensor("w2p0", [64, 128, D], BF16, kind="Internal").ap()

    with ExitStack() as ctx:
        kb = KB(nc, ctx)
        A = Arena(nc, ctx)
        pbh = [ctx.enter_context(nc.psum_tensor(f"pb{i}", [128, 512], F32))
               for i in range(8)]
        pb = [p[:, :] for p in pbh]
        consts = _consts_prologue(kb, A, tn, need=("h128", "h64", "normw"))
        k_sb = A.f32(128, JCN)
        kdone = emit_kprep(kb, A, tn, consts, k_sb)
        kb.barrier()
        emit_w2prep(kb, A, tn, 0, pb, consts)
        kb.barrier()
        emit_layer0(kb, A, tn, pb, consts, k_sb, [kdone])
        kb.wait("sync", ("sL", kb.cnt["sL"]))
        kb.finalize()
    return nc


def build_l1a(dbg=False):
    nc = bass.Bass("TRN2", target_bir_lowering=False)
    tn, inp, outp = _common_tensors(nc)
    inp("x1f", (DC, 128, T))
    inp("qkv", (D, 3 * D))
    inp("gamma", (128, DC))
    inp("beta", (128, DC))
    inp("w1t1", (D, HF))
    inp("w3t1", (D, HF))
    inp("w2r1", (64, NCD * 64))
    inp("h128", (128, 128))
    inp("h64", (64, 64))
    inp("onescol", (128, 1))
    inp("onesrow", (1, 128))
    outp("qd", (NSLOT, 128, DC, 128), BF16)
    outp("kown", (DC, 128, T), BF16)
    outp("vown", (NSLOT, 128, D), BF16)
    outp("xn1b", (DC, 128, T), BF16)
    outp("g1f", (DC, 128, T))
    if dbg:
        outp("xnorm_dbg", (DC, 128, T), BF16)
    tn["x1"] = tn["x1f"]
    tn["g1"] = tn["g1f"]
    tn["w2p1"] = nc.dram_tensor("w2p1", [64, 128, D], BF16, kind="Internal").ap()

    with ExitStack() as ctx:
        kb = KB(nc, ctx)
        A = Arena(nc, ctx)
        pbh = [ctx.enter_context(nc.psum_tensor(f"pb{i}", [128, 512], F32))
               for i in range(8)]
        pb = [p[:, :] for p in pbh]
        consts = _consts_prologue(kb, A, tn, need=("h128", "h64", "gamma"))
        emit_w2prep(kb, A, tn, 1, pb, consts)
        kb.barrier()
        emit_l1_front(kb, A, tn, pb, consts, fused=False)
        kb.wait("sync", ("sL", kb.cnt["sL"]))
        kb.finalize()
    return nc


def build_l1b():
    nc = bass.Bass("TRN2", target_bir_lowering=False)
    tn, inp, outp = _common_tensors(nc)
    inp("qd", (NSLOT, 128, DC, 128), BF16)
    inp("kg", (2 * DC, 128, T), BF16)
    inp("vg", (2 * NSLOT, 128, D), BF16)
    inp("mask", (128, 128 * NSLOT * (NSLOT + 1) * 2 // 2), BF16)
    inp("o_proj", (D, D))
    inp("xn1b", (DC, 128, T), BF16)
    inp("g1f", (DC, 128, T))
    inp("x1f", (DC, 128, T))
    inp("ident", (128, 128))
    inp("onescol", (128, 1))
    inp("onesrow", (1, 128))
    outp("outT", (D, T))
    tn["x1"] = tn["x1f"]
    tn["g1"] = tn["g1f"]

    with ExitStack() as ctx:
        kb = KB(nc, ctx)
        A = Arena(nc, ctx)
        pbh = [ctx.enter_context(nc.psum_tensor(f"pb{i}", [128, 512], F32))
               for i in range(8)]
        pb = [p[:, :] for p in pbh]
        consts = _consts_prologue(kb, A, tn, need=("ident",))
        emit_attention(kb, A, tn, pb, consts, fused=False)
        kb.wait("sync", ("sL", kb.cnt["sL"]))
        kb.finalize()
    return nc


def _mask_for(h):
    tri = np.triu(np.full((128, 128), NEGINF, np.float32), 1)
    zero = np.zeros((128, 128), np.float32)
    full = np.full((128, 128), NEGINF, np.float32)
    cols = []
    for s in range(NSLOT):
        g = gtile(h, s)
        for r in (0, 1):
            for l in range(s + 1):
                gp = gtile(r, l)
                cols.append(zero if gp < g else (tri if gp == g else full))
    return np.ascontiguousarray(
        np.concatenate(cols, axis=1).astype(ml_dtypes.bfloat16))


def build_fused():
    nc = bass.Bass("TRN2", target_bir_lowering=False)
    tn, inp, outp = _common_tensors(nc)
    inp("xT", (D, T))
    inp("w_inT", (D, NCD))
    inp("w_outT", (NCD, D))
    for nm in ("kbr", "kbi", "kcr", "kci"):
        inp(nm, (NCD, 16))
    inp("normw", (128, DC))
    inp("w1t0", (D, HF))
    inp("w3t0", (D, HF))
    inp("w2r0", (64, NCD * 64))
    inp("qkv", (D, 3 * D))
    inp("gamma", (128, DC))
    inp("beta", (128, DC))
    inp("w1t1", (D, HF))
    inp("w3t1", (D, HF))
    inp("w2r1", (64, NCD * 64))
    inp("o_proj", (D, D))
    inp("mask", (128, 128 * NSLOT * (NSLOT + 1)), BF16)
    inp("h128", (128, 128))
    inp("h64", (64, 64))
    inp("ident", (128, 128))
    inp("onescol", (128, 1))
    inp("onesrow", (1, 128))
    inp("parity", (1, 1), I32)
    inp("nonce", (1, 16), I32)
    outp("outT", (D, T))

    tn["x1"] = nc.dram_tensor("x1i", [DC, 128, T], F32, kind="Internal").ap()
    tn["s_sp"] = nc.dram_tensor("s_sp", [DC, 128, T], F32, kind="Internal").ap()
    tn["w2p0"] = nc.dram_tensor("w2p0", [64, 128, D], BF16, kind="Internal").ap()
    tn["w2p1"] = nc.dram_tensor("w2p1", [64, 128, D], BF16, kind="Internal").ap()
    tn["qd"] = nc.dram_tensor("qdi", [NSLOT, 128, DC, 128], BF16, kind="Internal").ap()
    tn["kown"] = nc.dram_tensor("kowni", [DC, 128, T], BF16, kind="Internal").ap()
    tn["vown"] = nc.dram_tensor("vowni", [NSLOT, 128, D], BF16, kind="Internal").ap()
    tn["xn1b"] = nc.dram_tensor("xn1bi", [DC, 128, T], BF16, kind="Internal").ap()
    tn["g1"] = nc.dram_tensor("g1i", [DC, 128, T], F32, kind="Internal").ap()
    tn["shk"] = nc.dram_tensor("shk", [2 * T, T], BF16, kind="Internal",
                               addr_space="Shared").ap()
    tn["shv"] = nc.dram_tensor("shv", [2 * T, D], BF16, kind="Internal",
                               addr_space="Shared").ap()
    tn["shflag"] = nc.dram_tensor("shflag", [2, 16], I32, kind="Internal",
                                  addr_space="Shared").ap()
    tn["kg"] = tn["shk"].rearrange("(a p) t -> a p t", p=128)
    tn["vg"] = tn["shv"].rearrange("(a p) t -> a p t", p=128)

    with ExitStack() as ctx:
        kb = KB(nc, ctx)
        A = Arena(nc, ctx)
        pbh = [ctx.enter_context(nc.psum_tensor(f"pb{i}", [128, 512], F32))
               for i in range(8)]
        pb = [p[:, :] for p in pbh]
        consts = _consts_prologue(kb, A, tn,
                                  need=("h128", "h64", "ident", "normw", "gamma"))
        par_sb = A.i32(1, 1)
        non_sb = A.i32(1, 16)
        fsb = A.i32(1, 16)
        kb.op("sync", lambda e: e.dma_start(out=par_sb, in_=tn["parity"]),
              inc="sL", amt=16)
        cN = kb.op("sync", lambda e: e.dma_start(out=non_sb, in_=tn["nonce"]),
                   inc="sL", amt=16)
        emit_reg_setup(kb, par_sb, non_sb, cN)
        k_sb = A.f32(128, JCN)
        kdone = emit_kprep(kb, A, tn, consts, k_sb)
        kb.barrier()
        emit_w2prep(kb, A, tn, 0, pb, consts)
        kb.barrier()
        emit_w2prep(kb, A, tn, 1, pb, consts)
        kb.barrier()
        emit_layer0(kb, A, tn, pb, consts, k_sb, [kdone])
        emit_l1_front(kb, A, tn, pb, consts, fused=True)
        emit_kv_handshake(kb, A, tn, non_sb, fsb)
        emit_attention(kb, A, tn, pb, consts, fused=True)
        kb.wait("sync", ("sL", kb.cnt["sL"]))
        kb.finalize()
    return nc


_CACHE = {}
KERNEL_MODE = "fused"


def kernel(**inputs):
    if KERNEL_MODE == "fused":
        return kernel_fused(**inputs)
    return kernel_3launch(**inputs)


def kernel_fused(**inputs):
    com = _host_common(inputs)
    masks = [_mask_for(h) for h in (0, 1)]
    x = inputs["L0_attn_x"]
    if "ncf" not in _CACHE:
        _CACHE["ncf"] = build_fused()
    ncf = _CACHE["ncf"]
    idxs = [_token_idx(0), _token_idx(1)]
    cores = list(range(8))
    nonce = np.array([[np.random.randint(1, 2 ** 30)] * 16], dtype=np.int32)
    in_maps = []
    for c in cores:
        b, h = c // 2, c % 2
        m = {"xT": np.ascontiguousarray(x[b][idxs[h]].T),
             "mask": masks[h],
             "parity": np.array([[h]], dtype=np.int32),
             "nonce": nonce}
        for k in ("w_inT", "w_outT", "kbr", "kbi", "kcr", "kci", "normw",
                  "w1t0", "w3t0", "w2r0", "qkv", "gamma", "beta",
                  "w1t1", "w3t1", "w2r1", "o_proj",
                  "h128", "h64", "ident", "onescol", "onesrow"):
            m[k] = com[k]
        in_maps.append(m)
    res = run_bass_kernel_spmd(ncf, in_maps, core_ids=cores)
    out = np.empty((4, 4096, D), dtype=np.float32)
    for c in cores:
        b, h = c // 2, c % 2
        out[b][idxs[h]] = res.results[c]["outT"].T
    return out


def kernel_3launch(**inputs):
    com = _host_common(inputs)
    masks = [_mask_for(h) for h in (0, 1)]
    x = inputs["L0_attn_x"]

    if "ncs" not in _CACHE:
        _CACHE["ncs"] = (build_l0(), build_l1a(), build_l1b())
    nc0, nc1, nc2 = _CACHE["ncs"]

    idxs = [_token_idx(0), _token_idx(1)]
    cores = list(range(8))

    m1 = []
    for c in cores:
        b, h = c // 2, c % 2
        m = {"xT": np.ascontiguousarray(x[b][idxs[h]].T)}
        for k in ("w_inT", "w_outT", "kbr", "kbi", "kcr", "kci", "normw",
                  "w1t0", "w3t0", "w2r0", "h128", "h64", "onescol", "onesrow"):
            m[k] = com[k]
        m1.append(m)
    r1 = run_bass_kernel_spmd(nc0, m1, core_ids=cores)

    m2 = []
    for c in cores:
        m = {"x1f": r1.results[c]["x1f"]}
        for k in ("qkv", "gamma", "beta", "w1t1", "w3t1", "w2r1",
                  "h128", "h64", "onescol", "onesrow"):
            m[k] = com[k]
        m2.append(m)
    r2 = run_bass_kernel_spmd(nc1, m2, core_ids=cores)

    m3 = []
    for c in cores:
        b, h = c // 2, c % 2
        e, o = 2 * b, 2 * b + 1
        kgather = np.concatenate([r2.results[e]["kown"], r2.results[o]["kown"]], axis=0)
        vgather = np.concatenate([r2.results[e]["vown"], r2.results[o]["vown"]], axis=0)
        m = {"qd": r2.results[c]["qd"], "kg": kgather, "vg": vgather,
             "mask": masks[h], "o_proj": com["o_proj"],
             "xn1b": r2.results[c]["xn1b"], "g1f": r2.results[c]["g1f"],
             "x1f": r1.results[c]["x1f"], "ident": com["ident"],
             "onescol": com["onescol"], "onesrow": com["onesrow"]}
        m3.append(m)
    r3 = run_bass_kernel_spmd(nc2, m3, core_ids=cores)

    out = np.empty((4, 4096, D), dtype=np.float32)
    for c in cores:
        b, h = c // 2, c % 2
        out[b][idxs[h]] = r3.results[c]["outT"].T
    return out


# version 08 (final)
# speedup vs baseline: 1.3033x; 1.0398x over previous
import numpy as np
import ml_dtypes
from contextlib import ExitStack

import concourse.bass as bass
import concourse.mybir as mybir
from concourse.bass_utils import run_bass_kernel_spmd
from concourse.expressions_rust import RuntimeValue

F32 = mybir.dt.float32
BF16 = mybir.dt.bfloat16
I32 = mybir.dt.int32
AF = mybir.ActivationFunctionType
ALU = mybir.AluOpType
AX = mybir.AxisListType
ds = bass.ds
ts = bass.ts

D, NCD, HF, T = 2048, 4096, 8192, 2048
DC, JCN, HCN = 16, 32, 64
GT, NG, STW, NS = 1024, 2, 512, 2
NSLOT = 16
EPS_LN = 1e-5
EPS_RMS = float(np.finfo(np.float32).eps)
QSCALE = float(1.0 / np.sqrt(2048.0))
WSCALE = float(1.0 / np.sqrt(8192.0))
INV_D = float(1.0 / 2048.0)
NEGINF = -1e30
FFN_HARDEN = True

def gtile(h, s):
    if h == 0:
        return 2 * s + (1 if s % 2 else 0)
    return 2 * s + (0 if s % 2 else 1)



class Arena:
    def __init__(self, nc, ctx, elems=106400):
        self.t = ctx.enter_context(nc.sbuf_tensor("arena", [128, elems], BF16))
        self.off = 0
        self.elems = elems

    def _take(self, n_bf):
        off = (self.off + 31) & ~31
        self.off = off + n_bf
        assert self.off <= self.elems, f"SBUF arena overflow: {self.off} > {self.elems}"
        return off

    def bf(self, *shape):
        n = int(np.prod(shape[1:]))
        off = self._take(n)
        ap = self.t[0:shape[0], off:off + n]
        if len(shape) == 3:
            ap = ap.rearrange("p (a b) -> p a b", a=shape[1], b=shape[2])
        return ap

    def f32(self, *shape):
        n = int(np.prod(shape[1:]))
        off = self._take(2 * n)
        ap = self.t[0:shape[0], off:off + 2 * n].bitcast(F32)
        if len(shape) == 3:
            ap = ap.rearrange("p (a b) -> p a b", a=shape[1], b=shape[2])
        return ap

    def i32(self, *shape):
        n = int(np.prod(shape[1:]))
        off = self._take(2 * n)
        return self.t[0:shape[0], off:off + 2 * n].bitcast(I32)

    def mark(self):
        return self.off

    def reset(self, m):
        self.off = m


class KB:
    ENGINES = ("sync", "gpsimd", "scalar", "vector", "tensor")

    def __init__(self, nc, ctx):
        self.nc, self.ctx = nc, ctx
        self.th = {e: [] for e in self.ENGINES}
        self.semh, self.cnt = {}, {}
        for s in ("sL", "sG", "sP", "sV", "sA", "sB", "sO", "sF"):
            self.semh[s] = ctx.enter_context(nc.semaphore(s))
            self.cnt[s] = 0
        self.regs = {}

    def bump(self, s, amt):
        self.cnt[s] += amt
        return self.cnt[s]

    def emit(self, eng, fn):
        self.th[eng].append(fn)

    def now(self):
        return tuple((s, self.cnt[s]) for s in ("sP", "sV", "sA", "sL", "sG")
                     if self.cnt[s] > 0)

    def op(self, eng, build, wait=(), inc=None, amt=1):
        semh = self.semh
        tgt = None
        if inc:
            tgt = (semh[inc], amt)
            self.bump(inc, amt)
        wl = []
        for w in wait:
            if isinstance(w, tuple) and len(w) == 2 and isinstance(w[0], str):
                wl.append(w)
            else:
                wl.extend(w)
        wl = tuple(wl)

        def f(e):
            for sn, v in wl:
                e.wait_ge(semh[sn], v)
            ins = build(e)
            if tgt is not None:
                ins.then_inc(*tgt)
        self.emit(eng, f)
        return self.cnt[inc] if inc else None

    def wait(self, eng, *waits):
        semh = self.semh
        wl = tuple(waits)

        def f(e):
            for sn, v in wl:
                e.wait_ge(semh[sn], v)
        self.emit(eng, f)

    def barrier(self):
        semh = self.semh["sB"]
        self.bump("sB", 5)
        target = self.cnt["sB"]
        own_sem = {"sync": "sL", "gpsimd": "sG", "vector": "sV",
                   "scalar": "sA", "tensor": "sP"}
        for en in self.ENGINES:
            own = own_sem[en]
            own_v = self.cnt[own]
            ownh = self.semh[own]

            def f(e, own_v=own_v, ownh=ownh):
                if own_v > 0:
                    e.wait_ge(ownh, own_v)
                e.nop().then_inc(semh, 1)
                e.wait_ge(semh, target)
            self.emit(en, f)

    def finalize(self):
        with self.nc.Block() as block:
            for ename, meth in (("sync", block.sync), ("gpsimd", block.gpsimd),
                                ("scalar", block.scalar), ("vector", block.vector),
                                ("tensor", block.tensor)):
                ths = self.th[ename]

                def body(eng, ths=ths):
                    for f in ths:
                        f(eng)
                meth(body)



def emit_ln(kb, A, src_chunk_fn, xn_bf, pb, consts, g0, eps, n_loads_eng="sync"):
    onescol_bf, onesrow_f = consts["onescol_bf"], consts["onesrow_f"]
    m = A.mark()
    xf = [A.f32(128, STW), A.f32(128, STW)]
    xb = A.bf(128, STW)
    sq = A.bf(128, STW)
    rows = A.f32(1, 8 * STW).rearrange("p (a b) -> p a b", a=8, b=STW)
    xf_free = [kb.now(), kb.now()]
    xb_free = kb.now()
    sq_free = kb.now()

    for st in range(NS):
        statx = pb[0][0:1, :]
        statq = pb[1][0:1, :]
        for dc in range(DC):
            b = dc % 2
            cL = kb.op(n_loads_eng, lambda e, dc=dc, st=st, b=b: e.dma_start(
                out=xf[b], in_=src_chunk_fn(dc, st)),
                wait=[xf_free[b]], inc="sL", amt=16)
            cV = kb.op("vector", lambda e, b=b: e.tensor_copy(xb, xf[b]),
                       wait=[("sL", cL), xb_free], inc="sV")
            xf_free[b] = ("sV", cV)
            cA = kb.op("scalar", lambda e: e.activation(sq, xb, AF.Square),
                       wait=[("sV", cV), sq_free], inc="sA")
            xb_free = ("sA", cA)
            cP1 = kb.op("tensor", lambda e, dc=dc: e.matmul(
                statx, onescol_bf, xb, start=(dc == 0), stop=(dc == DC - 1)),
                wait=[("sV", cV)], inc="sP")
            xb_free = ("sP", cP1)
            cP2 = kb.op("tensor", lambda e, dc=dc: e.matmul(
                statq, onescol_bf, sq, start=(dc == 0), stop=(dc == DC - 1)),
                wait=[("sA", cA)], inc="sP")
            sq_free = ("sP", cP2)
        mean_r = rows[:, 0, :]
        msq_r = rows[:, 1, :]
        var_r = rows[:, 2, :]
        sd_r = rows[:, 3, :]
        rstd_r = rows[:, 4, :]
        mr_r = rows[:, 5, :]
        c1 = kb.op("vector", lambda e: e.tensor_scalar_mul(mean_r, statx, INV_D),
                   wait=[("sP", kb.cnt["sP"])], inc="sV")
        c2 = kb.op("vector", lambda e: e.tensor_scalar_mul(msq_r, statq, INV_D),
                   wait=[("sV", c1)], inc="sV")
        c3 = kb.op("vector", lambda e: e.tensor_tensor(var_r, mean_r, mean_r, ALU.mult),
                   wait=[("sV", c2)], inc="sV")
        c4 = kb.op("vector", lambda e: e.tensor_tensor(var_r, msq_r, var_r, ALU.subtract),
                   wait=[("sV", c3)], inc="sV")
        c5 = kb.op("scalar", lambda e: e.activation(sd_r, var_r, AF.Sqrt, bias=eps),
                   wait=[("sV", c4)], inc="sA")
        c6 = kb.op("vector", lambda e: e.reciprocal(rstd_r, sd_r),
                   wait=[("sA", c5)], inc="sV")
        c7 = kb.op("vector", lambda e: e.tensor_tensor(mr_r, mean_r, rstd_r, ALU.mult),
                   wait=[("sV", c6)], inc="sV")
        rstd_bc = pb[2][:, :]
        mr_bc = pb[3][:, :]
        cb1 = kb.op("tensor", lambda e: e.matmul(rstd_bc, onesrow_f, rstd_r,
                                                 start=True, stop=True),
                    wait=[("sV", c7)], inc="sP")
        cb2 = kb.op("tensor", lambda e: e.matmul(mr_bc, onesrow_f, mr_r,
                                                 start=True, stop=True),
                    wait=[("sV", c7)], inc="sP")
        tmp = A.f32(128, STW)
        tmp_free = kb.now()
        for dc in range(DC):
            b = dc % 2
            cL = kb.op(n_loads_eng, lambda e, dc=dc, st=st, b=b: e.dma_start(
                out=xf[b], in_=src_chunk_fn(dc, st)),
                wait=[xf_free[b]], inc="sL", amt=16)
            cV = kb.op("vector", lambda e, b=b: e.tensor_tensor(tmp, xf[b], rstd_bc, ALU.mult),
                       wait=[("sL", cL), ("sP", cb1), tmp_free], inc="sV")
            xf_free[b] = ("sV", cV)
            cV2 = kb.op("vector", lambda e, dc=dc, st=st: e.tensor_tensor(
                xn_bf[dc][:, st * STW:(st + 1) * STW], tmp, mr_bc, ALU.subtract),
                wait=[("sV", cV), ("sP", cb2)], inc="sV")
            tmp_free = ("sV", cV2)
        kb.wait("tensor", ("sV", kb.cnt["sV"]))
    A.reset(m)


def emit_w2prep(kb, A, tn, layer, pb, consts):
    h64_bf = consts["h64_bf"]
    w2r = tn[f"w2r{layer}"]
    w2p = tn[f"w2p{layer}"]
    m = A.mark()
    NSTR = 8
    wbuf = A.bf(64, NSTR * STW)
    obuf = A.bf(64, NSTR * STW)
    wfree, ofree = kb.now(), kb.now()
    total = NCD * 64
    for blk in range(total // (NSTR * STW)):
        x0 = blk * NSTR * STW
        cG = kb.op("gpsimd", lambda e, x0=x0: e.dma_start(
            out=wbuf, in_=w2r[:, x0:x0 + NSTR * STW]),
            wait=[wfree], inc="sG", amt=16)
        lastA = None
        for k in range(NSTR):
            ps = pb[k % 2][0:64, :]
            cP = kb.op("tensor", lambda e, k=k, ps=ps: e.matmul(
                ps, h64_bf, wbuf[:, k * STW:(k + 1) * STW], start=True, stop=True),
                wait=[("sG", cG), ("sA", kb.cnt["sA"])],
                inc="sP")
            cA = kb.op("scalar", lambda e, k=k, ps=ps: e.activation(
                obuf[:, k * STW:(k + 1) * STW], ps, AF.Copy, scale=WSCALE),
                wait=[("sP", cP), ofree], inc="sA")
            lastA = cA
        wfree = ("sP", kb.cnt["sP"])
        b0 = x0 // 2048
        cS = kb.op("sync", lambda e, b0=b0: e.dma_start(
            out=w2p[:, b0:b0 + 2, :], in_=obuf.rearrange("p (b d) -> p b d", b=2, d=2048)),
            wait=[("sA", lastA)], inc="sL", amt=16)
        ofree = ("sL", cS)
    A.reset(m)


def emit_kprep(kb, A, tn, consts, k_sb):
    m = A.mark()
    tiles = {}
    for nm in ("kbr", "kbi", "kcr", "kci"):
        t = A.f32(128, JCN, 16)
        c = kb.op("sync", lambda e, nm=nm, t=t: e.dma_start(
            out=t, in_=tn[nm].rearrange("(c p) s -> p c s", p=128)),
            inc="sL", amt=16)
        tiles[nm] = (t, c)
    t1 = A.f32(128, JCN, 16)
    t2 = A.f32(128, JCN, 16)
    cw = kb.cnt["sL"]
    c1 = kb.op("vector", lambda e: e.tensor_mul(t1, tiles["kcr"][0], tiles["kbr"][0]),
               wait=[("sL", cw)], inc="sV")
    c2 = kb.op("vector", lambda e: e.tensor_mul(t2, tiles["kci"][0], tiles["kbi"][0]),
               wait=[("sV", c1)], inc="sV")
    c3 = kb.op("vector", lambda e: e.tensor_tensor(t1, t1, t2, ALU.subtract),
               wait=[("sV", c2)], inc="sV")
    c4 = kb.op("vector", lambda e: e.reduce_sum(k_sb.unsqueeze(2), t1, axis=AX.X),
               wait=[("sV", c3)], inc="sV")
    A.reset(m)
    return c4


def emit_proj(kb, A, tn, wname, wcols, kchunks, xn_bf, out_cb, pb, bank0,
              extra_wait=(), col_off=0):
    w = tn[wname]
    nfc = wcols // 128
    m = A.mark()
    wsb = [A.bf(128, kchunks, 128), A.bf(128, kchunks, 128)]
    wfree = [kb.now(), kb.now()]
    bank_free = [kb.now(), kb.now()]
    for fc in range(nfc):
        b = fc % 2
        cG = kb.op("gpsimd", lambda e, fc=fc, b=b: e.dma_start(
            out=wsb[b], in_=w.rearrange("(k p) f -> p k f", p=128)[
                :, :, col_off + fc * 128: col_off + (fc + 1) * 128]),
            wait=[wfree[b]] + list(extra_wait), inc="sG", amt=16)
        for st in range(NS):
            ps = pb[bank0 + (st % 2)][:, :]
            cP = None
            for dk in range(kchunks):
                cP = kb.op("tensor", lambda e, dk=dk, st=st, b=b, ps=ps: e.matmul(
                    ps, wsb[b][:, dk, :], xn_bf[dk][:, st * STW:(st + 1) * STW],
                    start=(dk == 0), stop=(dk == kchunks - 1)),
                    wait=([("sG", cG), bank_free[st % 2]] if dk == 0 else []),
                    inc="sP")
            bank_free[st % 2] = out_cb(fc, st, ps, cP)
        wfree[b] = ("sP", kb.cnt["sP"])
    A.reset(m)


def emit_layer0(kb, A, tn, pb, consts, k_sb, kprep_done):
    xT = tn["xT"]
    x1 = tn["x1"]
    s_sp = tn["s_sp"]
    normw_sb = consts["normw_sb"]
    onescol_bf, onesrow_f = consts["onescol_bf"], consts["onesrow_f"]
    h128_bf = consts["h128_bf"]

    for g in range(NG):
        t0 = g * GT
        mG = A.mark()
        xn_bf = [A.bf(128, GT) for _ in range(DC)]

        emit_ln(kb, A, lambda dc, st, t0=t0: xT[dc * 128:(dc + 1) * 128,
                                                t0 + st * STW: t0 + (st + 1) * STW],
                xn_bf, pb, consts, g, consts["eps_ln"])
        kb.barrier()

        mM = A.mark()
        y_bf = [A.bf(128, GT) for _ in range(JCN)]

        def y_evac(jc, st, ps, cP):
            cV = kb.op("vector", lambda e, jc=jc, st=st, ps=ps: e.tensor_scalar(
                y_bf[jc][:, st * STW:(st + 1) * STW], ps,
                k_sb[:, jc:jc + 1], None, ALU.mult),
                wait=[("sP", cP), ("sV", kprep_done[0])], inc="sV")
            return ("sV", cV)
        emit_proj(kb, A, tn, "w_inT", NCD, DC, xn_bf, y_evac, pb, 0)
        kb.wait("tensor", ("sV", kb.cnt["sV"]))

        out0_bf = [A.bf(128, GT) for _ in range(DC)]
        sq_t = A.bf(128, STW)
        sq_free = [kb.now()]
        rms_ps = [pb[4][0:1, :], pb[5][0:1, :]]

        def out0_evac(dc, st, ps, cP):
            cV = kb.op("vector", lambda e, dc=dc, st=st, ps=ps: e.tensor_copy(
                out0_bf[dc][:, st * STW:(st + 1) * STW], ps),
                wait=[("sP", cP)], inc="sV")
            cA = kb.op("scalar", lambda e, ps=ps: e.activation(sq_t, ps, AF.Square),
                       wait=[("sV", cV), sq_free[0]], inc="sA")
            cP2 = kb.op("tensor", lambda e, dc=dc, st=st: e.matmul(
                rms_ps[st], onescol_bf, sq_t, start=(dc == 0), stop=(dc == DC - 1)),
                wait=[("sA", cA)], inc="sP")
            sq_free[0] = ("sP", cP2)
            return ("sA", cA)
        emit_proj(kb, A, tn, "w_outT", D, JCN, y_bf, out0_evac, pb, 0)
        sV_out0 = kb.cnt["sV"]

        mrow = A.mark()
        rows = A.f32(1, 4 * STW).rearrange("p (a b) -> p a b", a=4, b=STW)
        rstd2_bc = [pb[6][:, :], pb[7][:, :]]
        for st in range(NS):
            ms_r, sd_r, rstd_r = rows[:, 0, :], rows[:, 1, :], rows[:, 2, :]
            c1 = kb.op("vector", lambda e, st=st: e.tensor_scalar_mul(ms_r, rms_ps[st], INV_D),
                       wait=[("sP", kb.cnt["sP"])], inc="sV")
            c2 = kb.op("scalar", lambda e: e.activation(sd_r, ms_r, AF.Sqrt, bias=consts["eps_rms"]),
                       wait=[("sV", c1)], inc="sA")
            c3 = kb.op("vector", lambda e: e.reciprocal(rstd_r, sd_r),
                       wait=[("sA", c2)], inc="sV")
            cb = kb.op("tensor", lambda e, st=st: e.matmul(
                rstd2_bc[st], onesrow_f, rstd_r, start=True, stop=True),
                wait=[("sV", c3)], inc="sP")
        cb_done = kb.cnt["sP"]
        tmpa = A.f32(128, STW)
        tmpa_free = kb.now()
        for dc in range(DC):
            for st in range(NS):
                cV = kb.op("vector", lambda e, dc=dc, st=st: e.tensor_scalar(
                    tmpa, out0_bf[dc][:, st * STW:(st + 1) * STW],
                    normw_sb[:, dc:dc + 1], None, ALU.mult),
                    wait=[tmpa_free, ("sP", cb_done), ("sV", sV_out0)], inc="sV")
                cV = kb.op("vector", lambda e, st=st: e.tensor_tensor(
                    tmpa, tmpa, rstd2_bc[st], ALU.mult),
                    wait=[("sV", cV)], inc="sV")
                cV = kb.op("vector", lambda e, dc=dc, st=st: e.tensor_tensor(
                    tmpa, tmpa, xn_bf[dc][:, st * STW:(st + 1) * STW], ALU.add),
                    wait=[("sV", cV)], inc="sV")
                cS = kb.op("sync", lambda e, dc=dc, st=st, t0=t0: e.dma_start(
                    out=s_sp[dc, :, t0 + st * STW: t0 + (st + 1) * STW], in_=tmpa),
                    wait=[("sV", cV)], inc="sL", amt=16)
                tmpa_free = ("sL", cS)
        if "y_dbg" in tn and g == 0:
            for jc in range(JCN):
                kb.op("sync", lambda e, jc=jc: e.dma_start(
                    out=tn["y_dbg"][jc, :, 0:GT], in_=y_bf[jc]),
                    wait=[("sV", kb.cnt["sV"])], inc="sL", amt=16)
            for dc in range(DC):
                kb.op("sync", lambda e, dc=dc: e.dma_start(
                    out=tn["o0_dbg"][dc, :, 0:GT], in_=out0_bf[dc]),
                    inc="sL", amt=16)
            for dc in range(DC):
                kb.op("sync", lambda e, dc=dc: e.dma_start(
                    out=tn["xn_dbg"][dc, :, 0:GT], in_=xn_bf[dc]),
                    inc="sL", amt=16)
            kb.wait("sync", ("sL", kb.cnt["sL"]))
        A.reset(mrow)
        A.reset(mM)
        kb.barrier()

        emit_ffn(kb, A, tn, pb, consts, xn_bf, layer=0, g=g)
        A.reset(mG)
        kb.barrier()


def emit_ffn(kb, A, tn, pb, consts, xn_bf, layer, g):
    h128_bf = consts["h128_bf"]
    t0 = g * GT
    w1 = tn[f"w1t{layer}"]
    w3 = tn[f"w3t{layer}"]
    w2p = tn[f"w2p{layer}"]
    mF = A.mark()
    y1 = [A.bf(128, GT) for _ in range(HCN)]
    mW = A.mark()
    w1sb = A.bf(128, DC, 512)
    w3sb = A.bf(128, DC, 512)
    silu_t = A.bf(128, STW)
    hid_t = A.bf(128, STW)
    w1free = [kb.now(), kb.now()]
    w3free = [kb.now(), kb.now()]
    silu_free, hid_free = kb.now(), kb.now()
    bga_free = kb.now()
    bgl_free = kb.now()
    by1_free = kb.now()

    cG1 = cG3 = None
    for hc in range(HCN):
        hcl = hc % 4
        if hcl == 0:
            hb = hc // 4
            cG1 = kb.op("gpsimd", lambda e, hb=hb: e.dma_start(
                out=w1sb, in_=w1.rearrange("(k p) f -> p k f", p=128)[:, :, hb * 512:(hb + 1) * 512]),
                wait=[w1free[0]] + ([kb.now()] if FFN_HARDEN else []), inc="sG", amt=16)
            cG3 = kb.op("gpsimd", lambda e, hb=hb: e.dma_start(
                out=w3sb, in_=w3.rearrange("(k p) f -> p k f", p=128)[:, :, hb * 512:(hb + 1) * 512]),
                wait=[w3free[0]] + ([kb.now()] if FFN_HARDEN else []), inc="sG", amt=16)
        for st in range(NS):
            psg = pb[0 + st][:, :]
            psl = pb[2 + st][:, :]
            psy = pb[4 + st][:, :]
            cPg = None
            for dk in range(DC):
                cPg = kb.op("tensor", lambda e, dk=dk, st=st, hcl=hcl: e.matmul(
                    psg, w1sb[:, dk, hcl * 128:(hcl + 1) * 128],
                    xn_bf[dk][:, st * STW:(st + 1) * STW],
                    start=(dk == 0), stop=(dk == DC - 1)),
                    wait=([("sG", cG1), bga_free] + ([kb.now()] if FFN_HARDEN else []) if dk == 0 else []), inc="sP")
            cPl = None
            for dk in range(DC):
                cPl = kb.op("tensor", lambda e, dk=dk, st=st, hcl=hcl: e.matmul(
                    psl, w3sb[:, dk, hcl * 128:(hcl + 1) * 128],
                    xn_bf[dk][:, st * STW:(st + 1) * STW],
                    start=(dk == 0), stop=(dk == DC - 1)),
                    wait=([("sG", cG3), bgl_free] + ([kb.now()] if FFN_HARDEN else []) if dk == 0 else []), inc="sP")
            cA = kb.op("scalar", lambda e: e.activation(silu_t, psg, AF.Silu),
                       wait=[("sP", cPg), silu_free] + ([kb.now()] if FFN_HARDEN else []), inc="sA")
            bga_free = ("sA", cA)
            cV = kb.op("vector", lambda e: e.tensor_tensor(hid_t, silu_t, psl, ALU.mult),
                       wait=[("sA", cA), ("sP", cPl), hid_free] + ([kb.now()] if FFN_HARDEN else []), inc="sV")
            silu_free = ("sV", cV)
            bgl_free = ("sV", cV)
            cPy = kb.op("tensor", lambda e: e.matmul(psy, h128_bf, hid_t,
                                                     start=True, stop=True),
                        wait=[("sV", cV), by1_free] + ([kb.now()] if FFN_HARDEN else []), inc="sP")
            hid_free = ("sP", cPy)
            cA2 = kb.op("scalar", lambda e, hc=hc, st=st: e.activation(
                y1[hc][:, st * STW:(st + 1) * STW], psy, AF.Copy),
                wait=[("sP", cPy)] + ([kb.now()] if FFN_HARDEN else []), inc="sA")
            by1_free = ("sA", cA2)
        w1free[0] = ("sP", kb.cnt["sP"])
        w3free[0] = ("sP", kb.cnt["sP"])

    kb.wait("gpsimd", ("sP", kb.cnt["sP"]), ("sV", kb.cnt["sV"]), ("sA", kb.cnt["sA"]))
    A.reset(mW)
    mC = A.mark()
    w2sb = [A.bf(128, 8, 256), A.bf(128, 8, 256)]
    w2free = [kb.now(), kb.now()]
    gx_t = A.f32(128, STW)
    sx_t = A.f32(128, STW)
    xx_t = A.f32(128, STW)
    o_t = A.f32(128, STW)
    ev_free = kb.now()
    for ps_i in range(8):
        if FFN_HARDEN:
            kb.barrier()
        dc0 = ps_i * 2
        banks = [pb[4 + i][:, :] for i in range(4)]
        bank_wait = ("sA", kb.cnt["sA"])
        cP = None
        for hb in range(HCN // 8):
            b = hb % 2
            cG = kb.op("gpsimd", lambda e, hb=hb, dc0=dc0, b=b: e.dma_start(
                out=w2sb[b], in_=w2p[hb * 8:(hb + 1) * 8, :,
                                     dc0 * 128:(dc0 + 2) * 128].rearrange(
                    "h p t -> p h t")),
                wait=[w2free[b]] + ([kb.now()] if FFN_HARDEN else []), inc="sG", amt=16)
            for hcl in range(8):
                hc = hb * 8 + hcl
                for dcl in range(2):
                    for st in range(NS):
                        ps = banks[dcl * 2 + st]
                        cP = kb.op("tensor", lambda e, hcl=hcl, hc=hc, dcl=dcl, st=st, b=b, ps=ps: e.matmul(
                            ps, w2sb[b][:, hcl, dcl * 128:(dcl + 1) * 128],
                            y1[hc][:, st * STW:(st + 1) * STW],
                            start=(hc == 0), stop=(hc == HCN - 1)),
                            wait=([("sG", cG)] + ([bank_wait] if hc == 0 else [])
                                  if hcl == 0 and dcl == 0 and st == 0 else []),
                            inc="sP")
            w2free[b] = ("sP", kb.cnt["sP"])
        for dcl in range(2):
            dc = dc0 + dcl
            for st in range(NS):
                ps = banks[dcl * 2 + st]
                cA = kb.op("scalar", lambda e, ps=ps: e.activation(gx_t, ps, AF.Silu),
                           wait=[("sP", cP), ev_free], inc="sA")
                if layer == 0:
                    cL1 = kb.op("sync", lambda e, dc=dc, st=st: e.dma_start(
                        out=sx_t, in_=tn["s_sp"][dc, :, t0 + st * STW: t0 + (st + 1) * STW]),
                        wait=[ev_free], inc="sL", amt=16)
                    cL2 = kb.op("sync", lambda e, dc=dc, st=st: e.dma_start(
                        out=xx_t, in_=tn["xT"][dc * 128:(dc + 1) * 128,
                                               t0 + st * STW: t0 + (st + 1) * STW]),
                        inc="sL", amt=16)
                    cV = kb.op("vector", lambda e: e.tensor_tensor(o_t, sx_t, gx_t, ALU.mult),
                               wait=[("sA", cA), ("sL", cL2), ev_free], inc="sV")
                    cV = kb.op("vector", lambda e: e.tensor_tensor(o_t, o_t, xx_t, ALU.add),
                               wait=[("sV", cV)], inc="sV")
                    cS = kb.op("sync", lambda e, dc=dc, st=st: e.dma_start(
                        out=tn["x1"][dc, :, t0 + st * STW: t0 + (st + 1) * STW], in_=o_t),
                        wait=[("sV", cV)], inc="sL", amt=16)
                    ev_free = ("sL", cS)
                else:
                    cS = kb.op("sync", lambda e, dc=dc, st=st: e.dma_start(
                        out=tn["g1"][dc, :, t0 + st * STW: t0 + (st + 1) * STW], in_=gx_t),
                        wait=[("sA", cA)], inc="sL", amt=16)
                    ev_free = ("sL", cS)
    A.reset(mC)
    A.reset(mF)


def emit_ln_inner(kb, A, src_bf, out_bf, pb, consts, eps):
    onescol_bf, onesrow_f = consts["onescol_bf"], consts["onesrow_f"]
    gamma_sb, beta_sb = consts["gamma_sb"], consts["beta_sb"]
    m = A.mark()
    sq = A.bf(128, STW)
    rows = A.f32(1, 8 * STW).rearrange("p (a b) -> p a b", a=8, b=STW)
    tmp = A.f32(128, STW)
    sq_free = kb.now()
    tmp_free = kb.now()
    for st in range(NS):
        statx = pb[0][0:1, :]
        statq = pb[1][0:1, :]
        for dc in range(DC):
            cA = kb.op("scalar", lambda e, dc=dc, st=st: e.activation(
                sq, src_bf[dc][:, st * STW:(st + 1) * STW], AF.Square),
                wait=[sq_free], inc="sA")
            cP1 = kb.op("tensor", lambda e, dc=dc, st=st: e.matmul(
                statx, onescol_bf, src_bf[dc][:, st * STW:(st + 1) * STW],
                start=(dc == 0), stop=(dc == DC - 1)), inc="sP")
            cP2 = kb.op("tensor", lambda e, dc=dc: e.matmul(
                statq, onescol_bf, sq, start=(dc == 0), stop=(dc == DC - 1)),
                wait=[("sA", cA)], inc="sP")
            sq_free = ("sP", cP2)
        mean_r, msq_r, var_r = rows[:, 0, :], rows[:, 1, :], rows[:, 2, :]
        sd_r, rstd_r, mr_r = rows[:, 3, :], rows[:, 4, :], rows[:, 5, :]
        c1 = kb.op("vector", lambda e: e.tensor_scalar_mul(mean_r, statx, INV_D),
                   wait=[("sP", kb.cnt["sP"])], inc="sV")
        c2 = kb.op("vector", lambda e: e.tensor_scalar_mul(msq_r, statq, INV_D),
                   wait=[("sV", c1)], inc="sV")
        c3 = kb.op("vector", lambda e: e.tensor_tensor(var_r, mean_r, mean_r, ALU.mult),
                   wait=[("sV", c2)], inc="sV")
        c4 = kb.op("vector", lambda e: e.tensor_tensor(var_r, msq_r, var_r, ALU.subtract),
                   wait=[("sV", c3)], inc="sV")
        c5 = kb.op("scalar", lambda e: e.activation(sd_r, var_r, AF.Sqrt, bias=eps),
                   wait=[("sV", c4)], inc="sA")
        c6 = kb.op("vector", lambda e: e.reciprocal(rstd_r, sd_r),
                   wait=[("sA", c5)], inc="sV")
        c7 = kb.op("vector", lambda e: e.tensor_tensor(mr_r, mean_r, rstd_r, ALU.mult),
                   wait=[("sV", c6)], inc="sV")
        rstd_bc = pb[2][:, :]
        mr_bc = pb[3][:, :]
        cb1 = kb.op("tensor", lambda e: e.matmul(rstd_bc, onesrow_f, rstd_r,
                                                 start=True, stop=True),
                    wait=[("sV", c7)], inc="sP")
        cb2 = kb.op("tensor", lambda e: e.matmul(mr_bc, onesrow_f, mr_r,
                                                 start=True, stop=True),
                    wait=[("sV", c7)], inc="sP")
        for dc in range(DC):
            cV = kb.op("vector", lambda e, dc=dc, st=st: e.tensor_tensor(
                tmp, src_bf[dc][:, st * STW:(st + 1) * STW], rstd_bc, ALU.mult),
                wait=[("sP", cb1), tmp_free], inc="sV")
            cV = kb.op("vector", lambda e: e.tensor_tensor(tmp, tmp, mr_bc, ALU.subtract),
                       wait=[("sV", cV), ("sP", cb2)], inc="sV")
            cV = kb.op("vector", lambda e, dc=dc, st=st: e.tensor_scalar(
                out_bf[dc][:, st * STW:(st + 1) * STW], tmp,
                gamma_sb[:, dc:dc + 1], beta_sb[:, dc:dc + 1], ALU.mult, ALU.add),
                wait=[("sV", cV)], inc="sV")
            tmp_free = ("sV", cV)
        kb.wait("tensor", ("sV", kb.cnt["sV"]))
    A.reset(m)


def emit_l1_front(kb, A, tn, pb, consts, fused):
    x1 = tn["x1"]
    for g in range(NG):
        t0 = g * GT
        mG = A.mark()
        xn1_bf = [A.bf(128, GT) for _ in range(DC)]
        emit_ln(kb, A, lambda dc, st, t0=t0: x1[dc, :, t0 + st * STW: t0 + (st + 1) * STW],
                xn1_bf, pb, consts, g, consts["eps_ln"])
        for dc in range(DC):
            kb.op("sync", lambda e, dc=dc, t0=t0: e.dma_start(
                out=tn["xn1b"][dc, :, t0:t0 + GT], in_=xn1_bf[dc]),
                wait=[("sV", kb.cnt["sV"])], inc="sL", amt=16)
        kb.barrier()

        mX = A.mark()
        xnorm_bf = [A.bf(128, GT) for _ in range(DC)]
        emit_ln_inner(kb, A, xn1_bf, xnorm_bf, pb, consts, consts["eps_ln"])
        if "xnorm_dbg" in tn:
            for dc in range(DC):
                kb.op("sync", lambda e, dc=dc, t0=t0: e.dma_start(
                    out=tn["xnorm_dbg"][dc, :, t0:t0 + GT], in_=xnorm_bf[dc]),
                    wait=[("sV", kb.cnt["sV"])], inc="sL", amt=16)
        kb.barrier()

        qd = tn["qd"]
        mQ = A.mark()
        q_tile = A.bf(128, STW)
        qk_state = {"free": kb.now()}

        def q_evac(fc, st, ps, cP):
            cV = kb.op("vector", lambda e, ps=ps: e.tensor_scalar_mul(
                q_tile, ps, QSCALE), wait=[("sP", cP), qk_state["free"]], inc="sV")
            slot0 = g * 8 + st * 4
            dst = qd[slot0:slot0 + 4, :, fc, :].rearrange("j p t -> p j t")
            cS = kb.op("sync", lambda e, dst=dst: e.dma_start(
                out=dst, in_=q_tile.rearrange("p (j t) -> p j t", j=4, t=128)),
                wait=[("sV", cV)], inc="sL", amt=16)
            qk_state["free"] = ("sL", cS)
            return ("sV", cV)
        emit_proj(kb, A, tn, "qkv", D, DC, xnorm_bf, q_evac, pb, 0, col_off=0)

        def k_evac(fc, st, ps, cP):
            cV = kb.op("vector", lambda e, ps=ps: e.tensor_copy(q_tile, ps),
                       wait=[("sP", cP), qk_state["free"]], inc="sV")
            dst = tn["kown"][fc, :, t0 + st * STW: t0 + (st + 1) * STW]
            cS = kb.op("sync", lambda e, dst=dst: e.dma_start(
                out=dst, in_=q_tile), wait=[("sV", cV)], inc="sL", amt=16)
            qk_state["free"] = ("sL", cS)
            return ("sV", cV)
        emit_proj(kb, A, tn, "qkv", D, DC, xnorm_bf, k_evac, pb, 0, col_off=D)
        A.reset(mQ)
        kb.wait("vector", ("sL", kb.cnt["sL"]))

        mV = A.mark()
        vw = [A.bf(128, DC, STW), A.bf(128, DC, STW)]
        v_t = A.bf(128, STW)
        vw_free = [kb.now(), kb.now()]
        vt_free = kb.now()
        qkvr = tn["qkv"].rearrange("(k p) f -> p k f", p=128)
        for fst in range(4):
            b = fst % 2
            cG = kb.op("gpsimd", lambda e, fst=fst, b=b: e.dma_start(
                out=vw[b], in_=qkvr[:, :, 2 * D + fst * STW: 2 * D + (fst + 1) * STW]),
                wait=[vw_free[b]], inc="sG", amt=16)
            for tc in range(8):
                ps = pb[fst % 2][:, :]
                cP = None
                for dk in range(DC):
                    cP = kb.op("tensor", lambda e, dk=dk, tc=tc, b=b, ps=ps: e.matmul(
                        ps, xnorm_bf[dk][:, tc * 128:(tc + 1) * 128],
                        vw[b][:, dk, :], start=(dk == 0), stop=(dk == DC - 1)),
                        wait=([("sG", cG), ("sV", kb.cnt["sV"])]
                              if dk == 0 else []), inc="sP")
                cV = kb.op("vector", lambda e, ps=ps: e.tensor_copy(v_t, ps),
                           wait=[("sP", cP), vt_free], inc="sV")
                slot = g * 8 + tc
                dst = tn["vown"][slot, :, fst * STW:(fst + 1) * STW]
                cS = kb.op("sync", lambda e, dst=dst: e.dma_start(out=dst, in_=v_t),
                           wait=[("sV", cV)], inc="sL", amt=16)
                vt_free = ("sL", cS)
            vw_free[b] = ("sP", kb.cnt["sP"])
        A.reset(mV)
        A.reset(mX)
        kb.barrier()

        emit_ffn(kb, A, tn, pb, consts, xn1_bf, layer=1, g=g)
        A.reset(mG)
        kb.barrier()


def emit_attention(kb, A, tn, pb, consts, fused):
    ident_bf = consts["ident_bf"]
    kg = tn["kg"]
    vg = tn["vg"]
    mask = tn["mask"]
    mAt = A.mark()
    ao_fm = [A.bf(128, T) for _ in range(DC)]
    pt_bf = pb[2].bitcast(BF16)[:, 0:128]
    av = [pb[4 + i][:, :] for i in range(4)]

    for s in range(NSLOT):
        kb.barrier()
        W = 2 * (s + 1) * 128
        off_s = 128 * s * (s + 1)
        mS = A.mark()
        q_sb = A.bf(128, DC, 128)
        mask_sb = A.bf(128, W)
        S_f = A.f32(128, W)
        P_bf = A.bf(128, W)
        k_sb = [A.bf(128, DC, 256), A.bf(128, DC, 256)]
        v_sb = [A.bf(128, D), A.bf(128, D)]
        nm = A.f32(128, 1)
        l_f = A.f32(128, 1)
        linv = A.f32(128, 1)
        ao_tm = A.bf(128, D)
        pt_sb = A.bf(128, 128)

        cq = kb.op("sync", lambda e, s=s, q_sb=q_sb: e.dma_start(
            out=q_sb, in_=tn["qd"][s]), wait=[("sV", kb.cnt["sV"])], inc="sL", amt=16)
        cm = kb.op("sync", lambda e, off_s=off_s, W=W, mask_sb=mask_sb: e.dma_start(
            out=mask_sb, in_=mask[:, off_s:off_s + W]), inc="sL", amt=16)

        kb_free = [kb.now(), kb.now()]
        sbank_free = [kb.now(), kb.now()]
        si = 0
        for r in range(2):
            ncols = (s + 1) * 128
            c0 = 0
            while c0 < ncols:
                w = min(256, ncols - c0)
                b = si % 2
                cK = kb.op("sync", lambda e, r=r, c0=c0, w=w, b=b, k_sb=k_sb: e.dma_start(
                    out=k_sb[b][:, :, 0:w],
                    in_=kg[r * 16:(r + 1) * 16, :, c0:c0 + w].rearrange("f p t -> p f t")),
                    wait=[kb_free[b]], inc="sL", amt=16)
                ps = pb[si % 2][:, 0:w]
                cP = None
                for fc in range(DC):
                    cP = kb.op("tensor", lambda e, fc=fc, b=b, w=w, ps=ps, q_sb=q_sb, k_sb=k_sb: e.matmul(
                        ps, q_sb[:, fc, :], k_sb[b][:, fc, 0:w],
                        start=(fc == 0), stop=(fc == DC - 1)),
                        wait=([("sL", cK), sbank_free[si % 2]] if fc == 0 else []),
                        inc="sP")
                col0 = r * (s + 1) * 128 + c0
                cV = kb.op("vector", lambda e, col0=col0, w=w, ps=ps, S_f=S_f, mask_sb=mask_sb: e.tensor_tensor(
                    S_f[:, col0:col0 + w], ps, mask_sb[:, col0:col0 + w], ALU.add),
                    wait=[("sP", cP), ("sL", cm)], inc="sV")
                kb_free[b] = ("sP", cP)
                sbank_free[si % 2] = ("sV", cV)
                c0 += w
                si += 1

        c1 = kb.op("vector", lambda e, S_f=S_f, nm=nm: e.reduce_max(
            nm, S_f, axis=AX.X, negate=True),
            wait=[("sV", kb.cnt["sV"])], inc="sV")
        c2 = kb.op("scalar", lambda e, P_bf=P_bf, S_f=S_f, nm=nm, l_f=l_f: e.activation(
            P_bf, S_f, AF.Exp, bias=nm, accum_out=l_f),
            wait=[("sV", c1)], inc="sA")
        c3 = kb.op("vector", lambda e, linv=linv, l_f=l_f: e.reciprocal(linv, l_f),
                   wait=[("sA", c2)], inc="sV")

        pt_free = kb.now()
        ptsb_free = kb.now()
        v_free = [kb.now(), kb.now()]
        njs = 2 * (s + 1)
        cPav = None
        for j in range(njs):
            r = 0 if j <= s else 1
            l = j if j <= s else j - (s + 1)
            b = j % 2
            cT = kb.op("tensor", lambda e, j=j, P_bf=P_bf: e.transpose(
                pt_bf, P_bf[:, j * 128:(j + 1) * 128], ident_bf),
                wait=[("sA", c2), pt_free], inc="sP")
            cVt = kb.op("vector", lambda e: e.tensor_copy(pt_sb, pt_bf),
                        wait=[("sP", cT), ptsb_free], inc="sV")
            pt_free = ("sV", cVt)
            cL = kb.op("sync", lambda e, r=r, l=l, b=b, v_sb=v_sb: e.dma_start(
                out=v_sb[b], in_=vg[r * 16 + l]),
                wait=[v_free[b]], inc="sL", amt=16)
            for fst in range(4):
                cPav = kb.op("tensor", lambda e, fst=fst, j=j, b=b, v_sb=v_sb: e.matmul(
                    av[fst], pt_sb, v_sb[b][:, fst * STW:(fst + 1) * STW],
                    start=(j == 0), stop=(j == njs - 1)),
                    wait=([("sV", cVt), ("sL", cL)] +
                          ([("sV", kb.cnt["sV"])] if j == 0 else [])) if fst == 0
                    else [],
                    inc="sP")
            ptsb_free = ("sP", cPav)
            v_free[b] = ("sP", cPav)

        for fst in range(4):
            cV = kb.op("vector", lambda e, fst=fst, ao_tm=ao_tm, linv=linv: e.tensor_scalar(
                ao_tm[:, fst * STW:(fst + 1) * STW], av[fst], linv, None, ALU.mult),
                wait=[("sP", cPav), ("sV", c3)], inc="sV")
        sv_ao = kb.cnt["sV"]
        for dc in range(DC):
            cT = kb.op("tensor", lambda e, dc=dc, ao_tm=ao_tm: e.transpose(
                pt_bf, ao_tm[:, dc * 128:(dc + 1) * 128], ident_bf),
                wait=[("sV", sv_ao), pt_free], inc="sP")
            cVt = kb.op("vector", lambda e, dc=dc, s=s: e.tensor_copy(
                ao_fm[dc][:, s * 128:(s + 1) * 128], pt_bf),
                wait=[("sP", cT)], inc="sV")
            pt_free = ("sV", cVt)
        kb.wait("tensor", ("sV", kb.cnt["sV"]))
        A.reset(mS)

    kb.barrier()
    mO = A.mark()
    wsb = [A.bf(128, DC, 128), A.bf(128, DC, 128)]
    xn1_t = A.bf(128, STW)
    s1_t = A.f32(128, STW)
    g1_t = A.f32(128, STW)
    x1_t = A.f32(128, STW)
    o_t = A.f32(128, STW)
    w_free = [kb.now(), kb.now()]
    ev_free = kb.now()
    bank_free = [kb.now(), kb.now()]
    opr = tn["o_proj"].rearrange("(k p) f -> p k f", p=128)
    for fc in range(DC):
        b = fc % 2
        cG = kb.op("gpsimd", lambda e, fc=fc, b=b: e.dma_start(
            out=wsb[b], in_=opr[:, :, fc * 128:(fc + 1) * 128]),
            wait=[w_free[b]], inc="sG", amt=16)
        for st4 in range(4):
            ps = pb[st4 % 2][:, :]
            cP = None
            for dk in range(DC):
                cP = kb.op("tensor", lambda e, dk=dk, st4=st4, b=b, ps=ps: e.matmul(
                    ps, wsb[b][:, dk, :], ao_fm[dk][:, st4 * STW:(st4 + 1) * STW],
                    start=(dk == 0), stop=(dk == DC - 1)),
                    wait=([("sG", cG), bank_free[st4 % 2]] if dk == 0 else []),
                    inc="sP")
            cL1 = kb.op("sync", lambda e, fc=fc, st4=st4: e.dma_start(
                out=xn1_t, in_=tn["xn1b"][fc, :, st4 * STW:(st4 + 1) * STW]),
                wait=[ev_free], inc="sL", amt=16)
            cL2 = kb.op("sync", lambda e, fc=fc, st4=st4: e.dma_start(
                out=g1_t, in_=tn["g1"][fc, :, st4 * STW:(st4 + 1) * STW]),
                inc="sL", amt=16)
            cL3 = kb.op("sync", lambda e, fc=fc, st4=st4: e.dma_start(
                out=x1_t, in_=tn["x1"][fc, :, st4 * STW:(st4 + 1) * STW]),
                inc="sL", amt=16)
            cV = kb.op("vector", lambda e, ps=ps: e.tensor_tensor(s1_t, ps, xn1_t, ALU.add),
                       wait=[("sP", cP), ("sL", cL3), ev_free], inc="sV")
            cV = kb.op("vector", lambda e: e.tensor_tensor(s1_t, s1_t, g1_t, ALU.mult),
                       wait=[("sV", cV)], inc="sV")
            cV = kb.op("vector", lambda e: e.tensor_tensor(o_t, s1_t, x1_t, ALU.add),
                       wait=[("sV", cV)], inc="sV")
            cS = kb.op("sync", lambda e, fc=fc, st4=st4: e.dma_start(
                out=tn["outT"][fc * 128:(fc + 1) * 128, st4 * STW:(st4 + 1) * STW],
                in_=o_t), wait=[("sV", cV)], inc="sL", amt=16)
            ev_free = ("sL", cS)
            bank_free[st4 % 2] = ("sV", cV)
        w_free[b] = ("sP", kb.cnt["sP"])
    A.reset(mO)
    A.reset(mAt)


def emit_reg_setup(kb, par_sb, non_sb, wait_c):
    semh = kb.semh

    def f(e):
        e.wait_ge(semh["sL"], wait_c)
        pr = e.alloc_register("pr")
        e.reg_load(pr, par_sb[0:1, 0:1])
        kb.regs["par"] = RuntimeValue(pr, 0, 1)
        qr = e.alloc_register("qr")
        e.reg_alu(qr, 1, RuntimeValue(pr, 0, 1), ALU.subtract)
        kb.regs["peer"] = RuntimeValue(qr, 0, 1)
        nr = e.alloc_register("nr")
        e.reg_load(nr, non_sb[0:1, 0:1])
        kb.regs["nonce"] = RuntimeValue(nr)
    kb.emit("sync", f)


def emit_kv_handshake(kb, A, tn, non_sb, fsb):
    shf = tn["shflag"]
    shk2 = tn["shk"].rearrange("(r x) t -> r (x t)", r=2)
    shv2 = tn["shv"].rearrange("(r x) t -> r (x t)", r=2)
    ksrc = tn["kown"].rearrange("a p t -> (a p t)").rearrange("(o e) -> o e", o=1)
    vsrc = tn["vown"].rearrange("a p t -> (a p t)").rearrange("(o e) -> o e", o=1)
    cK = kb.op("sync", lambda e: e.dma_start(
        out=shk2[ds(kb.regs["par"], 1), :], in_=ksrc),
        wait=[("sL", kb.cnt["sL"])], inc="sL", amt=16)
    cV = kb.op("sync", lambda e: e.dma_start(
        out=shv2[ds(kb.regs["par"], 1), :], in_=vsrc),
        inc="sL", amt=16)
    cW = kb.op("sync", lambda e: e.dma_start(
        out=shf[ds(kb.regs["par"], 1), :], in_=non_sb),
        wait=[("sL", cV)], inc="sL", amt=16)
    kb.wait("sync", ("sL", cW))
    semF = kb.semh["sF"]

    def poll(e):
        freg = e.alloc_register("freg")
        creg = e.alloc_register("creg")
        e.reg_mov(creg, 1)
        with e.While(creg):
            e.sem_clear(semF)
            e.dma_start(out=fsb, in_=shf[ds(kb.regs["peer"], 1), :]).then_inc(semF, 16)
            e.wait_ge(semF, 16)
            e.reg_load(freg, fsb[0:1, 0:1])
            e.reg_alu(creg, RuntimeValue(freg), kb.regs["nonce"], ALU.subtract)
    kb.emit("sync", poll)



def _token_idx(h):
    gs = [gtile(h, s) for s in range(NSLOT)]
    return np.concatenate([np.arange(g * 128, (g + 1) * 128) for g in gs])


def _host_common(inputs):
    c = {}
    c["w_inT"] = np.ascontiguousarray(inputs["L0_attn_W_in"].T)
    c["w_outT"] = np.ascontiguousarray(inputs["L0_attn_W_out"].T)
    for nm, key in (("kbr", "L0_attn_B_real"), ("kbi", "L0_attn_B_imag"),
                    ("kcr", "L0_attn_C_real"), ("kci", "L0_attn_C_imag")):
        c[nm] = np.ascontiguousarray(inputs[key].reshape(NCD, 16))
    c["normw"] = np.ascontiguousarray(inputs["L0_attn_norm_w"].reshape(DC, 128).T)
    for L in (0, 1):
        c[f"w1t{L}"] = np.ascontiguousarray(inputs[f"L{L}_ffn_W1"].T)
        c[f"w3t{L}"] = np.ascontiguousarray(inputs[f"L{L}_ffn_W3"].T)
        c[f"w2r{L}"] = np.ascontiguousarray(
            inputs[f"L{L}_ffn_W2"].reshape(D, 64, 128).transpose(1, 2, 0).reshape(64, 64 * NCD))
    c["qkv"] = np.ascontiguousarray(inputs["L1_attn_qkv"])
    c["o_proj"] = np.ascontiguousarray(inputs["L1_attn_o_proj"])
    c["gamma"] = np.ascontiguousarray(inputs["L1_attn_gamma"].reshape(DC, 128).T)
    c["beta"] = np.ascontiguousarray(inputs["L1_attn_beta"].reshape(DC, 128).T)
    c["h128"] = _hadamard(128)
    c["h64"] = _hadamard(64)
    c["ident"] = np.eye(128, dtype=np.float32)
    c["onescol"] = np.ones((128, 1), dtype=np.float32)
    c["onesrow"] = np.ones((1, 128), dtype=np.float32)
    return c


def _hadamard(n):
    h = np.array([[1.0]], dtype=np.float32)
    while h.shape[0] < n:
        h = np.block([[h, h], [h, -h]])
    return h.astype(np.float32)


def _common_tensors(nc, mode_l0=True):
    tn = {}

    def inp(name, shape, dt=F32):
        tn[name] = nc.dram_tensor(name, list(shape), dt, kind="ExternalInput").ap()

    def outp(name, shape, dt=F32):
        tn[name] = nc.dram_tensor(name, list(shape), dt, kind="ExternalOutput").ap()
    return tn, inp, outp


def _consts_prologue(kb, A, tn, need=("h128", "h64", "ident", "normw", "gamma", "beta")):
    nc = kb.nc
    consts = {}
    consts["onescol_bf"] = A.bf(128, 1)
    consts["onesrow_f"] = A.f32(1, 128)
    consts["eps_ln"] = A.f32(1, 1)
    consts["eps_rms"] = A.f32(1, 1)
    kb.op("vector", lambda e: e.memset(consts["eps_ln"], EPS_LN), inc="sV")
    kb.op("vector", lambda e: e.memset(consts["eps_rms"], EPS_RMS), inc="sV")
    kb.op("gpsimd", lambda e: e.dma_start(out=consts["onescol_bf"], in_=tn["onescol"]),
          inc="sG", amt=16)
    kb.op("sync", lambda e: e.dma_start(out=consts["onesrow_f"], in_=tn["onesrow"]),
          inc="sL", amt=16)
    if "h128" in need:
        consts["h128_bf"] = A.bf(128, 128)
        kb.op("gpsimd", lambda e: e.dma_start(out=consts["h128_bf"], in_=tn["h128"]),
              inc="sG", amt=16)
    if "h64" in need:
        consts["h64_bf"] = A.bf(64, 64)
        kb.op("gpsimd", lambda e: e.dma_start(out=consts["h64_bf"], in_=tn["h64"]),
              inc="sG", amt=16)
    if "ident" in need:
        consts["ident_bf"] = A.bf(128, 128)
        kb.op("gpsimd", lambda e: e.dma_start(out=consts["ident_bf"], in_=tn["ident"]),
              inc="sG", amt=16)
    if "normw" in need:
        consts["normw_sb"] = A.f32(128, DC)
        kb.op("sync", lambda e: e.dma_start(
            out=consts["normw_sb"], in_=tn["normw"]), inc="sL", amt=16)
    if "gamma" in need:
        consts["gamma_sb"] = A.f32(128, DC)
        consts["beta_sb"] = A.f32(128, DC)
        kb.op("sync", lambda e: e.dma_start(out=consts["gamma_sb"], in_=tn["gamma"]),
              inc="sL", amt=16)
        kb.op("sync", lambda e: e.dma_start(out=consts["beta_sb"], in_=tn["beta"]),
              inc="sL", amt=16)
    kb.wait("vector", ("sG", kb.cnt["sG"]), ("sL", kb.cnt["sL"]))
    kb.wait("tensor", ("sG", kb.cnt["sG"]), ("sL", kb.cnt["sL"]))
    kb.wait("scalar", ("sG", kb.cnt["sG"]), ("sL", kb.cnt["sL"]))
    return consts


def build_l0(dbg=False):
    nc = bass.Bass("TRN2", target_bir_lowering=False)
    tn, inp, outp = _common_tensors(nc)
    inp("xT", (D, T))
    inp("w_inT", (D, NCD))
    inp("w_outT", (NCD, D))
    for nm in ("kbr", "kbi", "kcr", "kci"):
        inp(nm, (NCD, 16))
    inp("normw", (128, DC))
    inp("w1t0", (D, HF))
    inp("w3t0", (D, HF))
    inp("w2r0", (64, NCD * 64))
    inp("h128", (128, 128))
    inp("h64", (64, 64))
    inp("onescol", (128, 1))
    inp("onesrow", (1, 128))
    outp("x1f", (DC, 128, T))
    tn["x1"] = tn["x1f"]
    if dbg:
        outp("s_dbg", (DC, 128, T))
        tn["s_sp"] = tn["s_dbg"]
        outp("w2p_dbg", (64, 128, D), BF16)
        tn["w2p0"] = tn["w2p_dbg"]
        outp("y_dbg", (JCN, 128, GT), BF16)
        outp("o0_dbg", (DC, 128, GT), BF16)
        outp("xn_dbg", (DC, 128, GT), BF16)
    else:
        tn["s_sp"] = nc.dram_tensor("s_sp", [DC, 128, T], F32, kind="Internal").ap()
        tn["w2p0"] = nc.dram_tensor("w2p0", [64, 128, D], BF16, kind="Internal").ap()

    with ExitStack() as ctx:
        kb = KB(nc, ctx)
        A = Arena(nc, ctx)
        pbh = [ctx.enter_context(nc.psum_tensor(f"pb{i}", [128, 512], F32))
               for i in range(8)]
        pb = [p[:, :] for p in pbh]
        consts = _consts_prologue(kb, A, tn, need=("h128", "h64", "normw"))
        k_sb = A.f32(128, JCN)
        kdone = emit_kprep(kb, A, tn, consts, k_sb)
        kb.barrier()
        emit_w2prep(kb, A, tn, 0, pb, consts)
        kb.barrier()
        emit_layer0(kb, A, tn, pb, consts, k_sb, [kdone])
        kb.wait("sync", ("sL", kb.cnt["sL"]))
        kb.finalize()
    return nc


def build_l1a(dbg=False):
    nc = bass.Bass("TRN2", target_bir_lowering=False)
    tn, inp, outp = _common_tensors(nc)
    inp("x1f", (DC, 128, T))
    inp("qkv", (D, 3 * D))
    inp("gamma", (128, DC))
    inp("beta", (128, DC))
    inp("w1t1", (D, HF))
    inp("w3t1", (D, HF))
    inp("w2r1", (64, NCD * 64))
    inp("h128", (128, 128))
    inp("h64", (64, 64))
    inp("onescol", (128, 1))
    inp("onesrow", (1, 128))
    outp("qd", (NSLOT, 128, DC, 128), BF16)
    outp("kown", (DC, 128, T), BF16)
    outp("vown", (NSLOT, 128, D), BF16)
    outp("xn1b", (DC, 128, T), BF16)
    outp("g1f", (DC, 128, T))
    if dbg:
        outp("xnorm_dbg", (DC, 128, T), BF16)
    tn["x1"] = tn["x1f"]
    tn["g1"] = tn["g1f"]
    tn["w2p1"] = nc.dram_tensor("w2p1", [64, 128, D], BF16, kind="Internal").ap()

    with ExitStack() as ctx:
        kb = KB(nc, ctx)
        A = Arena(nc, ctx)
        pbh = [ctx.enter_context(nc.psum_tensor(f"pb{i}", [128, 512], F32))
               for i in range(8)]
        pb = [p[:, :] for p in pbh]
        consts = _consts_prologue(kb, A, tn, need=("h128", "h64", "gamma"))
        emit_w2prep(kb, A, tn, 1, pb, consts)
        kb.barrier()
        emit_l1_front(kb, A, tn, pb, consts, fused=False)
        kb.wait("sync", ("sL", kb.cnt["sL"]))
        kb.finalize()
    return nc


def build_l1b():
    nc = bass.Bass("TRN2", target_bir_lowering=False)
    tn, inp, outp = _common_tensors(nc)
    inp("qd", (NSLOT, 128, DC, 128), BF16)
    inp("kg", (2 * DC, 128, T), BF16)
    inp("vg", (2 * NSLOT, 128, D), BF16)
    inp("mask", (128, 128 * NSLOT * (NSLOT + 1) * 2 // 2), BF16)
    inp("o_proj", (D, D))
    inp("xn1b", (DC, 128, T), BF16)
    inp("g1f", (DC, 128, T))
    inp("x1f", (DC, 128, T))
    inp("ident", (128, 128))
    inp("onescol", (128, 1))
    inp("onesrow", (1, 128))
    outp("outT", (D, T))
    tn["x1"] = tn["x1f"]
    tn["g1"] = tn["g1f"]

    with ExitStack() as ctx:
        kb = KB(nc, ctx)
        A = Arena(nc, ctx)
        pbh = [ctx.enter_context(nc.psum_tensor(f"pb{i}", [128, 512], F32))
               for i in range(8)]
        pb = [p[:, :] for p in pbh]
        consts = _consts_prologue(kb, A, tn, need=("ident",))
        emit_attention(kb, A, tn, pb, consts, fused=False)
        kb.wait("sync", ("sL", kb.cnt["sL"]))
        kb.finalize()
    return nc


def _mask_for(h):
    tri = np.triu(np.full((128, 128), NEGINF, np.float32), 1)
    zero = np.zeros((128, 128), np.float32)
    full = np.full((128, 128), NEGINF, np.float32)
    cols = []
    for s in range(NSLOT):
        g = gtile(h, s)
        for r in (0, 1):
            for l in range(s + 1):
                gp = gtile(r, l)
                cols.append(zero if gp < g else (tri if gp == g else full))
    return np.ascontiguousarray(
        np.concatenate(cols, axis=1).astype(ml_dtypes.bfloat16))


def build_fused():
    nc = bass.Bass("TRN2", target_bir_lowering=False)
    tn, inp, outp = _common_tensors(nc)
    inp("xT", (D, T))
    inp("w_inT", (D, NCD))
    inp("w_outT", (NCD, D))
    for nm in ("kbr", "kbi", "kcr", "kci"):
        inp(nm, (NCD, 16))
    inp("normw", (128, DC))
    inp("w1t0", (D, HF))
    inp("w3t0", (D, HF))
    inp("w2r0", (64, NCD * 64))
    inp("qkv", (D, 3 * D))
    inp("gamma", (128, DC))
    inp("beta", (128, DC))
    inp("w1t1", (D, HF))
    inp("w3t1", (D, HF))
    inp("w2r1", (64, NCD * 64))
    inp("o_proj", (D, D))
    inp("mask", (128, 128 * NSLOT * (NSLOT + 1)), BF16)
    inp("h128", (128, 128))
    inp("h64", (64, 64))
    inp("ident", (128, 128))
    inp("onescol", (128, 1))
    inp("onesrow", (1, 128))
    inp("parity", (1, 1), I32)
    inp("nonce", (1, 16), I32)
    outp("outT", (D, T))

    tn["x1"] = nc.dram_tensor("x1i", [DC, 128, T], F32, kind="Internal").ap()
    tn["s_sp"] = nc.dram_tensor("s_sp", [DC, 128, T], F32, kind="Internal").ap()
    tn["w2p0"] = nc.dram_tensor("w2p0", [64, 128, D], BF16, kind="Internal").ap()
    tn["w2p1"] = nc.dram_tensor("w2p1", [64, 128, D], BF16, kind="Internal").ap()
    tn["qd"] = nc.dram_tensor("qdi", [NSLOT, 128, DC, 128], BF16, kind="Internal").ap()
    tn["kown"] = nc.dram_tensor("kowni", [DC, 128, T], BF16, kind="Internal").ap()
    tn["vown"] = nc.dram_tensor("vowni", [NSLOT, 128, D], BF16, kind="Internal").ap()
    tn["xn1b"] = nc.dram_tensor("xn1bi", [DC, 128, T], BF16, kind="Internal").ap()
    tn["g1"] = nc.dram_tensor("g1i", [DC, 128, T], F32, kind="Internal").ap()
    tn["shk"] = nc.dram_tensor("shk", [2 * T, T], BF16, kind="Internal",
                               addr_space="Shared").ap()
    tn["shv"] = nc.dram_tensor("shv", [2 * T, D], BF16, kind="Internal",
                               addr_space="Shared").ap()
    tn["shflag"] = nc.dram_tensor("shflag", [2, 16], I32, kind="Internal",
                                  addr_space="Shared").ap()
    tn["kg"] = tn["shk"].rearrange("(a p) t -> a p t", p=128)
    tn["vg"] = tn["shv"].rearrange("(a p) t -> a p t", p=128)

    with ExitStack() as ctx:
        kb = KB(nc, ctx)
        A = Arena(nc, ctx)
        pbh = [ctx.enter_context(nc.psum_tensor(f"pb{i}", [128, 512], F32))
               for i in range(8)]
        pb = [p[:, :] for p in pbh]
        consts = _consts_prologue(kb, A, tn,
                                  need=("h128", "h64", "ident", "normw", "gamma"))
        par_sb = A.i32(1, 1)
        non_sb = A.i32(1, 16)
        fsb = A.i32(1, 16)
        kb.op("sync", lambda e: e.dma_start(out=par_sb, in_=tn["parity"]),
              inc="sL", amt=16)
        cN = kb.op("sync", lambda e: e.dma_start(out=non_sb, in_=tn["nonce"]),
                   inc="sL", amt=16)
        emit_reg_setup(kb, par_sb, non_sb, cN)
        k_sb = A.f32(128, JCN)
        kdone = emit_kprep(kb, A, tn, consts, k_sb)
        kb.barrier()
        emit_w2prep(kb, A, tn, 0, pb, consts)
        kb.barrier()
        emit_w2prep(kb, A, tn, 1, pb, consts)
        kb.barrier()
        emit_layer0(kb, A, tn, pb, consts, k_sb, [kdone])
        emit_l1_front(kb, A, tn, pb, consts, fused=True)
        emit_kv_handshake(kb, A, tn, non_sb, fsb)
        emit_attention(kb, A, tn, pb, consts, fused=True)
        kb.wait("sync", ("sL", kb.cnt["sL"]))
        kb.finalize()
    return nc


_CACHE = {}
KERNEL_MODE = "fused"


def kernel(**inputs):
    if KERNEL_MODE == "fused":
        return kernel_fused(**inputs)
    return kernel_3launch(**inputs)


def kernel_fused(**inputs):
    com = _host_common(inputs)
    masks = [_mask_for(h) for h in (0, 1)]
    x = inputs["L0_attn_x"]
    if "ncf" not in _CACHE:
        _CACHE["ncf"] = build_fused()
    ncf = _CACHE["ncf"]
    idxs = [_token_idx(0), _token_idx(1)]
    cores = list(range(8))
    nonce = np.array([[np.random.randint(1, 2 ** 30)] * 16], dtype=np.int32)
    in_maps = []
    for c in cores:
        b, h = c // 2, c % 2
        m = {"xT": np.ascontiguousarray(x[b][idxs[h]].T),
             "mask": masks[h],
             "parity": np.array([[h]], dtype=np.int32),
             "nonce": nonce}
        for k in ("w_inT", "w_outT", "kbr", "kbi", "kcr", "kci", "normw",
                  "w1t0", "w3t0", "w2r0", "qkv", "gamma", "beta",
                  "w1t1", "w3t1", "w2r1", "o_proj",
                  "h128", "h64", "ident", "onescol", "onesrow"):
            m[k] = com[k]
        in_maps.append(m)
    res = run_bass_kernel_spmd(ncf, in_maps, core_ids=cores)
    out = np.empty((4, 4096, D), dtype=np.float32)
    for c in cores:
        b, h = c // 2, c % 2
        out[b][idxs[h]] = res.results[c]["outT"].T
    return out


def kernel_3launch(**inputs):
    com = _host_common(inputs)
    masks = [_mask_for(h) for h in (0, 1)]
    x = inputs["L0_attn_x"]

    if "ncs" not in _CACHE:
        _CACHE["ncs"] = (build_l0(), build_l1a(), build_l1b())
    nc0, nc1, nc2 = _CACHE["ncs"]

    idxs = [_token_idx(0), _token_idx(1)]
    cores = list(range(8))

    m1 = []
    for c in cores:
        b, h = c // 2, c % 2
        m = {"xT": np.ascontiguousarray(x[b][idxs[h]].T)}
        for k in ("w_inT", "w_outT", "kbr", "kbi", "kcr", "kci", "normw",
                  "w1t0", "w3t0", "w2r0", "h128", "h64", "onescol", "onesrow"):
            m[k] = com[k]
        m1.append(m)
    r1 = run_bass_kernel_spmd(nc0, m1, core_ids=cores)

    m2 = []
    for c in cores:
        m = {"x1f": r1.results[c]["x1f"]}
        for k in ("qkv", "gamma", "beta", "w1t1", "w3t1", "w2r1",
                  "h128", "h64", "onescol", "onesrow"):
            m[k] = com[k]
        m2.append(m)
    r2 = run_bass_kernel_spmd(nc1, m2, core_ids=cores)

    m3 = []
    for c in cores:
        b, h = c // 2, c % 2
        e, o = 2 * b, 2 * b + 1
        kgather = np.concatenate([r2.results[e]["kown"], r2.results[o]["kown"]], axis=0)
        vgather = np.concatenate([r2.results[e]["vown"], r2.results[o]["vown"]], axis=0)
        m = {"qd": r2.results[c]["qd"], "kg": kgather, "vg": vgather,
             "mask": masks[h], "o_proj": com["o_proj"],
             "xn1b": r2.results[c]["xn1b"], "g1f": r2.results[c]["g1f"],
             "x1f": r1.results[c]["x1f"], "ident": com["ident"],
             "onescol": com["onescol"], "onesrow": com["onesrow"]}
        m3.append(m)
    r3 = run_bass_kernel_spmd(nc2, m3, core_ids=cores)

    out = np.empty((4, 4096, D), dtype=np.float32)
    for c in cores:
        b, h = c // 2, c % 2
        out[b][idxs[h]] = r3.results[c]["outT"].T
    return out
